# version 27
# baseline (speedup 1.0000x reference)
import numpy as np
import concourse.bass as bass
import concourse.mybir as mybir
from concourse.bass_utils import run_bass_kernel_spmd
from contextlib import ExitStack

F32 = mybir.dt.float32
BF16 = mybir.dt.bfloat16
AF = mybir.ActivationFunctionType
ALU = mybir.AluOpType
AX = mybir.AxisListType

ISCALE = 0.08838834764831845


class Chan:
    __slots__ = ("name", "inc", "count")

    def __init__(self, name, inc=1):
        self.name = name
        self.inc = inc
        self.count = 0

    def tok(self):
        return (self, self.count)


class Plan:

    def __init__(self):
        self.steps = []

    def step(self, engine_attr, chan, fn, waits=(), incs=1):
        for (c, v) in waits:
            assert v <= c.count, f"future token {c.name}: {v} > {c.count}"
        self.steps.append((engine_attr, chan,
                           [w for w in waits if w[1] > 0], fn, incs))
        chan.count += incs * chan.inc
        return chan.tok()

    def replay(self, block, sems, waited):
        per_engine = {}
        for s in self.steps:
            per_engine.setdefault(s[0], []).append(s)

        def mk(ea, steps):
            def body(eng):
                w_ = waited.setdefault(ea, {})
                for (_ea, ch, waits, fn, incs) in steps:
                    for (wc, val) in waits:
                        if w_.get(wc.name, 0) >= val:
                            continue
                        eng.wait_ge(sems[wc.name], val)
                        w_[wc.name] = val

                    def inc(instr, cc=ch):
                        instr.then_inc(sems[cc.name], cc.inc)
                        return instr
                    fn(eng, inc)
            return body
        for ea, steps in per_engine.items():
            getattr(block, ea)(mk(ea, steps))


class Builder:
    def __init__(self, D=4096, H=32, G=8, T_own=2048, T_keys=4096):
        self.D, self.H, self.G = D, H, G
        self.T_own, self.T_keys = T_own, T_keys
        self.DC = D // 128
        self.E, self.R = 8, 16
        self.SCALING = 8.0 / self.R
        self.n_slots = T_own // 512
        self.CH = T_keys // 128
        self.EPC = self.CH // self.n_slots
        assert self.EPC == 8 and T_keys - T_own == T_own
        self.NB = T_own // 512
        self.HPG = H // G
        self.NT = T_own // 128
        self.OB = D // 512
        self.D2 = D // 2
        self.TGRP = min(16, self.DC)

        self.nc = bass.Bass("TRN2", target_bir_lowering=False)
        self.plan = None
        self.chans = []
        self.es = ExitStack()
        self.waited = {}

    def chan(self, name, inc=1):
        c = Chan(name, inc)
        self.chans.append(c)
        return c

    def st(self, ch, fn, waits=(), incs=1):
        return self.plan.step(self.eng_of(ch), ch, fn, waits, incs)

    def cload(self, fn, waits=()):
        t = self.st(self.c_const, fn, list(waits) + self._const_chain)
        self._const_chain = [t]
        return t

    def eng_of(self, ch):
        if ch is self.PE:
            return "tensor"
        if ch is self.DVE:
            return "vector"
        if ch is self.ACT:
            return "scalar"
        if ch is self.POOL or ch is self.c_cc:
            return "gpsimd"
        return "gpsimd" if ch.name in self.gp_ch else "sync"

    def run_phase(self, emit_fn):
        self.plan = Plan()
        emit_fn()
        with self.nc.Block() as block:
            self.plan.replay(block, self.sems, self.waited)
        self.plan = None

    def barrier_toks(self):
        return [c.tok() for c in self.chans]

    def build(self):
        self.declare_io()
        self.declare_channels()
        self.sems = {c.name: self.es.enter_context(
            self.nc.semaphore(c.name)) for c in self.chans}
        self.declare_persistent()
        self.ctx_toks = []
        self._const_chain = []
        self.run_phase(self.emit_phase_a)
        self.run_phase(self.emit_phase_b)
        self.run_phase(self.emit_phase_f)
        self.es.close()
        return self.nc

    def declare_io(self):
        nc, D = self.nc, self.D
        T_own, G, H, E, R = self.T_own, self.G, self.H, self.E, self.R

        def din(name, shape):
            return nc.dram_tensor(name, list(shape), F32,
                                  kind="ExternalInput").ap()
        self.x_own = din("x_own", (T_own, D))
        self.wq = din("wq", (D, H * 128))
        self.wk = din("wk", (D, G * 128))
        self.wv = din("wv", (D, G * 128))
        self.wo = din("wo", (H * 128, D))
        self.a_t = din("a_t", (D, E * R))
        self.bm_eff = din("bm_eff", (E * R, D))
        self.router_w = din("router_w", (E, D))
        self.router_b = din("router_b", (1, E))
        self.cos_own = din("cos_own", (64, T_own))
        self.sin_own = din("sin_own", (64, T_own))
        self.masksin = din("masksin", (8, 128, 512))
        self.identin = din("identin", (128, 128))
        self.expandin = din("expandin", (E, E * R))
        self.out = nc.dram_tensor("out", [T_own, D], F32,
                                  kind="ExternalOutput").ap()
        self.rs_scr = [nc.dram_tensor(f"rs_scr{j}", [1, 512], F32).ap()
                       for j in range(4)]
        self.q_spill = nc.dram_tensor("q_spill", [H * 128, T_own], BF16).ap()
        self.k_spill = nc.dram_tensor("k_spill", [G * 128, T_own], BF16).ap()
        self.v_spill = nc.dram_tensor("v_spill", [G * 128, T_own], BF16).ap()
        self.kg = nc.dram_tensor("kg", [2 * G * 128, T_own], BF16).ap()
        self.vg = nc.dram_tensor("vg", [2 * G * 128, T_own], BF16).ap()

    def declare_channels(self):
        C = self.chan
        self.PE, self.DVE, self.ACT, self.POOL = (C("pe"), C("dve"),
                                                  C("act"), C("pool"))
        self.c_x = [C(f"c_x{i}", 16) for i in range(4)]
        self.c_rope = [C(f"c_rope{i}", 16) for i in range(2)]
        self.c_spq = [C(f"c_spq{i}", 16) for i in range(2)]
        self.c_spk = [C(f"c_spk{i}", 16) for i in range(2)]
        self.c_spv = [C(f"c_spv{i}", 16) for i in range(2)]
        self.c_q = [C(f"c_q{i}", 16) for i in range(4)]
        self.c_k = [C(f"c_k{i}", 16) for i in range(2)]
        self.c_v = [C(f"c_v{i}", 16) for i in range(2)]
        self.c_st = [C(f"c_st{i}", 16) for i in range(2)]
        self.c_xf = [C(f"c_xf{i}", 16) for i in range(2)]
        self.c_w = [C(f"c_w{i}", 16) for i in range(2)]
        self.c_wo = [C(f"c_wo{i}", 16) for i in range(15)]
        self.c_const = C("c_const", 16)
        self.c_cc = C("c_cc", 1)
        self.c_bc = [C(f"c_bc{i}", 16) for i in range(4)]
        self.gp_ch = set(c.name for c in self.c_w + self.c_wo + self.c_rope
                         + self.c_x + [self.c_const])
        self.sync_extra = set(c.name for c in self.c_bc)

    def declare_persistent(self):
        E, R, DC, H, T_own = self.E, self.R, self.DC, self.H, self.T_own
        sb = lambda n, sh, dt: self.es.enter_context(
            self.nc.sbuf_tensor(n, sh, dt))
        self.ones_col = sb("ones_col", [128, 1], BF16)
        self.ones_row = sb("ones_row", [1, 128], BF16)
        self.ones_row32 = sb("ones_row32", [1, 128], F32)
        self.ident_bf = sb("ident_bf", [128, 128], BF16)
        self.expand_bf = sb("expand_bf", [E, E * R], BF16)
        self.rb_sb = sb("rb_sb", [1, E], BF16)
        self.rwT = sb("rwT", [128, DC * E], BF16)
        self.hw = sb("hw", [128, T_own], BF16)


    def emit_phase_a(self):
        st, DVE, ACT, PE, POOL = (self.st, self.DVE, self.ACT, self.PE,
                                  self.POOL)
        D, D2, DC, T_own = self.D, self.D2, self.DC, self.T_own
        cc = self.c_const

        esA = ExitStack()
        sbA = lambda n, sh, dt: esA.enter_context(
            self.nc.sbuf_tensor(n, sh, dt))
        psA = lambda n, sh, dt: esA.enter_context(
            self.nc.psum_tensor(n, sh, dt))

        self.t_ident = self.cload(lambda e, i, s=self: i(e.dma_start(
            s.ident_bf[:, :], s.identin)))
        self.t_expand = self.cload(lambda e, i, s=self: i(e.dma_start(
            s.expand_bf[:, :], s.expandin)))
        self.t_rb = self.cload(lambda e, i, s=self: i(e.dma_start(
            s.rb_sb[:, :], s.router_b)))
        self.t_oc = st(POOL, lambda e, i, s=self: i(e.memset(
            s.ones_col[:, :], 1.0)))
        self.t_or = st(POOL, lambda e, i, s=self: i(e.memset(
            s.ones_row[:, :], 1.0)))
        self.t_or32 = st(POOL, lambda e, i, s=self: i(e.memset(
            s.ones_row32[:, :], 1.0)))

        rw_tmp = sbA("rw_tmp", [self.E, 128], BF16)
        pMiscF = psA("pMiscF", [128, 8 + 128], F32)
        pMiscB = psA("pMiscB", [128, 136], BF16)
        self.pl = pMiscF[:, 0:self.E]
        self.pwx = pMiscF[:, 8:136]
        self.pw = pMiscB[0:self.E, 0:128]
        self._rw_tmp = rw_tmp
        self._p_rw = pMiscB[:, 128:136]

        self.xnT = sbA("xnT", [128, DC * T_own], BF16)
        self.hT = sbA("hT", [128, self.T_own], BF16)
        self.xnT3 = self.xnT[:, :].rearrange("p (c t) -> p c t", c=DC)
        self.xbuf = [sbA(f"xbuf{i}", [128, D2], BF16) for i in range(4)]
        self.xn_tok = sbA("xn_tok", [128, D], BF16)
        self.wbuf = [sbA(f"wbuf{i}", [128, DC * 128], BF16) for i in range(2)]
        _stgp = [sbA(f"stg{i}", [128, 512], BF16) for i in range(2)]
        self.stg = {k: _stgp for k in "qkv"}
        self.cos_sl = sbA("cos_sl", [64, T_own], BF16)
        self.sin_sl = sbA("sin_sl", [64, T_own], BF16)
        self.rtmp = [sbA(f"rtmp{i}", [64, 512], F32) for i in range(2)]
        self.stats2 = []
        for k in range(2):
            d = {nm: sbA(f"st{k}_" + nm, [128, 1], F32)
                 for nm in ["dd", "s0", "r0", "s1", "rstd", "nb"]}
            d["bn"] = sbA(f"st{k}_bn", [128, 2 * max(1, self.D2 // 512) * 6], F32)
            d["mv"] = sbA(f"st{k}_mv", [128, 2], F32)
            self.stats2.append(d)
        self.gt = {nm: sbA("gt_" + nm, [128, self.E], F32)
                   for nm in ["t", "i1", "tm", "i2", "ind", "et", "w"]}
        self.gs = {nm: sbA("gs_" + nm, [128, 1], F32)
                   for nm in ["m1", "m2", "em2", "den", "rden"]}
        self.w_bf = sbA("w_bf", [128, self.E], BF16)
        self.wT_sb = sbA("wT_sb", [self.E, 128], BF16)

        self.pA = [psA(f"pA{i}", [128, 512], F32) for i in range(2)]
        self.pT = [psA(f"pT{i}", [128, self.TGRP * 128], BF16)
                   for i in range(2)]

        self.pA_free = [[] for _ in range(2)]
        self.pT_free = [[] for _ in range(2)]
        self.xbuf_free = [[] for _ in range(4)]
        _stgf = [[], []]
        self.stg_free = {k: _stgf for k in "qkv"}
        self.wbuf_free = [[] for _ in range(2)]
        self.rtmp_free = [[], []]
        self.pl_free = []
        self.pw_free = []
        self.pwx_free = []
        self.wT_free = []

        self.pA_i = 0
        self.w_i = 0
        _stgi = [0]
        self.stg_i = {k: _stgi for k in "qkv"}
        self.xt_i = 0
        self.kspill_toks = [[] for _ in range(self.G)]
        self.vspill_toks = [[] for _ in range(self.G)]
        self.qspill_toks = []
        self.hT_toks = []
        self.hw_toks = []
        self.xnT_tb_tok = {}
        self.rope_users = []
        self.prev_tile_norm = []
        self.prev_gate = []
        self.last_wmm = []
        self.pend_evac = []
        self.tb_evac_tok = []
        self.last_ttr = []
        self.pend_post = []

        self.emit_slab("own")

        esA.close()

    def emit_slab(self, slab):
        st = self.st
        rl1 = st(self.c_rope[0], lambda e, i, s=self: i(
            e.dma_start(s.cos_sl[:, :], s.cos_own)))
        rl2 = st(self.c_rope[1], lambda e, i, s=self: i(
            e.dma_start(s.sin_sl[:, :], s.sin_own)))
        self.rope_now = [rl1, rl2]

        lw0, iw0 = self.load_wslab(self.wk, 0)
        for tb in range(self.NB):
            for t in range(tb * 4, tb * 4 + 4):
                self.emit_stage1_tile(slab, t)
            for fn in self.pend_post:
                fn()
            self.pend_post = []
            self.flush_evacs()
            self.xnT_tb_tok[tb] = self.tb_evac_tok[-1]
            self.emit_wcell(slab, "k", 0, tb, lw0, iw0)

        self.emit_wpass(slab, self.wk, self.G, "k", skip_first=True)
        self.emit_wpass(slab, self.wv, self.G, "v")
        groups = [[0, 1], [2, 3], [4, 5], [6, 7]]
        kw = [t for toks in self.kspill_toks for t in toks[-2:]]
        self.t_ccK = st(self.c_cc, lambda e, i, s=self, g=groups: i(
            e.collective_compute("AllGather", ALU.bypass,
                                 replica_groups=g,
                                 ins=[s.k_spill.opt()],
                                 outs=[s.kg.opt()])), kw)
        vw = [t for toks in self.vspill_toks for t in toks[-2:]]
        self.t_ccV = st(self.c_cc, lambda e, i, s=self, g=groups: i(
            e.collective_compute("AllGather", ALU.bypass,
                                 replica_groups=g,
                                 ins=[s.v_spill.opt()],
                                 outs=[s.vg.opt()])), vw)
        self.emit_wpass(slab, self.wq, self.H, "q")
        self.emit_wpass(slab, self.a_t, 1, "h")
        self.emit_rwt_init()
        self.emit_router()

    def emit_stage1_tile(self, slab, t):
        st, DVE, ACT = self.st, self.DVE, self.ACT
        x_src = self.x_own if slab == "own" else self.x_oth
        DC = self.DC
        ia = self.xt_i % 4
        ib = (self.xt_i + 1) % 4
        self.xt_i += 2
        S = self.stats2[(self.xt_i // 2) % 2]
        for fn in self.pend_post:
            fn()
        self.pend_post = []
        self.flush_evacs()
        la = st(self.c_x[ia], lambda e, i, s=self, t=t, ia=ia, x=x_src: i(
            e.dma_start(s.xbuf[ia][:, :], x[t * 128:(t + 1) * 128, 0:s.D2])),
            self.xbuf_free[ia][-1:] + self.prev_tile_norm[-2:-1])
        lb = st(self.c_x[ib], lambda e, i, s=self, t=t, ib=ib, x=x_src: i(
            e.dma_start(s.xbuf[ib][:, :], x[t * 128:(t + 1) * 128,
                                            s.D2:s.D])),
            self.xbuf_free[ib][-1:] + self.prev_tile_norm[-2:-1])
        pv = self.prev_tile_norm[-3:-1]
        ck = min(512, self.D2)
        nst = self.D2 // ck

        def f_bn(e, i, s=self, ia=ia, ib=ib, nst=nst, ck=ck):
            ins = None
            for half, ix in ((0, ia), (1, ib)):
                for c in range(nst):
                    ins = e.bn_stats(
                        S["bn"][:, (half * nst + c) * 6:
                                (half * nst + c + 1) * 6],
                        s.xbuf[ix][:, c * ck:(c + 1) * ck])
            i(ins)
        s2 = st(DVE, f_bn, [la, lb] + pv)
        s4 = st(DVE, lambda e, i: i(e.bn_aggr(
            S["mv"][:, :], S["bn"][:, :])), [s2])
        s9 = st(DVE, lambda e, i: i(e.tensor_scalar(
            S["dd"][:, :], S["mv"][:, 1:2], 1e-5, None, ALU.add)), [s4])
        a1 = st(ACT, lambda e, i: i(e.activation(
            S["s0"][:, :], S["dd"][:, :], AF.Sqrt)), [s9])

        def mk_post(s=self, S=S, a1=a1, ia=ia, ib=ib, t=t):
            st = s.st
            s10 = st(s.DVE, lambda e, i: i(e.reciprocal(
                S["r0"][:, :], S["s0"][:, :])), [a1])
            s11 = st(s.DVE, lambda e, i: i(e.tensor_tensor(
                S["s1"][:, :], S["dd"][:, :], S["r0"][:, :], ALU.mult)),
                [s10])
            s12 = st(s.DVE, lambda e, i: i(e.tensor_tensor(
                S["s1"][:, :], S["s1"][:, :], S["s0"][:, :], ALU.add)),
                [s11])
            s13 = st(s.DVE, lambda e, i: i(e.tensor_scalar(
                S["s1"][:, :], S["s1"][:, :], 0.5, None, ALU.mult)), [s12])
            s14 = st(s.DVE, lambda e, i: i(e.reciprocal(
                S["rstd"][:, :], S["s1"][:, :])), [s13])
            s16 = st(s.DVE, lambda e, i: i(e.scalar_tensor_tensor(
                S["nb"][:, :], S["mv"][:, 0:1], -1.0, S["rstd"][:, :],
                ALU.mult, ALU.mult)), [s14])
            n1 = st(s.ACT, lambda e, i: i(e.activation(
                s.xn_tok[:, 0:s.D2], s.xbuf[ia][:, :], AF.Identity,
                bias=S["nb"][:, :], scale=S["rstd"][:, :])),
                [s16, a1] + s.last_ttr[-1:])
            n2 = st(s.ACT, lambda e, i: i(e.activation(
                s.xn_tok[:, s.D2:s.D], s.xbuf[ib][:, :], AF.Identity,
                bias=S["nb"][:, :], scale=S["rstd"][:, :])), [s16, n1])
            s.prev_tile_norm.append(n2)
            s.xbuf_free[ia].append(n1)
            s.xbuf_free[ib].append(n2)
            nr = (s.DC + s.TGRP - 1) // s.TGRP
            for r in range(nr):
                ir = r % 2
                c0 = r * s.TGRP
                ncs = min(s.TGRP, s.DC - c0)

                def f_tr(e, i, s=s, c0=c0, ncs=ncs, ir=ir):
                    ins = None
                    for k in range(ncs):
                        ins = e.transpose(
                            s.pT[ir][:, k * 128:(k + 1) * 128],
                            s.xn_tok[:, (c0 + k) * 128:(c0 + k + 1) * 128],
                            s.ident_bf[:, :])
                    i(ins)
                ttr = st(s.PE, f_tr,
                         [n1, n2, s.t_ident] + s.pT_free[ir][-1:])
                s.last_ttr.append(ttr)

                def mk_ev(s=s, c0=c0, ncs=ncs, ir=ir, t=t, ttr=ttr):
                    def f_ev(e, i):
                        dst = s.xnT3[:, c0:c0 + ncs, t * 128:(t + 1) * 128]
                        src = s.pT[ir][:, 0:ncs * 128].rearrange(
                            "p (c k) -> p c k", c=ncs)
                        i(e.tensor_copy(dst, src))

                    def f_ev_act(e, i):
                        dst = s.xnT3[:, c0:c0 + ncs, t * 128:(t + 1) * 128]
                        src = s.pT[ir][:, 0:ncs * 128].rearrange(
                            "p (c k) -> p c k", c=ncs)
                        i(e.activation(dst, src, AF.Identity))
                    if ir == 0:
                        tev = s.st(s.DVE, f_ev, [ttr] + s.last_wmm)
                    else:
                        tev = s.st(s.ACT, f_ev_act, [ttr] + s.last_wmm)
                    s.pT_free[ir].append(tev)
                    s.tb_evac_tok.append(tev)
                s.pend_evac.append(mk_ev)
        self.pend_post.append(mk_post)
        return None

    def load_wslab(self, whbm, j):
        iw = self.w_i % 2
        self.w_i += 1
        lw = self.st(self.c_w[iw], lambda e, i, s=self, j=j, iw=iw,
                     w=whbm: i(
            e.dma_start(s.wbuf[iw][:, :].rearrange(
                            "p (c n) -> p c n", n=128),
                        w[:, j * 128:(j + 1) * 128].rearrange(
                            "(c p) n -> p c n", p=128))),
            self.wbuf_free[iw][-1:])
        return lw, iw

    def emit_wcell(self, slab, kind, j, tb, lw, iw):
        ip = self.pA_i % 2
        self.pA_i += 1

        def f_mm(e, i, s=self, tb=tb, iw=iw, ip=ip):
            ins = None
            for c in range(s.DC):
                ins = e.matmul(
                    s.pA[ip][:, :],
                    s.wbuf[iw][:, c * 128:(c + 1) * 128],
                    s.xnT[:, c * s.T_own + tb * 512:
                          c * s.T_own + (tb + 1) * 512],
                    start=(c == 0), stop=(c == s.DC - 1))
            i(ins)
        tmm = self.st(self.PE, f_mm,
                      [lw, self.xnT_tb_tok[tb]] + self.pA_free[ip][-1:])
        self.last_wmm = [tmm]
        self.wbuf_free[iw].append(tmm)
        self.emit_epilogue(slab, kind, j, tb, ip, tmm)

    def flush_evacs(self):
        for fn in self.pend_evac:
            fn()
        self.pend_evac = []

    def emit_wpass(self, slab, whbm, dout_chunks, kind, skip_first=False):
        st = self.st
        for j in range(1 if skip_first else 0, dout_chunks):
            lw, iw = self.load_wslab(whbm, j)
            for tb in range(self.NB):
                self.emit_wcell(slab, kind, j, tb, lw, iw)

    def emit_epilogue(self, slab, kind, j, tb, ip, tmm):
        st, DVE, ACT = self.st, self.DVE, self.ACT
        perm_off = 512 * tb
        if kind in ("k", "q"):
            ir = self.stg_i[kind][0] % 2
            self.stg_i[kind][0] += 1
            stg = self.stg[kind][ir]
            cA = self.cos_sl[:, tb * 512:(tb + 1) * 512]
            sA = self.sin_sl[:, tb * 512:(tb + 1) * 512]
            m1 = st(DVE, lambda e, i, s=self, ip=ip, cA=cA: i(e.tensor_tensor(
                s.rtmp[0][:, :], s.pA[ip][0:64, :], cA, ALU.mult)),
                [tmm] + self.rope_now + self.rtmp_free[0][-1:])
            m2 = st(DVE, lambda e, i, s=self, ip=ip, sA=sA: i(e.tensor_tensor(
                s.rtmp[1][:, :], s.pA[ip][64:128, :], sA, ALU.mult)),
                [m1] + self.rtmp_free[1][-1:])
            o1 = st(DVE, lambda e, i, s=self, stg=stg: i(e.tensor_tensor(
                stg[0:64, :], s.rtmp[0][:, :], s.rtmp[1][:, :],
                ALU.subtract)), [m2] + self.stg_free[kind][ir][-1:])
            m3 = st(DVE, lambda e, i, s=self, ip=ip, sA=sA: i(e.tensor_tensor(
                s.rtmp[0][:, :], s.pA[ip][0:64, :], sA, ALU.mult)), [o1])
            m4 = st(DVE, lambda e, i, s=self, ip=ip, cA=cA: i(e.tensor_tensor(
                s.rtmp[1][:, :], s.pA[ip][64:128, :], cA, ALU.mult)), [m3])
            o2 = st(DVE, lambda e, i, s=self, stg=stg: i(e.tensor_tensor(
                stg[64:128, :], s.rtmp[0][:, :], s.rtmp[1][:, :], ALU.add)),
                [m4])
            self.rtmp_free[0].append(o2)
            self.rtmp_free[1].append(o2)
            self.pA_free[ip].append(o2)
            self.rope_users.append(o2)
            if kind == "k":
                sp = st(self.c_spk[ir], lambda e, i, s=self, j=j, stg=stg,
                        po=perm_off: i(e.dma_start(
                            s.k_spill[j * 128:(j + 1) * 128, po:po + 512],
                            stg[:, :])), [o2])
                self.kspill_toks[j].append(sp)
            else:
                sp = st(self.c_spq[ir], lambda e, i, s=self, j=j, stg=stg,
                        tb=tb: i(e.dma_start(
                            s.q_spill[j * 128:(j + 1) * 128,
                                      tb * 512:(tb + 1) * 512], stg[:, :])),
                        [o2])
                self.qspill_toks.append(sp)
            self.stg_free[kind][ir].append(sp)
        elif kind == "v":
            ir = self.stg_i["v"][0] % 2
            self.stg_i["v"][0] += 1
            ev = st(ACT, lambda e, i, s=self, ip=ip, ir=ir: i(e.activation(
                s.stg["v"][ir][:, :], s.pA[ip][:, :], AF.Identity)),
                [tmm] + self.stg_free["v"][ir][-1:])
            self.pA_free[ip].append(ev)
            sp = st(self.c_spv[ir], lambda e, i, s=self, j=j, ir=ir,
                    po=perm_off: i(e.dma_start(
                        s.v_spill[j * 128:(j + 1) * 128, po:po + 512],
                        s.stg["v"][ir][:, :])), [ev])
            self.vspill_toks[j].append(sp)
            self.stg_free["v"][ir].append(sp)
        elif kind == "h":
            ev = st(ACT, lambda e, i, s=self, ip=ip, tb=tb: i(e.activation(
                s.hT[:, tb * 512:(tb + 1) * 512], s.pA[ip][:, :],
                AF.Identity)), [tmm])
            self.pA_free[ip].append(ev)
            self.hT_toks.append(ev)

    def emit_rwt_init(self):
        st, PE, DVE = self.st, self.PE, self.DVE
        rw_tmp, p_rw = self._rw_tmp, self._p_rw
        prev = []
        for c in range(self.DC):
            tl = self.cload(lambda e, i, s=self, c=c, rw=rw_tmp: i(
                e.dma_start(rw[:, :],
                            s.router_w[:, c * 128:(c + 1) * 128])), prev)
            tt = st(PE, lambda e, i, s=self, rw=rw_tmp, p=p_rw: i(
                e.transpose(p, rw[:, :], s.ident_bf[0:s.E, 0:s.E])),
                [tl, self.t_ident] + prev)
            tv = st(DVE, lambda e, i, s=self, c=c, p=p_rw: i(e.tensor_copy(
                s.rwT[:, c * s.E:(c + 1) * s.E], p)), [tt])
            prev = [tv]
        self.t_rwT = prev[0]
        self._pw_init = prev[0]

    def emit_router(self):
        st, DVE, ACT, PE = self.st, self.DVE, self.ACT, self.PE
        g, gs = self.gt, self.gs
        for tt in range(self.NT):
            def f_rmm(e, i, s=self, tt=tt):
                for c in range(s.DC):
                    e.matmul(s.pl,
                             s.xnT[:, c * s.T_own + tt * 128:
                                   c * s.T_own + (tt + 1) * 128],
                             s.rwT[:, c * s.E:(c + 1) * s.E],
                             start=(c == 0), stop=False)
                ins = e.matmul(s.pl, s.ones_row[:, :], s.rb_sb[:, :],
                               start=False, stop=True)
                i(ins)
            pg = self.prev_gate[-2:]
            tlg = st(PE, f_rmm,
                     [self.t_rwT, self.t_rb, self.t_or,
                      self.xnT_tb_tok[tt // 4]] + self.pl_free[-1:] + pg)
            g0 = st(DVE, lambda e, i, s=self: i(e.reduce_max(
                gs["m1"][:, :], s.pl, axis=AX.X)), [tlg] + pg)
            g1 = st(DVE, lambda e, i, s=self: i(e.tensor_scalar(
                g["t"][:, :], s.pl, gs["m1"][:, :], None,
                ALU.subtract)), [g0])
            self.pl_free.append(g1)
            g2 = st(DVE, lambda e, i: i(e.tensor_scalar(
                g["i1"][:, :], g["t"][:, :], 0.0, None, ALU.is_ge)), [g1])
            g3 = st(DVE, lambda e, i: i(e.tensor_scalar(
                g["tm"][:, :], g["i1"][:, :], -1e9, None, ALU.mult)), [g2])
            g4 = st(DVE, lambda e, i: i(e.tensor_tensor(
                g["tm"][:, :], g["t"][:, :], g["tm"][:, :], ALU.add)), [g3])
            g5 = st(DVE, lambda e, i: i(e.reduce_max(
                gs["m2"][:, :], g["tm"][:, :], axis=AX.X)), [g4])
            g6 = st(DVE, lambda e, i: i(e.tensor_scalar(
                g["i2"][:, :], g["tm"][:, :], gs["m2"][:, :], None,
                ALU.is_ge)), [g5])
            g7 = st(DVE, lambda e, i: i(e.tensor_tensor(
                g["ind"][:, :], g["i1"][:, :], g["i2"][:, :], ALU.add)),
                [g6])
            a1 = st(ACT, lambda e, i: i(e.activation(
                g["et"][:, :], g["t"][:, :], AF.Exp)), [g1] + pg)
            a2 = st(ACT, lambda e, i: i(e.activation(
                gs["em2"][:, :], gs["m2"][:, :], AF.Exp)), [g5, a1])
            g8 = st(DVE, lambda e, i: i(e.tensor_scalar(
                gs["den"][:, :], gs["em2"][:, :], 1.0, None, ALU.add)),
                [a2, g7])
            g9 = st(DVE, lambda e, i: i(e.reciprocal(
                gs["rden"][:, :], gs["den"][:, :])), [g8])
            g10 = st(DVE, lambda e, i: i(e.tensor_tensor(
                g["w"][:, :], g["ind"][:, :], g["et"][:, :], ALU.mult)),
                [g9, a1])
            g11 = st(DVE, lambda e, i: i(e.tensor_scalar(
                g["w"][:, :], g["w"][:, :], gs["rden"][:, :], None,
                ALU.mult)), [g10])
            g12 = st(DVE, lambda e, i, s=self: i(e.tensor_scalar(
                s.w_bf[:, :], g["w"][:, :], s.SCALING, None, ALU.mult)),
                [g11] + self.pw_free[-1:])
            tw = st(PE, lambda e, i, s=self: i(e.transpose(
                s.pw, s.w_bf[:, :], s.ident_bf[:, :])),
                [g12, self.t_ident, self._pw_init] + self.pw_free[-1:])
            ev = st(DVE, lambda e, i, s=self: i(e.tensor_copy(
                s.wT_sb[:, :], s.pw)), [tw] + self.wT_free[-1:])
            self.pw_free.append(ev)
            tx = st(PE, lambda e, i, s=self: i(e.matmul(
                s.pwx, s.expand_bf[:, :], s.wT_sb[:, :],
                start=True, stop=True)),
                [ev, self.t_expand] + self.pwx_free[-1:])
            self.wT_free.append(tx)
            hwt = st(DVE, lambda e, i, s=self, tt=tt: i(e.tensor_tensor(
                s.hw[:, tt * 128:(tt + 1) * 128],
                s.hT[:, tt * 128:(tt + 1) * 128], s.pwx, ALU.mult)),
                [tx] + self.hT_toks[-1:])
            self.pwx_free.append(hwt)
            self.hw_toks.append(hwt)
            self.prev_gate.append(hwt)

    def emit_phase_b(self):
        bar = self.barrier_toks()
        st, DVE, ACT, PE = self.st, self.DVE, self.ACT, self.PE
        CH, G, T_keys, T_own = self.CH, self.G, self.T_keys, self.T_own

        self.esBF = ExitStack()
        sbBF = lambda n, sh, dt: self.esBF.enter_context(
            self.nc.sbuf_tensor(n, sh, dt))
        self.ctx_all = sbBF("ctx_all", [128, self.H * T_own], BF16)

        esB = ExitStack()
        sbB = lambda n, sh, dt: esB.enter_context(
            self.nc.sbuf_tensor(n, sh, dt))
        psB = lambda n, sh, dt: esB.enter_context(
            self.nc.psum_tensor(n, sh, dt))
        self.masks_sb = sbB("masks_sb", [128, 8 * 512], BF16)
        self.t_masks = self.cload(lambda e, i, s=self: i(e.dma_start(
            s.masks_sb[:, :].rearrange("p (m q) -> p m q", m=8),
            s.masksin.rearrange("m p q -> p m q"))), bar)

        kb = [sbB(f"kb{i}", [128, T_keys], BF16) for i in range(2)]
        vtok = [sbB(f"vtok{i}", [128, T_keys], BF16) for i in range(2)]
        pu = [sbB(f"pu{i}", [128, 1024], BF16) for i in range(4)]
        qb = [sbB(f"qb{i}", [128, 512], BF16) for i in range(4)]
        recip_sb = [sbB(f"recip_sb{i}", [1, 512], F32) for i in range(4)]
        bcast_sb = [sbB(f"bcast_sb{i}", [128, 512], F32) for i in range(4)]
        ctx_un = [sbB(f"ctx_un{i}", [128, 512], BF16) for i in range(4)]

        pS = [psB(f"pS{i}", [128, 1024], F32) for i in range(2)]
        pC = [psB(f"pC{i}", [128, 512], F32) for i in range(2)]
        pRS = [psB(f"pRS{i}", [128, 512], F32) for i in range(2)]

        kb_free = [[], []]
        vtok_free = [[], []]
        pu_free = [[], [], [], []]
        qb_free = [[], [], [], []]
        pS_free = [[], []]
        pC_free = [[], []]
        pRS_free = [[], []]
        recip_free = [[] for _ in range(4)]
        bcast_free = [[] for _ in range(4)]
        cun_free = [[] for _ in range(4)]

        GR = self.G * 128

        def load_kv(g_):
            ig = g_ % 2

            def f_k(e, i, s=self, g_=g_, ig=ig):
                for r in range(2):
                    dst = kb[ig][:, :].rearrange(
                        "p (b n) -> p b n", n=1024)[:, :, r * 512:(r + 1) * 512]
                    i(e.dma_start(
                        dst,
                        s.kg[r * GR + g_ * 128:r * GR + (g_ + 1) * 128,
                             :].rearrange("p (b n) -> p b n", n=512)))
            tk = st(self.c_k[ig], f_k,
                    bar + [self.t_ccK] + kb_free[ig][-1:], incs=2)

            def f_v(e, i, s=self, g_=g_, ig=ig):
                for c in range(s.CH):
                    r, sblk, j = (c % 8) // 4, c // 8, c % 4
                    i(e.dma_start(
                        vtok[ig][:, c * 128:(c + 1) * 128],
                        s.vg[r * GR + g_ * 128:r * GR + (g_ + 1) * 128,
                             sblk * 512 + j * 128:sblk * 512 + (j + 1) * 128],
                        transpose=True))
            tv = st(self.c_v[ig], f_v,
                    bar + [self.t_ccV] + vtok_free[ig][-1:],
                    incs=self.CH)
            return (tk, tv)

        kv_tok = {0: load_kv(0)}
        vt_toks = {0: [kv_tok[0][1]]}
        norm_pend = []

        HPG = self.HPG
        seq = []
        for g_ in range(G):
            for hp in range(HPG // 2):
                for s_ in range(self.n_slots):
                    seq.append((g_, HPG * g_ + 2 * hp, s_))

        q_tok = {}

        def load_q(h, s_, iq):
            return st(self.c_q[iq], lambda e, i, s=self, h=h, s_=s_,
                      iq=iq: i(e.dma_start(
                          qb[iq][:, :],
                          s.q_spill[h * 128:(h + 1) * 128,
                                    s_ * 512:(s_ + 1) * 512])),
                      bar + self.qspill_toks[-2:] + qb_free[iq][-1:])

        for idx in range(min(2, len(seq))):
            g_, h0, s_ = seq[idx]
            q_tok[(idx, 0)] = load_q(h0, s_, (2 * idx) % 4)
            q_tok[(idx, 1)] = load_q(h0 + 1, s_, (2 * idx + 1) % 4)

        pair_i = 0
        s0_pre = {}

        for idx, (g_, h0, s_) in enumerate(seq):
            ig = g_ % 2
            E_s = self.EPC * (s_ + 1)
            if h0 == HPG * g_ and s_ == 0 and g_ + 1 < G:
                kv_tok[g_ + 1] = load_kv(g_ + 1)
                vt_toks[g_ + 1] = [kv_tok[g_ + 1][1]]
            tq = [q_tok[(idx, 0)], q_tok[(idx, 1)]]
            iqs = [(2 * idx) % 4, (2 * idx + 1) % 4]
            n_pairs = E_s // 2

            def emit_S(p, j):
                c0 = 2 * p

                def f_s(e, i, s=self, ig=ig, j=j, c0=c0, iq=iqs[j]):
                    ins = None
                    for k in range(2):
                        ins = e.matmul(
                            pS[j][:, k * 512:(k + 1) * 512],
                            kb[ig][:, (c0 + k) * 128:(c0 + k + 1) * 128],
                            qb[iq][:, :], start=True, stop=True)
                    i(ins)
                return st(PE, f_s, [kv_tok[g_][0], tq[j]] + pS_free[j][-1:])

            tS = {}
            if idx in s0_pre:
                tS[(0, 0)], tS[(0, 1)] = s0_pre.pop(idx)
            else:
                tS[(0, 0)] = emit_S(0, 0)
                tS[(0, 1)] = emit_S(0, 1)
            ctx_last = [None, None]
            rs_last = [None, None]
            tM_tok = {}
            pu_of = {}

            def emit_CR(p, j):
                isp = pu_of[(p, j)]
                c0 = 2 * p

                def f_c(e, i, ig=ig, isp=isp, c0=c0, j=j,
                        first=(p == 0), last=(p == n_pairs - 1)):
                    ins = None
                    for k in range(2):
                        ins = e.matmul(
                            pC[j][:, :],
                            vtok[ig][:, (c0 + k) * 128:(c0 + k + 1) * 128],
                            pu[isp][:, k * 512:(k + 1) * 512],
                            start=(first and k == 0),
                            stop=(last and k == 1))
                    i(ins)
                tC = st(PE, f_c,
                        [tM_tok[(p, j)]] + vt_toks[g_] + pC_free[j][-1:])

                def f_r(e, i, s=self, isp=isp, j=j,
                        first=(p == 0), last=(p == n_pairs - 1)):
                    ins = None
                    for k in range(2):
                        ins = e.matmul(
                            pRS[j][0:1, :], s.ones_col[:, :],
                            pu[isp][:, k * 512:(k + 1) * 512],
                            start=(first and k == 0),
                            stop=(last and k == 1))
                    i(ins)
                tR = st(PE, f_r, [tM_tok[(p, j)], self.t_oc] +
                        (pRS_free[j][-1:] if p == 0 else []))
                pu_free[isp].append(tC)
                pu_free[isp].append(tR)
                ctx_last[j], rs_last[j] = tC, tR

            for p in range(n_pairs):
                for j in (0, 1):
                    isp = pair_i % 4
                    pair_i += 1
                    pu_of[(p, j)] = isp
                    c0 = 2 * p
                    tE = st(ACT, lambda e, i, isp=isp, j=j: i(e.activation(
                        pu[isp][:, :], pS[j][:, :], AF.Exp, scale=ISCALE)),
                        [tS[(p, j)]] + pu_free[isp][-2:])
                    pS_free[j].append(tE)
                    tM = tE
                    for k in range(2):
                        c = c0 + k
                        if c >= E_s - 8:
                            mi = c - (E_s - 8)

                            def f_m(e, i, s=self, isp=isp, k=k, mi=mi):
                                i(e.tensor_tensor(
                                    pu[isp][:, k * 512:(k + 1) * 512],
                                    pu[isp][:, k * 512:(k + 1) * 512],
                                    s.masks_sb[:, mi * 512:(mi + 1) * 512],
                                    ALU.mult))
                            tM = st(DVE, f_m, [tM, self.t_masks])
                    tM_tok[(p, j)] = tM
                for j in (0, 1):
                    if p + 1 < n_pairs:
                        tS[(p + 1, j)] = emit_S(p + 1, j)
                    if p >= 1:
                        emit_CR(p - 1, j)
            if idx + 1 < len(seq):
                g2, h2, s2 = seq[idx + 1]
                if g2 == g_ and (idx + 1, 0) in q_tok:
                    def emit_S0(jn, idxn=idx + 1, ig2=g2 % 2):
                        def f_s(e, i, s=self, ig=ig2, jn=jn,
                                iq=(2 * idxn + jn) % 4):
                            ins = None
                            for k in range(2):
                                ins = e.matmul(
                                    pS[jn][:, k * 512:(k + 1) * 512],
                                    kb[ig][:, k * 128:(k + 1) * 128],
                                    qb[iq][:, :], start=True, stop=True)
                            i(ins)
                        return st(PE, f_s,
                                  [kv_tok[g_][0], q_tok[(idxn, jn)]]
                                  + pS_free[jn][-1:])
                    s0_pre[idx + 1] = (emit_S0(0), emit_S0(1))
            for j in (0, 1):
                emit_CR(n_pairs - 1, j)
            for j in (0, 1):
                qb_free[iqs[j]].append(ctx_last[j])
            kb_free[ig].append(ctx_last[1])
            vtok_free[ig].append(ctx_last[1])
            if idx + 2 < len(seq):
                g2, h2, s2 = seq[idx + 2]
                q_tok[(idx + 2, 0)] = load_q(h2, s2, (2 * idx + 4) % 4)
                q_tok[(idx + 2, 1)] = load_q(h2 + 1, s2, (2 * idx + 5) % 4)
            for j in (0, 1):
                jv = 2 * j + (idx % 2)
                tEv = st(DVE, lambda e, i, j=j, jv=jv: i(e.tensor_copy(
                    ctx_un[jv][:, :], pC[j][:, :])),
                    [ctx_last[j]] + cun_free[jv][-1:])
                pC_free[j].append(tEv)
                tRec = st(DVE, lambda e, i, j=j, jv=jv: i(e.reciprocal(
                    recip_sb[jv][:, :], pRS[j][0:1, :])),
                    [rs_last[j]] + recip_free[jv][-1:])
                pRS_free[j].append(tRec)
                tBo = st(self.c_bc[jv], lambda e, i, s=self, jv=jv: i(
                    e.dma_start(s.rs_scr[jv], recip_sb[jv][:, :])),
                    [tRec, self.c_bc[jv].tok()])
                recip_free[jv].append(tBo)
                tBi = st(self.c_bc[jv], lambda e, i, s=self, jv=jv: i(
                    e.dma_start(bcast_sb[jv][:, :],
                                bass.AP(s.rs_scr[jv].tensor, 0,
                                        [[0, 128], [1, 512]]))),
                    [tBo] + bcast_free[jv][-1:])

                def mk_norm(s=self, h=h0 + j, s_=s_, jv=jv, tBi=tBi,
                            tEv=tEv):
                    def f_n(e, i):
                        i(e.tensor_tensor(
                            s.ctx_all[:, h * s.T_own + s_ * 512:
                                      h * s.T_own + (s_ + 1) * 512],
                            ctx_un[jv][:, :], bcast_sb[jv][:, :], ALU.mult))
                    tN = s.st(s.DVE, f_n, [tBi, tEv])
                    cun_free[jv].append(tN)
                    bcast_free[jv].append(tN)
                    s.ctx_toks.append(tN)
                norm_pend.append(mk_norm)
            while len(norm_pend) > 2:
                norm_pend.pop(0)()

        for fn in norm_pend:
            fn()
        esB.close()

    def emit_phase_f(self):
        bar = self.barrier_toks()
        st, DVE, PE = self.st, self.DVE, self.PE
        DC, T_own, OB = self.DC, self.T_own, self.OB
        WPC = max(1, DC // 8)
        NPC = DC // WPC
        esF = ExitStack()
        sbF = lambda n, sh, dt: esF.enter_context(
            self.nc.sbuf_tensor(n, sh, dt))
        psF = lambda n, sh, dt: esF.enter_context(
            self.nc.psum_tensor(n, sh, dt))
        wob = [sbF(f"wob{i}", [128, WPC * 512], BF16) for i in range(15)]
        bm_sb = [sbF(f"bm_sb{i}", [128, 512], BF16) for i in range(2)]
        xf = [sbF(f"xf{i}", [128, 512], F32) for i in range(2)]
        ob_sb = [sbF(f"ob_sb{i}", [128, 512], F32) for i in range(2)]
        pF = [psF(f"pF{i}", [128, 512], F32) for i in range(2)]

        wob_free = [[] for _ in range(15)]
        xf_free = [[], []]
        ob_free = [[], []]
        pF_free = [[], []]

        bm_free = [[], []]

        def load_bm(ob):
            ib = ob % 2
            return self.cload(lambda e, i, s=self, ob=ob, ib=ib: i(
                e.dma_start(bm_sb[ib][:, :],
                            s.bm_eff[:, ob * 512:(ob + 1) * 512])),
                bar + bm_free[ib][-1:])
        bm_toks = {0: load_bm(0)}
        if OB > 1:
            bm_toks[1] = load_bm(1)

        def load_wo(piece):
            ob, pc = piece // NPC, piece % NPC
            slot = piece % 15
            rs = ob * 512
            c0 = pc * WPC
            return st(self.c_wo[slot], lambda e, i, s=self, rs=rs, c0=c0,
                      slot=slot: i(e.dma_start(
                          wob[slot][:, :].rearrange(
                              "p (c n) -> p c n", n=512),
                          s.wo[c0 * 128:(c0 + WPC) * 128,
                               rs:rs + 512].rearrange(
                              "(c p) n -> p c n", p=128))),
                      bar + wob_free[slot][-1:])

        npieces = OB * NPC
        wo_toks = {}
        for pv_ in range(min(15, npieces)):
            wo_toks[pv_] = load_wo(pv_)

        xf_tok = {}

        def load_xf(ob, tt, idx):
            ix = idx % 2
            return st(self.c_xf[ix], lambda e, i, s=self, ob=ob, tt=tt,
                      ix=ix: i(e.dma_start(
                          xf[ix][:, :],
                          s.x_own[tt * 128:(tt + 1) * 128,
                                  ob * 512:(ob + 1) * 512])),
                      bar + xf_free[ix][-1:])

        idx = 0
        total = OB * self.NT
        for ob in range(OB):
            pieces = [ob * NPC + pc for pc in range(NPC)]
            slots = [pz % 15 for pz in pieces]
            for tt in range(self.NT):
                if idx < 2:
                    xf_tok[idx] = load_xf(ob, tt, idx)

                ipf = idx % 2

                def f_mm(e, i, s=self, tt=tt, ob=ob, ipf=ipf, slots=slots,
                         WPC=WPC):
                    for c in range(s.DC):
                        sl = slots[c // WPC]
                        cc = c % WPC
                        e.matmul(pF[ipf][:, :],
                                 s.ctx_all[:, c * s.T_own + tt * 128:
                                           c * s.T_own + (tt + 1) * 128],
                                 wob[sl][:, cc * 512:(cc + 1) * 512],
                                 start=(c == 0), stop=False)
                    ins = e.matmul(pF[ipf][:, :],
                                   s.hw[:, tt * 128:(tt + 1) * 128],
                                   bm_sb[ob % 2][:, :],
                                   start=False, stop=True)
                    i(ins)
                tmm = st(PE, f_mm,
                         [wo_toks[pz] for pz in pieces] + [bm_toks[ob]]
                         + self.ctx_toks[-1:] + self.hw_toks[-1:]
                         + pF_free[ipf][-1:])
                bm_free[ob % 2].append(tmm)
                for sl in set(slots):
                    wob_free[sl].append(tmm)
                ta = st(DVE, lambda e, i, ipf=ipf, ix=idx % 2: i(
                    e.tensor_tensor(ob_sb[ipf][:, :], pF[ipf][:, :],
                                    xf[ix][:, :], ALU.add)),
                    [tmm, xf_tok[idx]] + ob_free[ipf][-1:])
                pF_free[ipf].append(ta)
                xf_free[idx % 2].append(ta)
                tst = st(self.c_st[ipf], lambda e, i, s=self, ob=ob, tt=tt,
                         ipf=ipf: i(e.dma_start(
                             s.out[tt * 128:(tt + 1) * 128,
                                   ob * 512:(ob + 1) * 512],
                             ob_sb[ipf][:, :])), [ta])
                ob_free[ipf].append(tst)
                if idx + 2 < total:
                    ob2 = (idx + 2) // self.NT
                    tt2 = (idx + 2) % self.NT
                    xf_tok[idx + 2] = load_xf(ob2, tt2, idx + 2)
                idx += 1
            for pz in range((ob + 1) * NPC, (ob + 2) * NPC):
                if pz < npieces and pz not in wo_toks:
                    wo_toks[pz] = load_wo(pz)
            if ob + 2 < OB and (ob + 2) not in bm_toks:
                bm_toks[ob + 2] = load_bm(ob + 2)

        toks = [c.tok() for c in self.chans]
        st(self.c_st[0], lambda e, i: None,
           [t for t in toks if self.eng_of(t[0]) == "sync"], incs=0)
        st(self.c_const, lambda e, i: None,
           [t for t in toks if self.eng_of(t[0]) == "gpsimd"], incs=0)
        esF.close()
        self.esBF.close()


def build_program(D=4096, H=32, G=8, T_own=2048, T_keys=4096):
    return Builder(D, H, G, T_own, T_keys).build()


def _rope_tables(T, base=500000.0):
    inv = 1.0 / base ** (np.arange(0, 128, 2, dtype=np.float64) / 128.0)
    fr = np.outer(np.arange(T, dtype=np.float64), inv)
    return (np.cos(fr).astype(np.float32).T.copy(),
            np.sin(fr).astype(np.float32).T.copy())


def _masks(parity):
    m = np.zeros((8, 128, 512), np.float32)
    i = np.arange(128)[:, None]
    jq = np.arange(512)[None, :]
    for j in range(8):
        if parity == 0:
            if j < 4:
                m[j] = (128 * j + i <= jq).astype(np.float32)
        else:
            if j < 4:
                m[j] = 1.0
            else:
                m[j] = (128 * (j - 4) + i <= jq).astype(np.float32)
    return m


_NC_CACHE = {}


def _get_program():
    if "real" not in _NC_CACHE:
        _NC_CACHE["real"] = build_program()
    return _NC_CACHE["real"]


def kernel(L0_attn_x, L0_attn_Wq, L0_attn_Wk, L0_attn_Wv, L0_attn_Wo,
           L0_ffn_router_w, L0_ffn_router_b, L0_ffn_A, L0_ffn_Bm):
    x = np.ascontiguousarray(np.asarray(L0_attn_x, dtype=np.float32))
    B, T, D = x.shape
    H, G, E, R = 32, 8, 8, 16

    wq = np.ascontiguousarray(np.asarray(L0_attn_Wq, np.float32))
    wk = np.ascontiguousarray(np.asarray(L0_attn_Wk, np.float32))
    wv = np.ascontiguousarray(np.asarray(L0_attn_Wv, np.float32))
    wo = np.ascontiguousarray(np.asarray(L0_attn_Wo, np.float32))
    rw = np.ascontiguousarray(np.asarray(L0_ffn_router_w, np.float32))
    rb = np.ascontiguousarray(
        np.asarray(L0_ffn_router_b, np.float32)).reshape(1, E)
    A = np.asarray(L0_ffn_A, np.float32)
    Bm = np.asarray(L0_ffn_Bm, np.float32)
    a_t = np.ascontiguousarray(A.reshape(E * R, D).T)
    bm_eff = np.ascontiguousarray(Bm.transpose(0, 2, 1).reshape(E * R, D))

    cosT, sinT = _rope_tables(T)
    ident = np.eye(128, dtype=np.float32)
    expand = np.zeros((E, E * R), np.float32)
    for e in range(E):
        expand[e, e * R:(e + 1) * R] = 1.0

    idx = {}
    for p in (0, 1):
        blocks = np.arange(p * 512, T, 1024)
        idx[p] = np.concatenate([np.arange(b0, b0 + 512) for b0 in blocks])

    nc = _get_program()
    in_maps = []
    for c in range(8):
        b, p = c // 2, c % 2
        own, oth = idx[p], idx[1 - p]
        in_maps.append({
            "x_own": np.ascontiguousarray(x[b][own]),
            "wq": wq, "wk": wk, "wv": wv, "wo": wo,
            "a_t": a_t, "bm_eff": bm_eff,
            "router_w": rw, "router_b": rb,
            "cos_own": np.ascontiguousarray(cosT[:, own]),
            "sin_own": np.ascontiguousarray(sinT[:, own]),
            "masksin": _masks(p),
            "identin": ident,
            "expandin": expand,
        })
    res = run_bass_kernel_spmd(nc, in_maps, core_ids=list(range(8)))
    y = np.empty_like(x)
    for c in range(8):
        b, p = c // 2, c % 2
        y[b][idx[p]] = res.results[c]["out"]
    return y


# version 28
# speedup vs baseline: 1.0017x; 1.0017x over previous
import numpy as np
import concourse.bass as bass
import concourse.mybir as mybir
from concourse.bass_utils import run_bass_kernel_spmd
from contextlib import ExitStack

F32 = mybir.dt.float32
BF16 = mybir.dt.bfloat16
AF = mybir.ActivationFunctionType
ALU = mybir.AluOpType
AX = mybir.AxisListType

ISCALE = 0.08838834764831845


class Chan:
    __slots__ = ("name", "inc", "count")

    def __init__(self, name, inc=1):
        self.name = name
        self.inc = inc
        self.count = 0

    def tok(self):
        return (self, self.count)


class Plan:

    def __init__(self):
        self.steps = []

    def step(self, engine_attr, chan, fn, waits=(), incs=1):
        for (c, v) in waits:
            assert v <= c.count, f"future token {c.name}: {v} > {c.count}"
        self.steps.append((engine_attr, chan,
                           [w for w in waits if w[1] > 0], fn, incs))
        chan.count += incs * chan.inc
        return chan.tok()

    def replay(self, block, sems, waited):
        per_engine = {}
        for s in self.steps:
            per_engine.setdefault(s[0], []).append(s)

        def mk(ea, steps):
            def body(eng):
                w_ = waited.setdefault(ea, {})
                for (_ea, ch, waits, fn, incs) in steps:
                    for (wc, val) in waits:
                        if w_.get(wc.name, 0) >= val:
                            continue
                        eng.wait_ge(sems[wc.name], val)
                        w_[wc.name] = val

                    def inc(instr, cc=ch):
                        instr.then_inc(sems[cc.name], cc.inc)
                        return instr
                    fn(eng, inc)
            return body
        for ea, steps in per_engine.items():
            getattr(block, ea)(mk(ea, steps))


class Builder:
    def __init__(self, D=4096, H=32, G=8, T_own=2048, T_keys=4096):
        self.D, self.H, self.G = D, H, G
        self.T_own, self.T_keys = T_own, T_keys
        self.DC = D // 128
        self.E, self.R = 8, 16
        self.SCALING = 8.0 / self.R
        self.n_slots = T_own // 512
        self.CH = T_keys // 128
        self.EPC = self.CH // self.n_slots
        assert self.EPC == 8 and T_keys - T_own == T_own
        self.NB = T_own // 512
        self.HPG = H // G
        self.NT = T_own // 128
        self.OB = D // 512
        self.D2 = D // 2
        self.TGRP = min(16, self.DC)

        self.nc = bass.Bass("TRN2", target_bir_lowering=False)
        self.plan = None
        self.chans = []
        self.es = ExitStack()
        self.waited = {}

    def chan(self, name, inc=1):
        c = Chan(name, inc)
        self.chans.append(c)
        return c

    def st(self, ch, fn, waits=(), incs=1):
        return self.plan.step(self.eng_of(ch), ch, fn, waits, incs)

    def cload(self, fn, waits=()):
        t = self.st(self.c_const, fn, list(waits) + self._const_chain)
        self._const_chain = [t]
        return t

    def eng_of(self, ch):
        if ch is self.PE:
            return "tensor"
        if ch is self.DVE:
            return "vector"
        if ch is self.ACT:
            return "scalar"
        if ch is self.POOL or ch is self.c_cc:
            return "gpsimd"
        return "gpsimd" if ch.name in self.gp_ch else "sync"

    def run_phase(self, emit_fn):
        self.plan = Plan()
        emit_fn()
        with self.nc.Block() as block:
            self.plan.replay(block, self.sems, self.waited)
        self.plan = None

    def barrier_toks(self):
        return [c.tok() for c in self.chans]

    def build(self):
        self.declare_io()
        self.declare_channels()
        self.sems = {c.name: self.es.enter_context(
            self.nc.semaphore(c.name)) for c in self.chans}
        self.declare_persistent()
        self.ctx_toks = []
        self._const_chain = []
        self.run_phase(self.emit_phase_a)
        self.run_phase(self.emit_phase_b)
        self.run_phase(self.emit_phase_f)
        self.es.close()
        return self.nc

    def declare_io(self):
        nc, D = self.nc, self.D
        T_own, G, H, E, R = self.T_own, self.G, self.H, self.E, self.R

        def din(name, shape):
            return nc.dram_tensor(name, list(shape), F32,
                                  kind="ExternalInput").ap()
        self.x_own = din("x_own", (T_own, D))
        self.wq = din("wq", (D, H * 128))
        self.wk = din("wk", (D, G * 128))
        self.wv = din("wv", (D, G * 128))
        self.wo = din("wo", (H * 128, D))
        self.a_t = din("a_t", (D, E * R))
        self.bm_eff = din("bm_eff", (E * R, D))
        self.router_w = din("router_w", (E, D))
        self.router_b = din("router_b", (1, E))
        self.cos_own = din("cos_own", (64, T_own))
        self.sin_own = din("sin_own", (64, T_own))
        self.masksin = din("masksin", (8, 128, 512))
        self.identin = din("identin", (128, 128))
        self.expandin = din("expandin", (E, E * R))
        self.out = nc.dram_tensor("out", [T_own, D], F32,
                                  kind="ExternalOutput").ap()
        self.rs_scr = [nc.dram_tensor(f"rs_scr{j}", [1, 512], F32).ap()
                       for j in range(4)]
        self.q_spill = nc.dram_tensor("q_spill", [H * 128, T_own], BF16).ap()
        self.k_spill = nc.dram_tensor("k_spill", [G * 128, T_own], BF16).ap()
        self.v_spill = nc.dram_tensor("v_spill", [G * 128, T_own], BF16).ap()
        self.kg = nc.dram_tensor("kg", [2 * G * 128, T_own], BF16).ap()
        self.vg = nc.dram_tensor("vg", [2 * G * 128, T_own], BF16).ap()

    def declare_channels(self):
        C = self.chan
        self.PE, self.DVE, self.ACT, self.POOL = (C("pe"), C("dve"),
                                                  C("act"), C("pool"))
        self.c_x = [C(f"c_x{i}", 16) for i in range(4)]
        self.c_rope = [C(f"c_rope{i}", 16) for i in range(2)]
        self.c_spq = [C(f"c_spq{i}", 16) for i in range(2)]
        self.c_spk = [C(f"c_spk{i}", 16) for i in range(2)]
        self.c_spv = [C(f"c_spv{i}", 16) for i in range(2)]
        self.c_q = [C(f"c_q{i}", 16) for i in range(4)]
        self.c_k = [C(f"c_k{i}", 16) for i in range(2)]
        self.c_v = [C(f"c_v{i}", 16) for i in range(2)]
        self.c_st = [C(f"c_st{i}", 16) for i in range(2)]
        self.c_xf = [C(f"c_xf{i}", 16) for i in range(2)]
        self.c_w = [C(f"c_w{i}", 16) for i in range(2)]
        self.c_wo = [C(f"c_wo{i}", 16) for i in range(15)]
        self.c_const = C("c_const", 16)
        self.c_cc = C("c_cc", 1)
        self.c_bc = [C(f"c_bc{i}", 16) for i in range(4)]
        self.gp_ch = set(c.name for c in self.c_w + self.c_wo + self.c_rope
                         + self.c_x + [self.c_const])
        self.sync_extra = set(c.name for c in self.c_bc)

    def declare_persistent(self):
        E, R, DC, H, T_own = self.E, self.R, self.DC, self.H, self.T_own
        sb = lambda n, sh, dt: self.es.enter_context(
            self.nc.sbuf_tensor(n, sh, dt))
        self.ones_col = sb("ones_col", [128, 1], BF16)
        self.ones_row = sb("ones_row", [1, 128], BF16)
        self.ones_row32 = sb("ones_row32", [1, 128], F32)
        self.ident_bf = sb("ident_bf", [128, 128], BF16)
        self.expand_bf = sb("expand_bf", [E, E * R], BF16)
        self.rb_sb = sb("rb_sb", [1, E], BF16)
        self.rwT = sb("rwT", [128, DC * E], BF16)
        self.hw = sb("hw", [128, T_own], BF16)


    def emit_phase_a(self):
        st, DVE, ACT, PE, POOL = (self.st, self.DVE, self.ACT, self.PE,
                                  self.POOL)
        D, D2, DC, T_own = self.D, self.D2, self.DC, self.T_own
        cc = self.c_const

        esA = ExitStack()
        sbA = lambda n, sh, dt: esA.enter_context(
            self.nc.sbuf_tensor(n, sh, dt))
        psA = lambda n, sh, dt: esA.enter_context(
            self.nc.psum_tensor(n, sh, dt))

        self.t_ident = self.cload(lambda e, i, s=self: i(e.dma_start(
            s.ident_bf[:, :], s.identin)))
        self.t_expand = self.cload(lambda e, i, s=self: i(e.dma_start(
            s.expand_bf[:, :], s.expandin)))
        self.t_rb = self.cload(lambda e, i, s=self: i(e.dma_start(
            s.rb_sb[:, :], s.router_b)))
        self.t_oc = st(POOL, lambda e, i, s=self: i(e.memset(
            s.ones_col[:, :], 1.0)))
        self.t_or = st(POOL, lambda e, i, s=self: i(e.memset(
            s.ones_row[:, :], 1.0)))
        self.t_or32 = st(POOL, lambda e, i, s=self: i(e.memset(
            s.ones_row32[:, :], 1.0)))

        rw_tmp = sbA("rw_tmp", [self.E, 128], BF16)
        pMiscF = psA("pMiscF", [128, 8 + 128], F32)
        pMiscB = psA("pMiscB", [128, 136], BF16)
        self.pl = pMiscF[:, 0:self.E]
        self.pwx = pMiscF[:, 8:136]
        self.pw = pMiscB[0:self.E, 0:128]
        self._rw_tmp = rw_tmp
        self._p_rw = pMiscB[:, 128:136]

        self.xnT = sbA("xnT", [128, DC * T_own], BF16)
        self.hT = sbA("hT", [128, self.T_own], BF16)
        self.xnT3 = self.xnT[:, :].rearrange("p (c t) -> p c t", c=DC)
        self.xbuf = [sbA(f"xbuf{i}", [128, D2], BF16) for i in range(4)]
        self.xn_tok = sbA("xn_tok", [128, D], BF16)
        self.wbuf = [sbA(f"wbuf{i}", [128, DC * 128], BF16) for i in range(2)]
        _stgp = [sbA(f"stg{i}", [128, 512], BF16) for i in range(2)]
        self.stg = {k: _stgp for k in "qkv"}
        self.cos_sl = sbA("cos_sl", [64, T_own], BF16)
        self.sin_sl = sbA("sin_sl", [64, T_own], BF16)
        self.rtmp = [sbA(f"rtmp{i}", [64, 512], F32) for i in range(2)]
        self.stats2 = []
        for k in range(2):
            d = {nm: sbA(f"st{k}_" + nm, [128, 1], F32)
                 for nm in ["dd", "s0", "r0", "s1", "rstd", "nb"]}
            d["bn"] = sbA(f"st{k}_bn", [128, 2 * max(1, self.D2 // 512) * 6], F32)
            d["mv"] = sbA(f"st{k}_mv", [128, 2], F32)
            self.stats2.append(d)
        self.gt = {nm: sbA("gt_" + nm, [128, self.E], F32)
                   for nm in ["t", "i1", "tm", "i2", "ind", "et", "w"]}
        self.gs = {nm: sbA("gs_" + nm, [128, 1], F32)
                   for nm in ["m1", "m2", "em2", "den", "rden"]}
        self.w_bf = sbA("w_bf", [128, self.E], BF16)
        self.wT_sb = sbA("wT_sb", [self.E, 128], BF16)

        self.pA = [psA(f"pA{i}", [128, 512], F32) for i in range(2)]
        self.pT = [psA(f"pT{i}", [128, self.TGRP * 128], BF16)
                   for i in range(2)]

        self.pA_free = [[] for _ in range(2)]
        self.pT_free = [[] for _ in range(2)]
        self.xbuf_free = [[] for _ in range(4)]
        _stgf = [[], []]
        self.stg_free = {k: _stgf for k in "qkv"}
        self.wbuf_free = [[] for _ in range(2)]
        self.rtmp_free = [[], []]
        self.pl_free = []
        self.pw_free = []
        self.pwx_free = []
        self.wT_free = []

        self.pA_i = 0
        self.w_i = 0
        _stgi = [0]
        self.stg_i = {k: _stgi for k in "qkv"}
        self.xt_i = 0
        self.kspill_toks = [[] for _ in range(self.G)]
        self.vspill_toks = [[] for _ in range(self.G)]
        self.qspill_toks = []
        self.hT_toks = []
        self.hw_toks = []
        self.xnT_tb_tok = {}
        self.rope_users = []
        self.prev_tile_norm = []
        self.prev_gate = []
        self.last_wmm = []
        self.pend_evac = []
        self.tb_evac_tok = []
        self.last_ttr = []
        self.pend_post = []

        self.emit_slab("own")

        esA.close()

    def emit_slab(self, slab):
        st = self.st
        rl1 = st(self.c_rope[0], lambda e, i, s=self: i(
            e.dma_start(s.cos_sl[:, :], s.cos_own)))
        rl2 = st(self.c_rope[1], lambda e, i, s=self: i(
            e.dma_start(s.sin_sl[:, :], s.sin_own)))
        self.rope_now = [rl1, rl2]

        lw0, iw0 = self.load_wslab(self.wk, 0)
        for tb in range(self.NB):
            for t in range(tb * 4, tb * 4 + 4):
                self.emit_stage1_tile(slab, t)
            for fn in self.pend_post:
                fn()
            self.pend_post = []
            self.flush_evacs()
            self.xnT_tb_tok[tb] = self.tb_evac_tok[-1]
            self.emit_wcell(slab, "k", 0, tb, lw0, iw0)

        self.emit_wpass(slab, self.wk, self.G, "k", skip_first=True)
        self.emit_wpass(slab, self.wv, self.G, "v")
        groups = [[0, 1], [2, 3], [4, 5], [6, 7]]
        kw = [t for toks in self.kspill_toks for t in toks[-2:]]
        self.t_ccK = st(self.c_cc, lambda e, i, s=self, g=groups: i(
            e.collective_compute("AllGather", ALU.bypass,
                                 replica_groups=g,
                                 ins=[s.k_spill.opt()],
                                 outs=[s.kg.opt()])), kw)
        vw = [t for toks in self.vspill_toks for t in toks[-2:]]
        self.t_ccV = st(self.c_cc, lambda e, i, s=self, g=groups: i(
            e.collective_compute("AllGather", ALU.bypass,
                                 replica_groups=g,
                                 ins=[s.v_spill.opt()],
                                 outs=[s.vg.opt()])), vw)
        self.emit_wpass(slab, self.wq, self.H, "q")
        self.emit_wpass(slab, self.a_t, 1, "h")
        self.emit_rwt_init()
        self.emit_router()

    def emit_stage1_tile(self, slab, t):
        st, DVE, ACT = self.st, self.DVE, self.ACT
        x_src = self.x_own if slab == "own" else self.x_oth
        DC = self.DC
        ia = self.xt_i % 4
        ib = (self.xt_i + 1) % 4
        self.xt_i += 2
        S = self.stats2[(self.xt_i // 2) % 2]
        for fn in self.pend_post:
            fn()
        self.pend_post = []
        self.flush_evacs()
        la = st(self.c_x[ia], lambda e, i, s=self, t=t, ia=ia, x=x_src: i(
            e.dma_start(s.xbuf[ia][:, :], x[t * 128:(t + 1) * 128, 0:s.D2])),
            self.xbuf_free[ia][-1:] + self.prev_tile_norm[-2:-1])
        lb = st(self.c_x[ib], lambda e, i, s=self, t=t, ib=ib, x=x_src: i(
            e.dma_start(s.xbuf[ib][:, :], x[t * 128:(t + 1) * 128,
                                            s.D2:s.D])),
            self.xbuf_free[ib][-1:] + self.prev_tile_norm[-2:-1])
        pv = self.prev_tile_norm[-3:-1]
        ck = min(512, self.D2)
        nst = self.D2 // ck

        def f_bn(e, i, s=self, ia=ia, ib=ib, nst=nst, ck=ck):
            ins = None
            for half, ix in ((0, ia), (1, ib)):
                for c in range(nst):
                    ins = e.bn_stats(
                        S["bn"][:, (half * nst + c) * 6:
                                (half * nst + c + 1) * 6],
                        s.xbuf[ix][:, c * ck:(c + 1) * ck])
            i(ins)
        s2 = st(DVE, f_bn, [la, lb] + pv)
        s4 = st(DVE, lambda e, i: i(e.bn_aggr(
            S["mv"][:, :], S["bn"][:, :])), [s2])
        s9 = st(DVE, lambda e, i: i(e.tensor_scalar(
            S["dd"][:, :], S["mv"][:, 1:2], 1e-5, None, ALU.add)), [s4])
        a1 = st(ACT, lambda e, i: i(e.activation(
            S["s0"][:, :], S["dd"][:, :], AF.Sqrt)), [s9])

        def mk_post(s=self, S=S, a1=a1, ia=ia, ib=ib, t=t):
            st = s.st
            s10 = st(s.DVE, lambda e, i: i(e.reciprocal(
                S["r0"][:, :], S["s0"][:, :])), [a1])
            s11 = st(s.DVE, lambda e, i: i(e.tensor_tensor(
                S["s1"][:, :], S["dd"][:, :], S["r0"][:, :], ALU.mult)),
                [s10])
            s12 = st(s.DVE, lambda e, i: i(e.tensor_tensor(
                S["s1"][:, :], S["s1"][:, :], S["s0"][:, :], ALU.add)),
                [s11])
            s13 = st(s.DVE, lambda e, i: i(e.tensor_scalar(
                S["s1"][:, :], S["s1"][:, :], 0.5, None, ALU.mult)), [s12])
            s14 = st(s.DVE, lambda e, i: i(e.reciprocal(
                S["rstd"][:, :], S["s1"][:, :])), [s13])
            s16 = st(s.DVE, lambda e, i: i(e.scalar_tensor_tensor(
                S["nb"][:, :], S["mv"][:, 0:1], -1.0, S["rstd"][:, :],
                ALU.mult, ALU.mult)), [s14])
            n1 = st(s.ACT, lambda e, i: i(e.activation(
                s.xn_tok[:, 0:s.D2], s.xbuf[ia][:, :], AF.Identity,
                bias=S["nb"][:, :], scale=S["rstd"][:, :])),
                [s16, a1] + s.last_ttr[-1:])
            n2 = st(s.ACT, lambda e, i: i(e.activation(
                s.xn_tok[:, s.D2:s.D], s.xbuf[ib][:, :], AF.Identity,
                bias=S["nb"][:, :], scale=S["rstd"][:, :])), [s16, n1])
            s.prev_tile_norm.append(n2)
            s.xbuf_free[ia].append(n1)
            s.xbuf_free[ib].append(n2)
            nr = (s.DC + s.TGRP - 1) // s.TGRP
            for r in range(nr):
                ir = r % 2
                c0 = r * s.TGRP
                ncs = min(s.TGRP, s.DC - c0)

                def f_tr(e, i, s=s, c0=c0, ncs=ncs, ir=ir):
                    ins = None
                    for k in range(ncs):
                        ins = e.transpose(
                            s.pT[ir][:, k * 128:(k + 1) * 128],
                            s.xn_tok[:, (c0 + k) * 128:(c0 + k + 1) * 128],
                            s.ident_bf[:, :])
                    i(ins)
                ttr = st(s.PE, f_tr,
                         [n1, n2, s.t_ident] + s.pT_free[ir][-1:])
                s.last_ttr.append(ttr)

                def mk_ev(s=s, c0=c0, ncs=ncs, ir=ir, t=t, ttr=ttr):
                    def f_ev(e, i):
                        dst = s.xnT3[:, c0:c0 + ncs, t * 128:(t + 1) * 128]
                        src = s.pT[ir][:, 0:ncs * 128].rearrange(
                            "p (c k) -> p c k", c=ncs)
                        i(e.tensor_copy(dst, src))

                    def f_ev_act(e, i):
                        dst = s.xnT3[:, c0:c0 + ncs, t * 128:(t + 1) * 128]
                        src = s.pT[ir][:, 0:ncs * 128].rearrange(
                            "p (c k) -> p c k", c=ncs)
                        i(e.activation(dst, src, AF.Identity))
                    if ir == 0:
                        tev = s.st(s.DVE, f_ev, [ttr] + s.last_wmm)
                    else:
                        tev = s.st(s.ACT, f_ev_act, [ttr] + s.last_wmm)
                    s.pT_free[ir].append(tev)
                    s.tb_evac_tok.append(tev)
                s.pend_evac.append(mk_ev)
        self.pend_post.append(mk_post)
        return None

    def load_wslab(self, whbm, j):
        iw = self.w_i % 2
        self.w_i += 1
        lw = self.st(self.c_w[iw], lambda e, i, s=self, j=j, iw=iw,
                     w=whbm: i(
            e.dma_start(s.wbuf[iw][:, :].rearrange(
                            "p (c n) -> p c n", n=128),
                        w[:, j * 128:(j + 1) * 128].rearrange(
                            "(c p) n -> p c n", p=128))),
            self.wbuf_free[iw][-1:])
        return lw, iw

    def emit_wcell(self, slab, kind, j, tb, lw, iw):
        ip = self.pA_i % 2
        self.pA_i += 1

        def f_mm(e, i, s=self, tb=tb, iw=iw, ip=ip):
            ins = None
            for c in range(s.DC):
                ins = e.matmul(
                    s.pA[ip][:, :],
                    s.wbuf[iw][:, c * 128:(c + 1) * 128],
                    s.xnT[:, c * s.T_own + tb * 512:
                          c * s.T_own + (tb + 1) * 512],
                    start=(c == 0), stop=(c == s.DC - 1))
            i(ins)
        tmm = self.st(self.PE, f_mm,
                      [lw, self.xnT_tb_tok[tb]] + self.pA_free[ip][-1:])
        self.last_wmm = [tmm]
        self.wbuf_free[iw].append(tmm)
        self.emit_epilogue(slab, kind, j, tb, ip, tmm)

    def flush_evacs(self):
        for fn in self.pend_evac:
            fn()
        self.pend_evac = []

    def emit_wpass(self, slab, whbm, dout_chunks, kind, skip_first=False):
        st = self.st
        for j in range(1 if skip_first else 0, dout_chunks):
            lw, iw = self.load_wslab(whbm, j)
            for tb in range(self.NB):
                self.emit_wcell(slab, kind, j, tb, lw, iw)

    def emit_epilogue(self, slab, kind, j, tb, ip, tmm):
        st, DVE, ACT = self.st, self.DVE, self.ACT
        perm_off = 512 * tb
        if kind in ("k", "q"):
            ir = self.stg_i[kind][0] % 2
            self.stg_i[kind][0] += 1
            stg = self.stg[kind][ir]
            cA = self.cos_sl[:, tb * 512:(tb + 1) * 512]
            sA = self.sin_sl[:, tb * 512:(tb + 1) * 512]
            m1 = st(DVE, lambda e, i, s=self, ip=ip, cA=cA: i(e.tensor_tensor(
                s.rtmp[0][:, :], s.pA[ip][0:64, :], cA, ALU.mult)),
                [tmm] + self.rope_now + self.rtmp_free[0][-1:])
            m2 = st(DVE, lambda e, i, s=self, ip=ip, sA=sA: i(e.tensor_tensor(
                s.rtmp[1][:, :], s.pA[ip][64:128, :], sA, ALU.mult)),
                [m1] + self.rtmp_free[1][-1:])
            o1 = st(DVE, lambda e, i, s=self, stg=stg: i(e.tensor_tensor(
                stg[0:64, :], s.rtmp[0][:, :], s.rtmp[1][:, :],
                ALU.subtract)), [m2] + self.stg_free[kind][ir][-1:])
            m3 = st(DVE, lambda e, i, s=self, ip=ip, sA=sA: i(e.tensor_tensor(
                s.rtmp[0][:, :], s.pA[ip][0:64, :], sA, ALU.mult)), [o1])
            m4 = st(DVE, lambda e, i, s=self, ip=ip, cA=cA: i(e.tensor_tensor(
                s.rtmp[1][:, :], s.pA[ip][64:128, :], cA, ALU.mult)), [m3])
            o2 = st(DVE, lambda e, i, s=self, stg=stg: i(e.tensor_tensor(
                stg[64:128, :], s.rtmp[0][:, :], s.rtmp[1][:, :], ALU.add)),
                [m4])
            self.rtmp_free[0].append(o2)
            self.rtmp_free[1].append(o2)
            self.pA_free[ip].append(o2)
            self.rope_users.append(o2)
            if kind == "k":
                sp = st(self.c_spk[ir], lambda e, i, s=self, j=j, stg=stg,
                        po=perm_off: i(e.dma_start(
                            s.k_spill[j * 128:(j + 1) * 128, po:po + 512],
                            stg[:, :])), [o2])
                self.kspill_toks[j].append(sp)
            else:
                sp = st(self.c_spq[ir], lambda e, i, s=self, j=j, stg=stg,
                        tb=tb: i(e.dma_start(
                            s.q_spill[j * 128:(j + 1) * 128,
                                      tb * 512:(tb + 1) * 512], stg[:, :])),
                        [o2])
                self.qspill_toks.append(sp)
            self.stg_free[kind][ir].append(sp)
        elif kind == "v":
            ir = self.stg_i["v"][0] % 2
            self.stg_i["v"][0] += 1
            ev = st(ACT, lambda e, i, s=self, ip=ip, ir=ir: i(e.activation(
                s.stg["v"][ir][:, :], s.pA[ip][:, :], AF.Identity)),
                [tmm] + self.stg_free["v"][ir][-1:])
            self.pA_free[ip].append(ev)
            sp = st(self.c_spv[ir], lambda e, i, s=self, j=j, ir=ir,
                    po=perm_off: i(e.dma_start(
                        s.v_spill[j * 128:(j + 1) * 128, po:po + 512],
                        s.stg["v"][ir][:, :])), [ev])
            self.vspill_toks[j].append(sp)
            self.stg_free["v"][ir].append(sp)
        elif kind == "h":
            ev = st(ACT, lambda e, i, s=self, ip=ip, tb=tb: i(e.activation(
                s.hT[:, tb * 512:(tb + 1) * 512], s.pA[ip][:, :],
                AF.Identity)), [tmm])
            self.pA_free[ip].append(ev)
            self.hT_toks.append(ev)

    def emit_rwt_init(self):
        st, PE, DVE = self.st, self.PE, self.DVE
        rw_tmp, p_rw = self._rw_tmp, self._p_rw
        prev = []
        for c in range(self.DC):
            tl = self.cload(lambda e, i, s=self, c=c, rw=rw_tmp: i(
                e.dma_start(rw[:, :],
                            s.router_w[:, c * 128:(c + 1) * 128])), prev)
            tt = st(PE, lambda e, i, s=self, rw=rw_tmp, p=p_rw: i(
                e.transpose(p, rw[:, :], s.ident_bf[0:s.E, 0:s.E])),
                [tl, self.t_ident] + prev)
            tv = st(DVE, lambda e, i, s=self, c=c, p=p_rw: i(e.tensor_copy(
                s.rwT[:, c * s.E:(c + 1) * s.E], p)), [tt])
            prev = [tv]
        self.t_rwT = prev[0]
        self._pw_init = prev[0]

    def emit_router(self):
        st, DVE, ACT, PE = self.st, self.DVE, self.ACT, self.PE
        g, gs = self.gt, self.gs
        for tt in range(self.NT):
            def f_rmm(e, i, s=self, tt=tt):
                for c in range(s.DC):
                    e.matmul(s.pl,
                             s.xnT[:, c * s.T_own + tt * 128:
                                   c * s.T_own + (tt + 1) * 128],
                             s.rwT[:, c * s.E:(c + 1) * s.E],
                             start=(c == 0), stop=False)
                ins = e.matmul(s.pl, s.ones_row[:, :], s.rb_sb[:, :],
                               start=False, stop=True)
                i(ins)
            pg = self.prev_gate[-2:]
            tlg = st(PE, f_rmm,
                     [self.t_rwT, self.t_rb, self.t_or,
                      self.xnT_tb_tok[tt // 4]] + self.pl_free[-1:] + pg)
            g0 = st(DVE, lambda e, i, s=self: i(e.reduce_max(
                gs["m1"][:, :], s.pl, axis=AX.X)), [tlg] + pg)
            g1 = st(DVE, lambda e, i, s=self: i(e.tensor_scalar(
                g["t"][:, :], s.pl, gs["m1"][:, :], None,
                ALU.subtract)), [g0])
            self.pl_free.append(g1)
            g2 = st(DVE, lambda e, i: i(e.tensor_scalar(
                g["i1"][:, :], g["t"][:, :], 0.0, None, ALU.is_ge)), [g1])
            g3 = st(DVE, lambda e, i: i(e.tensor_scalar(
                g["tm"][:, :], g["i1"][:, :], -1e9, None, ALU.mult)), [g2])
            g4 = st(DVE, lambda e, i: i(e.tensor_tensor(
                g["tm"][:, :], g["t"][:, :], g["tm"][:, :], ALU.add)), [g3])
            g5 = st(DVE, lambda e, i: i(e.reduce_max(
                gs["m2"][:, :], g["tm"][:, :], axis=AX.X)), [g4])
            g6 = st(DVE, lambda e, i: i(e.tensor_scalar(
                g["i2"][:, :], g["tm"][:, :], gs["m2"][:, :], None,
                ALU.is_ge)), [g5])
            g7 = st(DVE, lambda e, i: i(e.tensor_tensor(
                g["ind"][:, :], g["i1"][:, :], g["i2"][:, :], ALU.add)),
                [g6])
            a1 = st(ACT, lambda e, i: i(e.activation(
                g["et"][:, :], g["t"][:, :], AF.Exp)), [g1] + pg)
            a2 = st(ACT, lambda e, i: i(e.activation(
                gs["em2"][:, :], gs["m2"][:, :], AF.Exp)), [g5, a1])
            g8 = st(DVE, lambda e, i: i(e.tensor_scalar(
                gs["den"][:, :], gs["em2"][:, :], 1.0, None, ALU.add)),
                [a2, g7])
            g9 = st(DVE, lambda e, i: i(e.reciprocal(
                gs["rden"][:, :], gs["den"][:, :])), [g8])
            g10 = st(DVE, lambda e, i: i(e.tensor_tensor(
                g["w"][:, :], g["ind"][:, :], g["et"][:, :], ALU.mult)),
                [g9, a1])
            g11 = st(DVE, lambda e, i: i(e.tensor_scalar(
                g["w"][:, :], g["w"][:, :], gs["rden"][:, :], None,
                ALU.mult)), [g10])
            g12 = st(DVE, lambda e, i, s=self: i(e.tensor_scalar(
                s.w_bf[:, :], g["w"][:, :], s.SCALING, None, ALU.mult)),
                [g11] + self.pw_free[-1:])
            tw = st(PE, lambda e, i, s=self: i(e.transpose(
                s.pw, s.w_bf[:, :], s.ident_bf[:, :])),
                [g12, self.t_ident, self._pw_init] + self.pw_free[-1:])
            ev = st(DVE, lambda e, i, s=self: i(e.tensor_copy(
                s.wT_sb[:, :], s.pw)), [tw] + self.wT_free[-1:])
            self.pw_free.append(ev)
            tx = st(PE, lambda e, i, s=self: i(e.matmul(
                s.pwx, s.expand_bf[:, :], s.wT_sb[:, :],
                start=True, stop=True)),
                [ev, self.t_expand] + self.pwx_free[-1:])
            self.wT_free.append(tx)
            hwt = st(DVE, lambda e, i, s=self, tt=tt: i(e.tensor_tensor(
                s.hw[:, tt * 128:(tt + 1) * 128],
                s.hT[:, tt * 128:(tt + 1) * 128], s.pwx, ALU.mult)),
                [tx] + self.hT_toks[-1:])
            self.pwx_free.append(hwt)
            self.hw_toks.append(hwt)
            self.prev_gate.append(hwt)

    def emit_phase_b(self):
        bar = self.barrier_toks()
        st, DVE, ACT, PE = self.st, self.DVE, self.ACT, self.PE
        CH, G, T_keys, T_own = self.CH, self.G, self.T_keys, self.T_own

        self.esBF = ExitStack()
        sbBF = lambda n, sh, dt: self.esBF.enter_context(
            self.nc.sbuf_tensor(n, sh, dt))
        self.ctx_all = sbBF("ctx_all", [128, self.H * T_own], BF16)

        esB = ExitStack()
        sbB = lambda n, sh, dt: esB.enter_context(
            self.nc.sbuf_tensor(n, sh, dt))
        psB = lambda n, sh, dt: esB.enter_context(
            self.nc.psum_tensor(n, sh, dt))
        self.masks_sb = sbB("masks_sb", [128, 8 * 512], BF16)
        self.t_masks = self.cload(lambda e, i, s=self: i(e.dma_start(
            s.masks_sb[:, :].rearrange("p (m q) -> p m q", m=8),
            s.masksin.rearrange("m p q -> p m q"))), bar)

        kb = [sbB(f"kb{i}", [128, T_keys], BF16) for i in range(2)]
        vtok = [sbB(f"vtok{i}", [128, T_keys], BF16) for i in range(2)]
        pu = [sbB(f"pu{i}", [128, 1024], BF16) for i in range(4)]
        qb = [sbB(f"qb{i}", [128, 512], BF16) for i in range(4)]
        recip_sb = [sbB(f"recip_sb{i}", [1, 512], F32) for i in range(4)]
        bcast_sb = [sbB(f"bcast_sb{i}", [128, 512], F32) for i in range(4)]
        ctx_un = [sbB(f"ctx_un{i}", [128, 512], BF16) for i in range(4)]

        pS = [psB(f"pS{i}", [128, 1024], F32) for i in range(2)]
        pC = [psB(f"pC{i}", [128, 512], F32) for i in range(2)]
        pRS = [psB(f"pRS{i}", [128, 512], F32) for i in range(2)]

        kb_free = [[], []]
        vtok_free = [[], []]
        pu_free = [[], [], [], []]
        qb_free = [[], [], [], []]
        pS_free = [[], []]
        pC_free = [[], []]
        pRS_free = [[], []]
        recip_free = [[] for _ in range(4)]
        bcast_free = [[] for _ in range(4)]
        cun_free = [[] for _ in range(4)]

        GR = self.G * 128

        def load_kv(g_):
            ig = g_ % 2

            def f_k(e, i, s=self, g_=g_, ig=ig):
                for r in range(2):
                    dst = kb[ig][:, :].rearrange(
                        "p (b n) -> p b n", n=1024)[:, :, r * 512:(r + 1) * 512]
                    i(e.dma_start(
                        dst,
                        s.kg[r * GR + g_ * 128:r * GR + (g_ + 1) * 128,
                             :].rearrange("p (b n) -> p b n", n=512)))
            tk = st(self.c_k[ig], f_k,
                    bar + [self.t_ccK] + kb_free[ig][-1:], incs=2)

            def f_v(e, i, s=self, g_=g_, ig=ig):
                for c in range(s.CH):
                    r, sblk, j = (c % 8) // 4, c // 8, c % 4
                    i(e.dma_start(
                        vtok[ig][:, c * 128:(c + 1) * 128],
                        s.vg[r * GR + g_ * 128:r * GR + (g_ + 1) * 128,
                             sblk * 512 + j * 128:sblk * 512 + (j + 1) * 128],
                        transpose=True))
            tv = st(self.c_v[ig], f_v,
                    bar + [self.t_ccV] + vtok_free[ig][-1:],
                    incs=self.CH)
            return (tk, tv)

        kv_tok = {0: load_kv(0)}
        vt_toks = {0: [kv_tok[0][1]]}
        norm_pend = []

        HPG = self.HPG
        seq = []
        for g_ in range(G):
            for hp in range(HPG // 2):
                for s_ in range(self.n_slots):
                    seq.append((g_, HPG * g_ + 2 * hp, s_))

        q_tok = {}

        def load_q(h, s_, iq):
            return st(self.c_q[iq], lambda e, i, s=self, h=h, s_=s_,
                      iq=iq: i(e.dma_start(
                          qb[iq][:, :],
                          s.q_spill[h * 128:(h + 1) * 128,
                                    s_ * 512:(s_ + 1) * 512])),
                      bar + self.qspill_toks[-2:] + qb_free[iq][-1:])

        for idx in range(min(2, len(seq))):
            g_, h0, s_ = seq[idx]
            q_tok[(idx, 0)] = load_q(h0, s_, (2 * idx) % 4)
            q_tok[(idx, 1)] = load_q(h0 + 1, s_, (2 * idx + 1) % 4)

        pair_i = 0

        for idx, (g_, h0, s_) in enumerate(seq):
            ig = g_ % 2
            E_s = self.EPC * (s_ + 1)
            if h0 == HPG * g_ and s_ == 0 and g_ + 1 < G:
                kv_tok[g_ + 1] = load_kv(g_ + 1)
                vt_toks[g_ + 1] = [kv_tok[g_ + 1][1]]
            tq = [q_tok[(idx, 0)], q_tok[(idx, 1)]]
            iqs = [(2 * idx) % 4, (2 * idx + 1) % 4]
            n_pairs = E_s // 2

            def emit_S(p, j):
                c0 = 2 * p

                def f_s(e, i, s=self, ig=ig, j=j, c0=c0, iq=iqs[j]):
                    ins = None
                    for k in range(2):
                        ins = e.matmul(
                            pS[j][:, k * 512:(k + 1) * 512],
                            kb[ig][:, (c0 + k) * 128:(c0 + k + 1) * 128],
                            qb[iq][:, :], start=True, stop=True)
                    i(ins)
                return st(PE, f_s, [kv_tok[g_][0], tq[j]] + pS_free[j][-1:])

            tS = {}
            tS[(0, 0)] = emit_S(0, 0)
            tS[(0, 1)] = emit_S(0, 1)
            ctx_last = [None, None]
            rs_last = [None, None]
            tM_tok = {}
            pu_of = {}

            def emit_CR(p, j):
                isp = pu_of[(p, j)]
                c0 = 2 * p

                def f_c(e, i, ig=ig, isp=isp, c0=c0, j=j,
                        first=(p == 0), last=(p == n_pairs - 1)):
                    ins = None
                    for k in range(2):
                        ins = e.matmul(
                            pC[j][:, :],
                            vtok[ig][:, (c0 + k) * 128:(c0 + k + 1) * 128],
                            pu[isp][:, k * 512:(k + 1) * 512],
                            start=(first and k == 0),
                            stop=(last and k == 1))
                    i(ins)
                tC = st(PE, f_c,
                        [tM_tok[(p, j)]] + vt_toks[g_] + pC_free[j][-1:])

                def f_r(e, i, s=self, isp=isp, j=j,
                        first=(p == 0), last=(p == n_pairs - 1)):
                    ins = None
                    for k in range(2):
                        ins = e.matmul(
                            pRS[j][0:1, :], s.ones_col[:, :],
                            pu[isp][:, k * 512:(k + 1) * 512],
                            start=(first and k == 0),
                            stop=(last and k == 1))
                    i(ins)
                tR = st(PE, f_r, [tM_tok[(p, j)], self.t_oc] +
                        (pRS_free[j][-1:] if p == 0 else []))
                pu_free[isp].append(tC)
                pu_free[isp].append(tR)
                ctx_last[j], rs_last[j] = tC, tR

            for p in range(n_pairs):
                for j in (0, 1):
                    isp = pair_i % 4
                    pair_i += 1
                    pu_of[(p, j)] = isp
                    c0 = 2 * p
                    tE = st(ACT, lambda e, i, isp=isp, j=j: i(e.activation(
                        pu[isp][:, :], pS[j][:, :], AF.Exp, scale=ISCALE)),
                        [tS[(p, j)]] + pu_free[isp][-2:])
                    pS_free[j].append(tE)
                    tM = tE
                    for k in range(2):
                        c = c0 + k
                        if c >= E_s - 8:
                            mi = c - (E_s - 8)

                            def f_m(e, i, s=self, isp=isp, k=k, mi=mi):
                                i(e.tensor_tensor(
                                    pu[isp][:, k * 512:(k + 1) * 512],
                                    pu[isp][:, k * 512:(k + 1) * 512],
                                    s.masks_sb[:, mi * 512:(mi + 1) * 512],
                                    ALU.mult))
                            tM = st(DVE, f_m, [tM, self.t_masks])
                    tM_tok[(p, j)] = tM
                for j in (0, 1):
                    if p + 1 < n_pairs:
                        tS[(p + 1, j)] = emit_S(p + 1, j)
                    if p >= 1:
                        emit_CR(p - 1, j)
            for j in (0, 1):
                emit_CR(n_pairs - 1, j)
            for j in (0, 1):
                qb_free[iqs[j]].append(ctx_last[j])
            kb_free[ig].append(ctx_last[1])
            vtok_free[ig].append(ctx_last[1])
            if idx + 2 < len(seq):
                g2, h2, s2 = seq[idx + 2]
                q_tok[(idx + 2, 0)] = load_q(h2, s2, (2 * idx + 4) % 4)
                q_tok[(idx + 2, 1)] = load_q(h2 + 1, s2, (2 * idx + 5) % 4)
            for j in (0, 1):
                jv = 2 * j + (idx % 2)
                tEv = st(DVE, lambda e, i, j=j, jv=jv: i(e.tensor_copy(
                    ctx_un[jv][:, :], pC[j][:, :])),
                    [ctx_last[j]] + cun_free[jv][-1:])
                pC_free[j].append(tEv)
                tRec = st(DVE, lambda e, i, j=j, jv=jv: i(e.reciprocal(
                    recip_sb[jv][:, :], pRS[j][0:1, :])),
                    [rs_last[j]] + recip_free[jv][-1:])
                pRS_free[j].append(tRec)
                tBo = st(self.c_bc[jv], lambda e, i, s=self, jv=jv: i(
                    e.dma_start(s.rs_scr[jv], recip_sb[jv][:, :])),
                    [tRec, self.c_bc[jv].tok()])
                recip_free[jv].append(tBo)
                tBi = st(self.c_bc[jv], lambda e, i, s=self, jv=jv: i(
                    e.dma_start(bcast_sb[jv][:, :],
                                bass.AP(s.rs_scr[jv].tensor, 0,
                                        [[0, 128], [1, 512]]))),
                    [tBo] + bcast_free[jv][-1:])

                def mk_norm(s=self, h=h0 + j, s_=s_, jv=jv, tBi=tBi,
                            tEv=tEv):
                    def f_n(e, i):
                        i(e.tensor_tensor(
                            s.ctx_all[:, h * s.T_own + s_ * 512:
                                      h * s.T_own + (s_ + 1) * 512],
                            ctx_un[jv][:, :], bcast_sb[jv][:, :], ALU.mult))
                    tN = s.st(s.DVE, f_n, [tBi, tEv])
                    cun_free[jv].append(tN)
                    bcast_free[jv].append(tN)
                    s.ctx_toks.append(tN)
                norm_pend.append(mk_norm)
            while len(norm_pend) > 2:
                norm_pend.pop(0)()

        for fn in norm_pend:
            fn()
        esB.close()

    def emit_phase_f(self):
        bar = self.barrier_toks()
        st, DVE, PE = self.st, self.DVE, self.PE
        DC, T_own, OB = self.DC, self.T_own, self.OB
        WPC = max(1, DC // 8)
        NPC = DC // WPC
        esF = ExitStack()
        sbF = lambda n, sh, dt: esF.enter_context(
            self.nc.sbuf_tensor(n, sh, dt))
        psF = lambda n, sh, dt: esF.enter_context(
            self.nc.psum_tensor(n, sh, dt))
        wob = [sbF(f"wob{i}", [128, WPC * 512], BF16) for i in range(15)]
        bm_sb = [sbF(f"bm_sb{i}", [128, 512], BF16) for i in range(2)]
        xf = [sbF(f"xf{i}", [128, 512], F32) for i in range(2)]
        ob_sb = [sbF(f"ob_sb{i}", [128, 512], F32) for i in range(2)]
        pF = [psF(f"pF{i}", [128, 512], F32) for i in range(2)]

        wob_free = [[] for _ in range(15)]
        xf_free = [[], []]
        ob_free = [[], []]
        pF_free = [[], []]

        bm_free = [[], []]

        def load_bm(ob):
            ib = ob % 2
            return self.cload(lambda e, i, s=self, ob=ob, ib=ib: i(
                e.dma_start(bm_sb[ib][:, :],
                            s.bm_eff[:, ob * 512:(ob + 1) * 512])),
                bar + bm_free[ib][-1:])
        bm_toks = {0: load_bm(0)}
        if OB > 1:
            bm_toks[1] = load_bm(1)

        def load_wo(piece):
            ob, pc = piece // NPC, piece % NPC
            slot = piece % 15
            rs = ob * 512
            c0 = pc * WPC
            return st(self.c_wo[slot], lambda e, i, s=self, rs=rs, c0=c0,
                      slot=slot: i(e.dma_start(
                          wob[slot][:, :].rearrange(
                              "p (c n) -> p c n", n=512),
                          s.wo[c0 * 128:(c0 + WPC) * 128,
                               rs:rs + 512].rearrange(
                              "(c p) n -> p c n", p=128))),
                      bar + wob_free[slot][-1:])

        npieces = OB * NPC
        wo_toks = {}
        for pv_ in range(min(15, npieces)):
            wo_toks[pv_] = load_wo(pv_)

        xf_tok = {}

        def load_xf(ob, tt, idx):
            ix = idx % 2
            return st(self.c_xf[ix], lambda e, i, s=self, ob=ob, tt=tt,
                      ix=ix: i(e.dma_start(
                          xf[ix][:, :],
                          s.x_own[tt * 128:(tt + 1) * 128,
                                  ob * 512:(ob + 1) * 512])),
                      bar + xf_free[ix][-1:])

        idx = 0
        total = OB * self.NT
        for ob in range(OB):
            pieces = [ob * NPC + pc for pc in range(NPC)]
            slots = [pz % 15 for pz in pieces]
            for tt in range(self.NT):
                if idx < 2:
                    xf_tok[idx] = load_xf(ob, tt, idx)

                ipf = idx % 2

                def f_mm(e, i, s=self, tt=tt, ob=ob, ipf=ipf, slots=slots,
                         WPC=WPC):
                    for c in range(s.DC):
                        sl = slots[c // WPC]
                        cc = c % WPC
                        e.matmul(pF[ipf][:, :],
                                 s.ctx_all[:, c * s.T_own + tt * 128:
                                           c * s.T_own + (tt + 1) * 128],
                                 wob[sl][:, cc * 512:(cc + 1) * 512],
                                 start=(c == 0), stop=False)
                    ins = e.matmul(pF[ipf][:, :],
                                   s.hw[:, tt * 128:(tt + 1) * 128],
                                   bm_sb[ob % 2][:, :],
                                   start=False, stop=True)
                    i(ins)
                tmm = st(PE, f_mm,
                         [wo_toks[pz] for pz in pieces] + [bm_toks[ob]]
                         + self.ctx_toks[-1:] + self.hw_toks[-1:]
                         + pF_free[ipf][-1:])
                bm_free[ob % 2].append(tmm)
                for sl in set(slots):
                    wob_free[sl].append(tmm)
                ta = st(DVE, lambda e, i, ipf=ipf, ix=idx % 2: i(
                    e.tensor_tensor(ob_sb[ipf][:, :], pF[ipf][:, :],
                                    xf[ix][:, :], ALU.add)),
                    [tmm, xf_tok[idx]] + ob_free[ipf][-1:])
                pF_free[ipf].append(ta)
                xf_free[idx % 2].append(ta)
                tst = st(self.c_st[ipf], lambda e, i, s=self, ob=ob, tt=tt,
                         ipf=ipf: i(e.dma_start(
                             s.out[tt * 128:(tt + 1) * 128,
                                   ob * 512:(ob + 1) * 512],
                             ob_sb[ipf][:, :])), [ta])
                ob_free[ipf].append(tst)
                if idx + 2 < total:
                    ob2 = (idx + 2) // self.NT
                    tt2 = (idx + 2) % self.NT
                    xf_tok[idx + 2] = load_xf(ob2, tt2, idx + 2)
                idx += 1
            for pz in range((ob + 1) * NPC, (ob + 2) * NPC):
                if pz < npieces and pz not in wo_toks:
                    wo_toks[pz] = load_wo(pz)
            if ob + 2 < OB and (ob + 2) not in bm_toks:
                bm_toks[ob + 2] = load_bm(ob + 2)

        toks = [c.tok() for c in self.chans]
        st(self.c_st[0], lambda e, i: None,
           [t for t in toks if self.eng_of(t[0]) == "sync"], incs=0)
        st(self.c_const, lambda e, i: None,
           [t for t in toks if self.eng_of(t[0]) == "gpsimd"], incs=0)
        esF.close()
        self.esBF.close()


def build_program(D=4096, H=32, G=8, T_own=2048, T_keys=4096):
    return Builder(D, H, G, T_own, T_keys).build()


def _rope_tables(T, base=500000.0):
    inv = 1.0 / base ** (np.arange(0, 128, 2, dtype=np.float64) / 128.0)
    fr = np.outer(np.arange(T, dtype=np.float64), inv)
    return (np.cos(fr).astype(np.float32).T.copy(),
            np.sin(fr).astype(np.float32).T.copy())


def _masks(parity):
    m = np.zeros((8, 128, 512), np.float32)
    i = np.arange(128)[:, None]
    jq = np.arange(512)[None, :]
    for j in range(8):
        if parity == 0:
            if j < 4:
                m[j] = (128 * j + i <= jq).astype(np.float32)
        else:
            if j < 4:
                m[j] = 1.0
            else:
                m[j] = (128 * (j - 4) + i <= jq).astype(np.float32)
    return m


_NC_CACHE = {}


def _get_program():
    if "real" not in _NC_CACHE:
        _NC_CACHE["real"] = build_program()
    return _NC_CACHE["real"]


def kernel(L0_attn_x, L0_attn_Wq, L0_attn_Wk, L0_attn_Wv, L0_attn_Wo,
           L0_ffn_router_w, L0_ffn_router_b, L0_ffn_A, L0_ffn_Bm):
    x = np.ascontiguousarray(np.asarray(L0_attn_x, dtype=np.float32))
    B, T, D = x.shape
    H, G, E, R = 32, 8, 8, 16

    wq = np.ascontiguousarray(np.asarray(L0_attn_Wq, np.float32))
    wk = np.ascontiguousarray(np.asarray(L0_attn_Wk, np.float32))
    wv = np.ascontiguousarray(np.asarray(L0_attn_Wv, np.float32))
    wo = np.ascontiguousarray(np.asarray(L0_attn_Wo, np.float32))
    rw = np.ascontiguousarray(np.asarray(L0_ffn_router_w, np.float32))
    rb = np.ascontiguousarray(
        np.asarray(L0_ffn_router_b, np.float32)).reshape(1, E)
    A = np.asarray(L0_ffn_A, np.float32)
    Bm = np.asarray(L0_ffn_Bm, np.float32)
    a_t = np.ascontiguousarray(A.reshape(E * R, D).T)
    bm_eff = np.ascontiguousarray(Bm.transpose(0, 2, 1).reshape(E * R, D))

    cosT, sinT = _rope_tables(T)
    ident = np.eye(128, dtype=np.float32)
    expand = np.zeros((E, E * R), np.float32)
    for e in range(E):
        expand[e, e * R:(e + 1) * R] = 1.0

    idx = {}
    for p in (0, 1):
        blocks = np.arange(p * 512, T, 1024)
        idx[p] = np.concatenate([np.arange(b0, b0 + 512) for b0 in blocks])

    nc = _get_program()
    in_maps = []
    for c in range(8):
        b, p = c // 2, c % 2
        own, oth = idx[p], idx[1 - p]
        in_maps.append({
            "x_own": np.ascontiguousarray(x[b][own]),
            "wq": wq, "wk": wk, "wv": wv, "wo": wo,
            "a_t": a_t, "bm_eff": bm_eff,
            "router_w": rw, "router_b": rb,
            "cos_own": np.ascontiguousarray(cosT[:, own]),
            "sin_own": np.ascontiguousarray(sinT[:, own]),
            "masksin": _masks(p),
            "identin": ident,
            "expandin": expand,
        })
    res = run_bass_kernel_spmd(nc, in_maps, core_ids=list(range(8)))
    y = np.empty_like(x)
    for c in range(8):
        b, p = c // 2, c % 2
        y[b][idx[p]] = res.results[c]["out"]
    return y


# version 29 (final)
# speedup vs baseline: 1.0229x; 1.0212x over previous
import numpy as np
import concourse.bass as bass
import concourse.mybir as mybir
from concourse.bass_utils import run_bass_kernel_spmd
from contextlib import ExitStack

F32 = mybir.dt.float32
BF16 = mybir.dt.bfloat16
AF = mybir.ActivationFunctionType
ALU = mybir.AluOpType
AX = mybir.AxisListType

ISCALE = 0.08838834764831845


class Chan:
    __slots__ = ("name", "inc", "count")

    def __init__(self, name, inc=1):
        self.name = name
        self.inc = inc
        self.count = 0

    def tok(self):
        return (self, self.count)


class Plan:

    def __init__(self):
        self.steps = []

    def step(self, engine_attr, chan, fn, waits=(), incs=1):
        for (c, v) in waits:
            assert v <= c.count, f"future token {c.name}: {v} > {c.count}"
        self.steps.append((engine_attr, chan,
                           [w for w in waits if w[1] > 0], fn, incs))
        chan.count += incs * chan.inc
        return chan.tok()

    def replay(self, block, sems, waited):
        per_engine = {}
        for s in self.steps:
            per_engine.setdefault(s[0], []).append(s)

        def mk(ea, steps):
            def body(eng):
                w_ = waited.setdefault(ea, {})
                for (_ea, ch, waits, fn, incs) in steps:
                    for (wc, val) in waits:
                        if w_.get(wc.name, 0) >= val:
                            continue
                        eng.wait_ge(sems[wc.name], val)
                        w_[wc.name] = val

                    def inc(instr, cc=ch):
                        instr.then_inc(sems[cc.name], cc.inc)
                        return instr
                    fn(eng, inc)
            return body
        for ea, steps in per_engine.items():
            getattr(block, ea)(mk(ea, steps))


class Builder:
    def __init__(self, D=4096, H=32, G=8, T_own=2048, T_keys=4096):
        self.D, self.H, self.G = D, H, G
        self.T_own, self.T_keys = T_own, T_keys
        self.DC = D // 128
        self.E, self.R = 8, 16
        self.SCALING = 8.0 / self.R
        self.n_slots = T_own // 512
        self.CH = T_keys // 128
        self.EPC = self.CH // self.n_slots
        assert self.EPC == 8 and T_keys - T_own == T_own
        self.NB = T_own // 512
        self.HPG = H // G
        self.NT = T_own // 128
        self.OB = D // 512
        self.D2 = D // 2
        self.TGRP = min(16, self.DC)

        self.nc = bass.Bass("TRN2", target_bir_lowering=False)
        self.plan = None
        self.chans = []
        self.es = ExitStack()
        self.waited = {}

    def chan(self, name, inc=1):
        c = Chan(name, inc)
        self.chans.append(c)
        return c

    def st(self, ch, fn, waits=(), incs=1):
        return self.plan.step(self.eng_of(ch), ch, fn, waits, incs)

    def cload(self, fn, waits=()):
        t = self.st(self.c_const, fn, list(waits) + self._const_chain)
        self._const_chain = [t]
        return t

    def eng_of(self, ch):
        if ch is self.PE:
            return "tensor"
        if ch is self.DVE:
            return "vector"
        if ch is self.ACT:
            return "scalar"
        if ch is self.POOL or ch is self.c_cc:
            return "gpsimd"
        return "gpsimd" if ch.name in self.gp_ch else "sync"

    def run_phase(self, emit_fn):
        self.plan = Plan()
        emit_fn()
        with self.nc.Block() as block:
            self.plan.replay(block, self.sems, self.waited)
        self.plan = None

    def barrier_toks(self):
        return [c.tok() for c in self.chans]

    def build(self):
        self.declare_io()
        self.declare_channels()
        self.sems = {c.name: self.es.enter_context(
            self.nc.semaphore(c.name)) for c in self.chans}
        self.declare_persistent()
        self.ctx_toks = []
        self._const_chain = []
        self.run_phase(self.emit_phase_a)
        self.run_phase(self.emit_phase_b)
        self.run_phase(self.emit_phase_f)
        self.es.close()
        return self.nc

    def declare_io(self):
        nc, D = self.nc, self.D
        T_own, G, H, E, R = self.T_own, self.G, self.H, self.E, self.R

        def din(name, shape):
            return nc.dram_tensor(name, list(shape), F32,
                                  kind="ExternalInput").ap()
        self.x_own = din("x_own", (T_own, D))
        self.wq = din("wq", (D, H * 128))
        self.wk = din("wk", (D, G * 128))
        self.wv = din("wv", (D, G * 128))
        self.wo = din("wo", (H * 128, D))
        self.a_t = din("a_t", (D, E * R))
        self.bm_eff = din("bm_eff", (E * R, D))
        self.router_w = din("router_w", (E, D))
        self.router_b = din("router_b", (1, E))
        self.cos_own = din("cos_own", (64, T_own))
        self.sin_own = din("sin_own", (64, T_own))
        self.masksin = din("masksin", (8, 128, 512))
        self.identin = din("identin", (128, 128))
        self.expandin = din("expandin", (E, E * R))
        self.out = nc.dram_tensor("out", [T_own, D], F32,
                                  kind="ExternalOutput").ap()
        self.rs_scr = [nc.dram_tensor(f"rs_scr{j}", [1, 512], F32).ap()
                       for j in range(4)]
        self.q_spill = nc.dram_tensor("q_spill", [H * 128, T_own], BF16).ap()
        self.k_spill = nc.dram_tensor("k_spill", [G * 128, T_own], BF16).ap()
        self.v_spill = nc.dram_tensor("v_spill", [G * 128, T_own], BF16).ap()
        self.kg = nc.dram_tensor("kg", [2 * G * 128, T_own], BF16).ap()
        self.vg = nc.dram_tensor("vg", [2 * G * 128, T_own], BF16).ap()

    def declare_channels(self):
        C = self.chan
        self.PE, self.DVE, self.ACT, self.POOL = (C("pe"), C("dve"),
                                                  C("act"), C("pool"))
        self.c_x = [C(f"c_x{i}", 16) for i in range(4)]
        self.c_rope = [C(f"c_rope{i}", 16) for i in range(2)]
        self.c_spq = [C(f"c_spq{i}", 16) for i in range(2)]
        self.c_spk = [C(f"c_spk{i}", 16) for i in range(2)]
        self.c_spv = [C(f"c_spv{i}", 16) for i in range(2)]
        self.c_q = [C(f"c_q{i}", 16) for i in range(4)]
        self.c_k = [C(f"c_k{i}", 16) for i in range(2)]
        self.c_v = [C(f"c_v{i}", 16) for i in range(2)]
        self.c_st = [C(f"c_st{i}", 16) for i in range(2)]
        self.c_xf = [C(f"c_xf{i}", 16) for i in range(2)]
        self.c_w = [C(f"c_w{i}", 16) for i in range(2)]
        self.c_wo = [C(f"c_wo{i}", 16) for i in range(15)]
        self.c_const = C("c_const", 16)
        self.c_cc = C("c_cc", 1)
        self.c_bc = [C(f"c_bc{i}", 16) for i in range(4)]
        self.gp_ch = set(c.name for c in self.c_w + self.c_wo + self.c_rope
                         + self.c_x + [self.c_const])
        self.sync_extra = set(c.name for c in self.c_bc)

    def declare_persistent(self):
        E, R, DC, H, T_own = self.E, self.R, self.DC, self.H, self.T_own
        sb = lambda n, sh, dt: self.es.enter_context(
            self.nc.sbuf_tensor(n, sh, dt))
        self.ones_col = sb("ones_col", [128, 1], BF16)
        self.ones_row = sb("ones_row", [1, 128], BF16)
        self.ones_row32 = sb("ones_row32", [1, 128], F32)
        self.ident_bf = sb("ident_bf", [128, 128], BF16)
        self.expand_bf = sb("expand_bf", [E, E * R], BF16)
        self.rb_sb = sb("rb_sb", [1, E], BF16)
        self.rwT = sb("rwT", [128, DC * E], BF16)
        self.hw = sb("hw", [128, T_own], BF16)


    def emit_phase_a(self):
        st, DVE, ACT, PE, POOL = (self.st, self.DVE, self.ACT, self.PE,
                                  self.POOL)
        D, D2, DC, T_own = self.D, self.D2, self.DC, self.T_own
        cc = self.c_const

        esA = ExitStack()
        sbA = lambda n, sh, dt: esA.enter_context(
            self.nc.sbuf_tensor(n, sh, dt))
        psA = lambda n, sh, dt: esA.enter_context(
            self.nc.psum_tensor(n, sh, dt))

        self.t_ident = self.cload(lambda e, i, s=self: i(e.dma_start(
            s.ident_bf[:, :], s.identin)))
        self.t_expand = self.cload(lambda e, i, s=self: i(e.dma_start(
            s.expand_bf[:, :], s.expandin)))
        self.t_rb = self.cload(lambda e, i, s=self: i(e.dma_start(
            s.rb_sb[:, :], s.router_b)))
        self.t_oc = st(POOL, lambda e, i, s=self: i(e.memset(
            s.ones_col[:, :], 1.0)))
        self.t_or = st(POOL, lambda e, i, s=self: i(e.memset(
            s.ones_row[:, :], 1.0)))
        self.t_or32 = st(POOL, lambda e, i, s=self: i(e.memset(
            s.ones_row32[:, :], 1.0)))

        rw_tmp = sbA("rw_tmp", [self.E, 128], BF16)
        pMiscF = psA("pMiscF", [128, 8 + 128], F32)
        pMiscB = psA("pMiscB", [128, 136], BF16)
        self.pl = pMiscF[:, 0:self.E]
        self.pwx = pMiscF[:, 8:136]
        self.pw = pMiscB[0:self.E, 0:128]
        self._rw_tmp = rw_tmp
        self._p_rw = pMiscB[:, 128:136]

        self.xnT = sbA("xnT", [128, DC * T_own], BF16)
        self.hT = sbA("hT", [128, self.T_own], BF16)
        self.xnT3 = self.xnT[:, :].rearrange("p (c t) -> p c t", c=DC)
        self.xbuf = [sbA(f"xbuf{i}", [128, D2], BF16) for i in range(4)]
        self.xn_tok = sbA("xn_tok", [128, D], BF16)
        self.wbuf = [sbA(f"wbuf{i}", [128, DC * 128], BF16) for i in range(2)]
        _stgp = [sbA(f"stg{i}", [128, 512], BF16) for i in range(2)]
        self.stg = {k: _stgp for k in "qkv"}
        self.cos_sl = sbA("cos_sl", [64, T_own], BF16)
        self.sin_sl = sbA("sin_sl", [64, T_own], BF16)
        self.rtmp = [sbA(f"rtmp{i}", [64, 512], F32) for i in range(2)]
        self.stats2 = []
        for k in range(2):
            d = {nm: sbA(f"st{k}_" + nm, [128, 1], F32)
                 for nm in ["dd", "s0", "r0", "s1", "rstd", "nb"]}
            d["bn"] = sbA(f"st{k}_bn", [128, 2 * max(1, self.D2 // 512) * 6], F32)
            d["mv"] = sbA(f"st{k}_mv", [128, 2], F32)
            self.stats2.append(d)
        self.gt = {nm: sbA("gt_" + nm, [128, self.E], F32)
                   for nm in ["t", "i1", "tm", "i2", "ind", "et", "w"]}
        self.gs = {nm: sbA("gs_" + nm, [128, 1], F32)
                   for nm in ["m1", "m2", "em2", "den", "rden"]}
        self.w_bf = sbA("w_bf", [128, self.E], BF16)
        self.wT_sb = sbA("wT_sb", [self.E, 128], BF16)

        self.pA = [psA(f"pA{i}", [128, 512], F32) for i in range(2)]
        self.pT = [psA(f"pT{i}", [128, self.TGRP * 128], BF16)
                   for i in range(2)]

        self.pA_free = [[] for _ in range(2)]
        self.pT_free = [[] for _ in range(2)]
        self.xbuf_free = [[] for _ in range(4)]
        _stgf = [[], []]
        self.stg_free = {k: _stgf for k in "qkv"}
        self.wbuf_free = [[] for _ in range(2)]
        self.rtmp_free = [[], []]
        self.pl_free = []
        self.pw_free = []
        self.pwx_free = []
        self.wT_free = []

        self.pA_i = 0
        self.w_i = 0
        _stgi = [0]
        self.stg_i = {k: _stgi for k in "qkv"}
        self.xt_i = 0
        self.kspill_toks = [[] for _ in range(self.G)]
        self.vspill_toks = [[] for _ in range(self.G)]
        self.qspill_toks = []
        self.hT_toks = []
        self.hw_toks = []
        self.xnT_tb_tok = {}
        self.rope_users = []
        self.prev_tile_norm = []
        self.prev_gate = []
        self.last_wmm = []
        self.pend_evac = []
        self.tb_evac_tok = []
        self.last_ttr = []
        self.pend_post = []

        self.emit_slab("own")

        esA.close()

    def emit_slab(self, slab):
        st = self.st
        rl1 = st(self.c_rope[0], lambda e, i, s=self: i(
            e.dma_start(s.cos_sl[:, :], s.cos_own)))
        rl2 = st(self.c_rope[1], lambda e, i, s=self: i(
            e.dma_start(s.sin_sl[:, :], s.sin_own)))
        self.rope_now = [rl1, rl2]

        lw0, iw0 = self.load_wslab(self.wk, 0)
        lv0, iv0 = self.load_wslab(self.wv, 0)
        for tb in range(self.NB):
            for t in range(tb * 4, tb * 4 + 4):
                self.emit_stage1_tile(slab, t)
            for fn in self.pend_post:
                fn()
            self.pend_post = []
            self.flush_evacs()
            self.xnT_tb_tok[tb] = self.tb_evac_tok[-1]
            self.emit_wcell(slab, "k", 0, tb, lw0, iw0)
            self.emit_wcell(slab, "v", 0, tb, lv0, iv0)

        self.emit_wpass(slab, self.wk, self.G, "k", skip_first=True)
        self.emit_wpass(slab, self.wv, self.G, "v", skip_first=True)
        groups = [[0, 1], [2, 3], [4, 5], [6, 7]]
        kw = [t for toks in self.kspill_toks for t in toks[-2:]]
        self.t_ccK = st(self.c_cc, lambda e, i, s=self, g=groups: i(
            e.collective_compute("AllGather", ALU.bypass,
                                 replica_groups=g,
                                 ins=[s.k_spill.opt()],
                                 outs=[s.kg.opt()])), kw)
        vw = [t for toks in self.vspill_toks for t in toks[-2:]]
        self.t_ccV = st(self.c_cc, lambda e, i, s=self, g=groups: i(
            e.collective_compute("AllGather", ALU.bypass,
                                 replica_groups=g,
                                 ins=[s.v_spill.opt()],
                                 outs=[s.vg.opt()])), vw)
        self.emit_wpass(slab, self.wq, self.H, "q")
        self.emit_wpass(slab, self.a_t, 1, "h")
        self.emit_rwt_init()
        self.emit_router()

    def emit_stage1_tile(self, slab, t):
        st, DVE, ACT = self.st, self.DVE, self.ACT
        x_src = self.x_own if slab == "own" else self.x_oth
        DC = self.DC
        ia = self.xt_i % 4
        ib = (self.xt_i + 1) % 4
        self.xt_i += 2
        S = self.stats2[(self.xt_i // 2) % 2]
        for fn in self.pend_post:
            fn()
        self.pend_post = []
        self.flush_evacs()
        la = st(self.c_x[ia], lambda e, i, s=self, t=t, ia=ia, x=x_src: i(
            e.dma_start(s.xbuf[ia][:, :], x[t * 128:(t + 1) * 128, 0:s.D2])),
            self.xbuf_free[ia][-1:] + self.prev_tile_norm[-2:-1])
        lb = st(self.c_x[ib], lambda e, i, s=self, t=t, ib=ib, x=x_src: i(
            e.dma_start(s.xbuf[ib][:, :], x[t * 128:(t + 1) * 128,
                                            s.D2:s.D])),
            self.xbuf_free[ib][-1:] + self.prev_tile_norm[-2:-1])
        pv = self.prev_tile_norm[-3:-1]
        ck = min(512, self.D2)
        nst = self.D2 // ck

        def f_bn(e, i, s=self, ia=ia, ib=ib, nst=nst, ck=ck):
            ins = None
            for half, ix in ((0, ia), (1, ib)):
                for c in range(nst):
                    ins = e.bn_stats(
                        S["bn"][:, (half * nst + c) * 6:
                                (half * nst + c + 1) * 6],
                        s.xbuf[ix][:, c * ck:(c + 1) * ck])
            i(ins)
        s2 = st(DVE, f_bn, [la, lb] + pv)
        s4 = st(DVE, lambda e, i: i(e.bn_aggr(
            S["mv"][:, :], S["bn"][:, :])), [s2])
        s9 = st(DVE, lambda e, i: i(e.tensor_scalar(
            S["dd"][:, :], S["mv"][:, 1:2], 1e-5, None, ALU.add)), [s4])
        a1 = st(ACT, lambda e, i: i(e.activation(
            S["s0"][:, :], S["dd"][:, :], AF.Sqrt)), [s9])

        def mk_post(s=self, S=S, a1=a1, ia=ia, ib=ib, t=t):
            st = s.st
            s10 = st(s.DVE, lambda e, i: i(e.reciprocal(
                S["r0"][:, :], S["s0"][:, :])), [a1])
            s11 = st(s.DVE, lambda e, i: i(e.tensor_tensor(
                S["s1"][:, :], S["dd"][:, :], S["r0"][:, :], ALU.mult)),
                [s10])
            s12 = st(s.DVE, lambda e, i: i(e.tensor_tensor(
                S["s1"][:, :], S["s1"][:, :], S["s0"][:, :], ALU.add)),
                [s11])
            s13 = st(s.DVE, lambda e, i: i(e.tensor_scalar(
                S["s1"][:, :], S["s1"][:, :], 0.5, None, ALU.mult)), [s12])
            s14 = st(s.DVE, lambda e, i: i(e.reciprocal(
                S["rstd"][:, :], S["s1"][:, :])), [s13])
            s16 = st(s.DVE, lambda e, i: i(e.scalar_tensor_tensor(
                S["nb"][:, :], S["mv"][:, 0:1], -1.0, S["rstd"][:, :],
                ALU.mult, ALU.mult)), [s14])
            n1 = st(s.ACT, lambda e, i: i(e.activation(
                s.xn_tok[:, 0:s.D2], s.xbuf[ia][:, :], AF.Identity,
                bias=S["nb"][:, :], scale=S["rstd"][:, :])),
                [s16, a1] + s.last_ttr[-1:])
            n2 = st(s.ACT, lambda e, i: i(e.activation(
                s.xn_tok[:, s.D2:s.D], s.xbuf[ib][:, :], AF.Identity,
                bias=S["nb"][:, :], scale=S["rstd"][:, :])), [s16, n1])
            s.prev_tile_norm.append(n2)
            s.xbuf_free[ia].append(n1)
            s.xbuf_free[ib].append(n2)
            nr = (s.DC + s.TGRP - 1) // s.TGRP
            for r in range(nr):
                ir = r % 2
                c0 = r * s.TGRP
                ncs = min(s.TGRP, s.DC - c0)

                def f_tr(e, i, s=s, c0=c0, ncs=ncs, ir=ir):
                    ins = None
                    for k in range(ncs):
                        ins = e.transpose(
                            s.pT[ir][:, k * 128:(k + 1) * 128],
                            s.xn_tok[:, (c0 + k) * 128:(c0 + k + 1) * 128],
                            s.ident_bf[:, :])
                    i(ins)
                ttr = st(s.PE, f_tr,
                         [n1, n2, s.t_ident] + s.pT_free[ir][-1:])
                s.last_ttr.append(ttr)

                def mk_ev(s=s, c0=c0, ncs=ncs, ir=ir, t=t, ttr=ttr):
                    def f_ev(e, i):
                        dst = s.xnT3[:, c0:c0 + ncs, t * 128:(t + 1) * 128]
                        src = s.pT[ir][:, 0:ncs * 128].rearrange(
                            "p (c k) -> p c k", c=ncs)
                        i(e.tensor_copy(dst, src))

                    def f_ev_act(e, i):
                        dst = s.xnT3[:, c0:c0 + ncs, t * 128:(t + 1) * 128]
                        src = s.pT[ir][:, 0:ncs * 128].rearrange(
                            "p (c k) -> p c k", c=ncs)
                        i(e.activation(dst, src, AF.Identity))
                    if ir == 0:
                        tev = s.st(s.DVE, f_ev, [ttr] + s.last_wmm)
                    else:
                        tev = s.st(s.ACT, f_ev_act, [ttr] + s.last_wmm)
                    s.pT_free[ir].append(tev)
                    s.tb_evac_tok.append(tev)
                s.pend_evac.append(mk_ev)
        self.pend_post.append(mk_post)
        return None

    def load_wslab(self, whbm, j):
        iw = self.w_i % 2
        self.w_i += 1
        lw = self.st(self.c_w[iw], lambda e, i, s=self, j=j, iw=iw,
                     w=whbm: i(
            e.dma_start(s.wbuf[iw][:, :].rearrange(
                            "p (c n) -> p c n", n=128),
                        w[:, j * 128:(j + 1) * 128].rearrange(
                            "(c p) n -> p c n", p=128))),
            self.wbuf_free[iw][-1:])
        return lw, iw

    def emit_wcell(self, slab, kind, j, tb, lw, iw):
        ip = self.pA_i % 2
        self.pA_i += 1

        def f_mm(e, i, s=self, tb=tb, iw=iw, ip=ip):
            ins = None
            for c in range(s.DC):
                ins = e.matmul(
                    s.pA[ip][:, :],
                    s.wbuf[iw][:, c * 128:(c + 1) * 128],
                    s.xnT[:, c * s.T_own + tb * 512:
                          c * s.T_own + (tb + 1) * 512],
                    start=(c == 0), stop=(c == s.DC - 1))
            i(ins)
        tmm = self.st(self.PE, f_mm,
                      [lw, self.xnT_tb_tok[tb]] + self.pA_free[ip][-1:])
        self.last_wmm = [tmm]
        self.wbuf_free[iw].append(tmm)
        self.emit_epilogue(slab, kind, j, tb, ip, tmm)

    def flush_evacs(self):
        for fn in self.pend_evac:
            fn()
        self.pend_evac = []

    def emit_wpass(self, slab, whbm, dout_chunks, kind, skip_first=False):
        st = self.st
        for j in range(1 if skip_first else 0, dout_chunks):
            lw, iw = self.load_wslab(whbm, j)
            for tb in range(self.NB):
                self.emit_wcell(slab, kind, j, tb, lw, iw)

    def emit_epilogue(self, slab, kind, j, tb, ip, tmm):
        st, DVE, ACT = self.st, self.DVE, self.ACT
        perm_off = 512 * tb
        if kind in ("k", "q"):
            ir = self.stg_i[kind][0] % 2
            self.stg_i[kind][0] += 1
            stg = self.stg[kind][ir]
            cA = self.cos_sl[:, tb * 512:(tb + 1) * 512]
            sA = self.sin_sl[:, tb * 512:(tb + 1) * 512]
            m1 = st(DVE, lambda e, i, s=self, ip=ip, cA=cA: i(e.tensor_tensor(
                s.rtmp[0][:, :], s.pA[ip][0:64, :], cA, ALU.mult)),
                [tmm] + self.rope_now + self.rtmp_free[0][-1:])
            m2 = st(DVE, lambda e, i, s=self, ip=ip, sA=sA: i(e.tensor_tensor(
                s.rtmp[1][:, :], s.pA[ip][64:128, :], sA, ALU.mult)),
                [m1] + self.rtmp_free[1][-1:])
            o1 = st(DVE, lambda e, i, s=self, stg=stg: i(e.tensor_tensor(
                stg[0:64, :], s.rtmp[0][:, :], s.rtmp[1][:, :],
                ALU.subtract)), [m2] + self.stg_free[kind][ir][-1:])
            m3 = st(DVE, lambda e, i, s=self, ip=ip, sA=sA: i(e.tensor_tensor(
                s.rtmp[0][:, :], s.pA[ip][0:64, :], sA, ALU.mult)), [o1])
            m4 = st(DVE, lambda e, i, s=self, ip=ip, cA=cA: i(e.tensor_tensor(
                s.rtmp[1][:, :], s.pA[ip][64:128, :], cA, ALU.mult)), [m3])
            o2 = st(DVE, lambda e, i, s=self, stg=stg: i(e.tensor_tensor(
                stg[64:128, :], s.rtmp[0][:, :], s.rtmp[1][:, :], ALU.add)),
                [m4])
            self.rtmp_free[0].append(o2)
            self.rtmp_free[1].append(o2)
            self.pA_free[ip].append(o2)
            self.rope_users.append(o2)
            if kind == "k":
                sp = st(self.c_spk[ir], lambda e, i, s=self, j=j, stg=stg,
                        po=perm_off: i(e.dma_start(
                            s.k_spill[j * 128:(j + 1) * 128, po:po + 512],
                            stg[:, :])), [o2])
                self.kspill_toks[j].append(sp)
            else:
                sp = st(self.c_spq[ir], lambda e, i, s=self, j=j, stg=stg,
                        tb=tb: i(e.dma_start(
                            s.q_spill[j * 128:(j + 1) * 128,
                                      tb * 512:(tb + 1) * 512], stg[:, :])),
                        [o2])
                self.qspill_toks.append(sp)
            self.stg_free[kind][ir].append(sp)
        elif kind == "v":
            ir = self.stg_i["v"][0] % 2
            self.stg_i["v"][0] += 1
            ev = st(ACT, lambda e, i, s=self, ip=ip, ir=ir: i(e.activation(
                s.stg["v"][ir][:, :], s.pA[ip][:, :], AF.Identity)),
                [tmm] + self.stg_free["v"][ir][-1:])
            self.pA_free[ip].append(ev)
            sp = st(self.c_spv[ir], lambda e, i, s=self, j=j, ir=ir,
                    po=perm_off: i(e.dma_start(
                        s.v_spill[j * 128:(j + 1) * 128, po:po + 512],
                        s.stg["v"][ir][:, :])), [ev])
            self.vspill_toks[j].append(sp)
            self.stg_free["v"][ir].append(sp)
        elif kind == "h":
            ev = st(ACT, lambda e, i, s=self, ip=ip, tb=tb: i(e.activation(
                s.hT[:, tb * 512:(tb + 1) * 512], s.pA[ip][:, :],
                AF.Identity)), [tmm])
            self.pA_free[ip].append(ev)
            self.hT_toks.append(ev)

    def emit_rwt_init(self):
        st, PE, DVE = self.st, self.PE, self.DVE
        rw_tmp, p_rw = self._rw_tmp, self._p_rw
        prev = []
        for c in range(self.DC):
            tl = self.cload(lambda e, i, s=self, c=c, rw=rw_tmp: i(
                e.dma_start(rw[:, :],
                            s.router_w[:, c * 128:(c + 1) * 128])), prev)
            tt = st(PE, lambda e, i, s=self, rw=rw_tmp, p=p_rw: i(
                e.transpose(p, rw[:, :], s.ident_bf[0:s.E, 0:s.E])),
                [tl, self.t_ident] + prev)
            tv = st(DVE, lambda e, i, s=self, c=c, p=p_rw: i(e.tensor_copy(
                s.rwT[:, c * s.E:(c + 1) * s.E], p)), [tt])
            prev = [tv]
        self.t_rwT = prev[0]
        self._pw_init = prev[0]

    def emit_router(self):
        st, DVE, ACT, PE = self.st, self.DVE, self.ACT, self.PE
        g, gs = self.gt, self.gs
        for tt in range(self.NT):
            def f_rmm(e, i, s=self, tt=tt):
                for c in range(s.DC):
                    e.matmul(s.pl,
                             s.xnT[:, c * s.T_own + tt * 128:
                                   c * s.T_own + (tt + 1) * 128],
                             s.rwT[:, c * s.E:(c + 1) * s.E],
                             start=(c == 0), stop=False)
                ins = e.matmul(s.pl, s.ones_row[:, :], s.rb_sb[:, :],
                               start=False, stop=True)
                i(ins)
            pg = self.prev_gate[-2:]
            tlg = st(PE, f_rmm,
                     [self.t_rwT, self.t_rb, self.t_or,
                      self.xnT_tb_tok[tt // 4]] + self.pl_free[-1:] + pg)
            g0 = st(DVE, lambda e, i, s=self: i(e.reduce_max(
                gs["m1"][:, :], s.pl, axis=AX.X)), [tlg] + pg)
            g1 = st(DVE, lambda e, i, s=self: i(e.tensor_scalar(
                g["t"][:, :], s.pl, gs["m1"][:, :], None,
                ALU.subtract)), [g0])
            self.pl_free.append(g1)
            g2 = st(DVE, lambda e, i: i(e.tensor_scalar(
                g["i1"][:, :], g["t"][:, :], 0.0, None, ALU.is_ge)), [g1])
            g3 = st(DVE, lambda e, i: i(e.tensor_scalar(
                g["tm"][:, :], g["i1"][:, :], -1e9, None, ALU.mult)), [g2])
            g4 = st(DVE, lambda e, i: i(e.tensor_tensor(
                g["tm"][:, :], g["t"][:, :], g["tm"][:, :], ALU.add)), [g3])
            g5 = st(DVE, lambda e, i: i(e.reduce_max(
                gs["m2"][:, :], g["tm"][:, :], axis=AX.X)), [g4])
            g6 = st(DVE, lambda e, i: i(e.tensor_scalar(
                g["i2"][:, :], g["tm"][:, :], gs["m2"][:, :], None,
                ALU.is_ge)), [g5])
            g7 = st(DVE, lambda e, i: i(e.tensor_tensor(
                g["ind"][:, :], g["i1"][:, :], g["i2"][:, :], ALU.add)),
                [g6])
            a1 = st(ACT, lambda e, i: i(e.activation(
                g["et"][:, :], g["t"][:, :], AF.Exp)), [g1] + pg)
            a2 = st(ACT, lambda e, i: i(e.activation(
                gs["em2"][:, :], gs["m2"][:, :], AF.Exp)), [g5, a1])
            g8 = st(DVE, lambda e, i: i(e.tensor_scalar(
                gs["den"][:, :], gs["em2"][:, :], 1.0, None, ALU.add)),
                [a2, g7])
            g9 = st(DVE, lambda e, i: i(e.reciprocal(
                gs["rden"][:, :], gs["den"][:, :])), [g8])
            g10 = st(DVE, lambda e, i: i(e.tensor_tensor(
                g["w"][:, :], g["ind"][:, :], g["et"][:, :], ALU.mult)),
                [g9, a1])
            g11 = st(DVE, lambda e, i: i(e.tensor_scalar(
                g["w"][:, :], g["w"][:, :], gs["rden"][:, :], None,
                ALU.mult)), [g10])
            g12 = st(DVE, lambda e, i, s=self: i(e.tensor_scalar(
                s.w_bf[:, :], g["w"][:, :], s.SCALING, None, ALU.mult)),
                [g11] + self.pw_free[-1:])
            tw = st(PE, lambda e, i, s=self: i(e.transpose(
                s.pw, s.w_bf[:, :], s.ident_bf[:, :])),
                [g12, self.t_ident, self._pw_init] + self.pw_free[-1:])
            ev = st(DVE, lambda e, i, s=self: i(e.tensor_copy(
                s.wT_sb[:, :], s.pw)), [tw] + self.wT_free[-1:])
            self.pw_free.append(ev)
            tx = st(PE, lambda e, i, s=self: i(e.matmul(
                s.pwx, s.expand_bf[:, :], s.wT_sb[:, :],
                start=True, stop=True)),
                [ev, self.t_expand] + self.pwx_free[-1:])
            self.wT_free.append(tx)
            hwt = st(DVE, lambda e, i, s=self, tt=tt: i(e.tensor_tensor(
                s.hw[:, tt * 128:(tt + 1) * 128],
                s.hT[:, tt * 128:(tt + 1) * 128], s.pwx, ALU.mult)),
                [tx] + self.hT_toks[-1:])
            self.pwx_free.append(hwt)
            self.hw_toks.append(hwt)
            self.prev_gate.append(hwt)

    def emit_phase_b(self):
        bar = self.barrier_toks()
        st, DVE, ACT, PE = self.st, self.DVE, self.ACT, self.PE
        CH, G, T_keys, T_own = self.CH, self.G, self.T_keys, self.T_own

        self.esBF = ExitStack()
        sbBF = lambda n, sh, dt: self.esBF.enter_context(
            self.nc.sbuf_tensor(n, sh, dt))
        self.ctx_all = sbBF("ctx_all", [128, self.H * T_own], BF16)

        esB = ExitStack()
        sbB = lambda n, sh, dt: esB.enter_context(
            self.nc.sbuf_tensor(n, sh, dt))
        psB = lambda n, sh, dt: esB.enter_context(
            self.nc.psum_tensor(n, sh, dt))
        self.masks_sb = sbB("masks_sb", [128, 8 * 512], BF16)
        self.t_masks = self.cload(lambda e, i, s=self: i(e.dma_start(
            s.masks_sb[:, :].rearrange("p (m q) -> p m q", m=8),
            s.masksin.rearrange("m p q -> p m q"))), bar)

        kb = [sbB(f"kb{i}", [128, T_keys], BF16) for i in range(2)]
        vtok = [sbB(f"vtok{i}", [128, T_keys], BF16) for i in range(2)]
        pu = [sbB(f"pu{i}", [128, 1024], BF16) for i in range(4)]
        qb = [sbB(f"qb{i}", [128, 512], BF16) for i in range(4)]
        recip_sb = [sbB(f"recip_sb{i}", [1, 512], F32) for i in range(4)]
        bcast_sb = [sbB(f"bcast_sb{i}", [128, 512], F32) for i in range(4)]
        ctx_un = [sbB(f"ctx_un{i}", [128, 512], BF16) for i in range(4)]

        pS = [psB(f"pS{i}", [128, 1024], F32) for i in range(2)]
        pC = [psB(f"pC{i}", [128, 512], F32) for i in range(2)]
        pRS = [psB(f"pRS{i}", [128, 512], F32) for i in range(2)]

        kb_free = [[], []]
        vtok_free = [[], []]
        pu_free = [[], [], [], []]
        qb_free = [[], [], [], []]
        pS_free = [[], []]
        pC_free = [[], []]
        pRS_free = [[], []]
        recip_free = [[] for _ in range(4)]
        bcast_free = [[] for _ in range(4)]
        cun_free = [[] for _ in range(4)]

        GR = self.G * 128

        def load_kv(g_):
            ig = g_ % 2

            def f_k(e, i, s=self, g_=g_, ig=ig):
                for r in range(2):
                    dst = kb[ig][:, :].rearrange(
                        "p (b n) -> p b n", n=1024)[:, :, r * 512:(r + 1) * 512]
                    i(e.dma_start(
                        dst,
                        s.kg[r * GR + g_ * 128:r * GR + (g_ + 1) * 128,
                             :].rearrange("p (b n) -> p b n", n=512)))
            tk = st(self.c_k[ig], f_k,
                    bar + [self.t_ccK] + kb_free[ig][-1:], incs=2)

            def f_v(e, i, s=self, g_=g_, ig=ig):
                for c in range(s.CH):
                    r, sblk, j = (c % 8) // 4, c // 8, c % 4
                    i(e.dma_start(
                        vtok[ig][:, c * 128:(c + 1) * 128],
                        s.vg[r * GR + g_ * 128:r * GR + (g_ + 1) * 128,
                             sblk * 512 + j * 128:sblk * 512 + (j + 1) * 128],
                        transpose=True))
            tv = st(self.c_v[ig], f_v,
                    bar + [self.t_ccV] + vtok_free[ig][-1:],
                    incs=self.CH)
            return (tk, tv)

        kv_tok = {0: load_kv(0)}
        vt_toks = {0: [kv_tok[0][1]]}
        norm_pend = []

        HPG = self.HPG
        seq = []
        for g_ in range(G):
            for hp in range(HPG // 2):
                for s_ in range(self.n_slots):
                    seq.append((g_, HPG * g_ + 2 * hp, s_))

        q_tok = {}

        def load_q(h, s_, iq):
            return st(self.c_q[iq], lambda e, i, s=self, h=h, s_=s_,
                      iq=iq: i(e.dma_start(
                          qb[iq][:, :],
                          s.q_spill[h * 128:(h + 1) * 128,
                                    s_ * 512:(s_ + 1) * 512])),
                      bar + self.qspill_toks[-2:] + qb_free[iq][-1:])

        for idx in range(min(2, len(seq))):
            g_, h0, s_ = seq[idx]
            q_tok[(idx, 0)] = load_q(h0, s_, (2 * idx) % 4)
            q_tok[(idx, 1)] = load_q(h0 + 1, s_, (2 * idx + 1) % 4)

        pair_i = 0

        for idx, (g_, h0, s_) in enumerate(seq):
            ig = g_ % 2
            E_s = self.EPC * (s_ + 1)
            if h0 == HPG * g_ and s_ == 0 and g_ + 1 < G:
                kv_tok[g_ + 1] = load_kv(g_ + 1)
                vt_toks[g_ + 1] = [kv_tok[g_ + 1][1]]
            tq = [q_tok[(idx, 0)], q_tok[(idx, 1)]]
            iqs = [(2 * idx) % 4, (2 * idx + 1) % 4]
            n_pairs = E_s // 2

            def emit_S(p, j):
                c0 = 2 * p

                def f_s(e, i, s=self, ig=ig, j=j, c0=c0, iq=iqs[j]):
                    ins = None
                    for k in range(2):
                        ins = e.matmul(
                            pS[j][:, k * 512:(k + 1) * 512],
                            kb[ig][:, (c0 + k) * 128:(c0 + k + 1) * 128],
                            qb[iq][:, :], start=True, stop=True)
                    i(ins)
                return st(PE, f_s, [kv_tok[g_][0], tq[j]] + pS_free[j][-1:])

            tS = {}
            tS[(0, 0)] = emit_S(0, 0)
            tS[(0, 1)] = emit_S(0, 1)
            ctx_last = [None, None]
            rs_last = [None, None]
            tM_tok = {}
            pu_of = {}

            def emit_CR(p, j):
                isp = pu_of[(p, j)]
                c0 = 2 * p

                def f_c(e, i, ig=ig, isp=isp, c0=c0, j=j,
                        first=(p == 0), last=(p == n_pairs - 1)):
                    ins = None
                    for k in range(2):
                        ins = e.matmul(
                            pC[j][:, :],
                            vtok[ig][:, (c0 + k) * 128:(c0 + k + 1) * 128],
                            pu[isp][:, k * 512:(k + 1) * 512],
                            start=(first and k == 0),
                            stop=(last and k == 1))
                    i(ins)
                tC = st(PE, f_c,
                        [tM_tok[(p, j)]] + vt_toks[g_] + pC_free[j][-1:])

                def f_r(e, i, s=self, isp=isp, j=j,
                        first=(p == 0), last=(p == n_pairs - 1)):
                    ins = None
                    for k in range(2):
                        ins = e.matmul(
                            pRS[j][0:1, :], s.ones_col[:, :],
                            pu[isp][:, k * 512:(k + 1) * 512],
                            start=(first and k == 0),
                            stop=(last and k == 1))
                    i(ins)
                tR = st(PE, f_r, [tM_tok[(p, j)], self.t_oc] +
                        (pRS_free[j][-1:] if p == 0 else []))
                pu_free[isp].append(tC)
                pu_free[isp].append(tR)
                ctx_last[j], rs_last[j] = tC, tR

            for p in range(n_pairs):
                for j in (0, 1):
                    isp = pair_i % 4
                    pair_i += 1
                    pu_of[(p, j)] = isp
                    c0 = 2 * p
                    tE = st(ACT, lambda e, i, isp=isp, j=j: i(e.activation(
                        pu[isp][:, :], pS[j][:, :], AF.Exp, scale=ISCALE)),
                        [tS[(p, j)]] + pu_free[isp][-2:])
                    pS_free[j].append(tE)
                    tM = tE
                    for k in range(2):
                        c = c0 + k
                        if c >= E_s - 8:
                            mi = c - (E_s - 8)

                            def f_m(e, i, s=self, isp=isp, k=k, mi=mi):
                                i(e.tensor_tensor(
                                    pu[isp][:, k * 512:(k + 1) * 512],
                                    pu[isp][:, k * 512:(k + 1) * 512],
                                    s.masks_sb[:, mi * 512:(mi + 1) * 512],
                                    ALU.mult))
                            tM = st(DVE, f_m, [tM, self.t_masks])
                    tM_tok[(p, j)] = tM
                for j in (0, 1):
                    if p + 1 < n_pairs:
                        tS[(p + 1, j)] = emit_S(p + 1, j)
                    if p >= 1:
                        emit_CR(p - 1, j)
            for j in (0, 1):
                emit_CR(n_pairs - 1, j)
            for j in (0, 1):
                qb_free[iqs[j]].append(ctx_last[j])
            kb_free[ig].append(ctx_last[1])
            vtok_free[ig].append(ctx_last[1])
            if idx + 2 < len(seq):
                g2, h2, s2 = seq[idx + 2]
                q_tok[(idx + 2, 0)] = load_q(h2, s2, (2 * idx + 4) % 4)
                q_tok[(idx + 2, 1)] = load_q(h2 + 1, s2, (2 * idx + 5) % 4)
            for j in (0, 1):
                jv = 2 * j + (idx % 2)
                tEv = st(DVE, lambda e, i, j=j, jv=jv: i(e.tensor_copy(
                    ctx_un[jv][:, :], pC[j][:, :])),
                    [ctx_last[j]] + cun_free[jv][-1:])
                pC_free[j].append(tEv)
                tRec = st(DVE, lambda e, i, j=j, jv=jv: i(e.reciprocal(
                    recip_sb[jv][:, :], pRS[j][0:1, :])),
                    [rs_last[j]] + recip_free[jv][-1:])
                pRS_free[j].append(tRec)
                tBo = st(self.c_bc[jv], lambda e, i, s=self, jv=jv: i(
                    e.dma_start(s.rs_scr[jv], recip_sb[jv][:, :])),
                    [tRec, self.c_bc[jv].tok()])
                recip_free[jv].append(tBo)
                tBi = st(self.c_bc[jv], lambda e, i, s=self, jv=jv: i(
                    e.dma_start(bcast_sb[jv][:, :],
                                bass.AP(s.rs_scr[jv].tensor, 0,
                                        [[0, 128], [1, 512]]))),
                    [tBo] + bcast_free[jv][-1:])

                def mk_norm(s=self, h=h0 + j, s_=s_, jv=jv, tBi=tBi,
                            tEv=tEv):
                    def f_n(e, i):
                        i(e.tensor_tensor(
                            s.ctx_all[:, h * s.T_own + s_ * 512:
                                      h * s.T_own + (s_ + 1) * 512],
                            ctx_un[jv][:, :], bcast_sb[jv][:, :], ALU.mult))
                    tN = s.st(s.DVE, f_n, [tBi, tEv])
                    cun_free[jv].append(tN)
                    bcast_free[jv].append(tN)
                    s.ctx_toks.append(tN)
                norm_pend.append(mk_norm)
            while len(norm_pend) > 2:
                norm_pend.pop(0)()

        for fn in norm_pend:
            fn()
        esB.close()

    def emit_phase_f(self):
        bar = self.barrier_toks()
        st, DVE, PE = self.st, self.DVE, self.PE
        DC, T_own, OB = self.DC, self.T_own, self.OB
        WPC = max(1, DC // 8)
        NPC = DC // WPC
        esF = ExitStack()
        sbF = lambda n, sh, dt: esF.enter_context(
            self.nc.sbuf_tensor(n, sh, dt))
        psF = lambda n, sh, dt: esF.enter_context(
            self.nc.psum_tensor(n, sh, dt))
        wob = [sbF(f"wob{i}", [128, WPC * 512], BF16) for i in range(15)]
        bm_sb = [sbF(f"bm_sb{i}", [128, 512], BF16) for i in range(2)]
        xf = [sbF(f"xf{i}", [128, 512], F32) for i in range(2)]
        ob_sb = [sbF(f"ob_sb{i}", [128, 512], F32) for i in range(2)]
        pF = [psF(f"pF{i}", [128, 512], F32) for i in range(2)]

        wob_free = [[] for _ in range(15)]
        xf_free = [[], []]
        ob_free = [[], []]
        pF_free = [[], []]

        bm_free = [[], []]

        def load_bm(ob):
            ib = ob % 2
            return self.cload(lambda e, i, s=self, ob=ob, ib=ib: i(
                e.dma_start(bm_sb[ib][:, :],
                            s.bm_eff[:, ob * 512:(ob + 1) * 512])),
                bar + bm_free[ib][-1:])
        bm_toks = {0: load_bm(0)}
        if OB > 1:
            bm_toks[1] = load_bm(1)

        def load_wo(piece):
            ob, pc = piece // NPC, piece % NPC
            slot = piece % 15
            rs = ob * 512
            c0 = pc * WPC
            return st(self.c_wo[slot], lambda e, i, s=self, rs=rs, c0=c0,
                      slot=slot: i(e.dma_start(
                          wob[slot][:, :].rearrange(
                              "p (c n) -> p c n", n=512),
                          s.wo[c0 * 128:(c0 + WPC) * 128,
                               rs:rs + 512].rearrange(
                              "(c p) n -> p c n", p=128))),
                      bar + wob_free[slot][-1:])

        npieces = OB * NPC
        wo_toks = {}
        for pv_ in range(min(15, npieces)):
            wo_toks[pv_] = load_wo(pv_)

        xf_tok = {}

        def load_xf(ob, tt, idx):
            ix = idx % 2
            return st(self.c_xf[ix], lambda e, i, s=self, ob=ob, tt=tt,
                      ix=ix: i(e.dma_start(
                          xf[ix][:, :],
                          s.x_own[tt * 128:(tt + 1) * 128,
                                  ob * 512:(ob + 1) * 512])),
                      bar + xf_free[ix][-1:])

        idx = 0
        total = OB * self.NT
        for ob in range(OB):
            pieces = [ob * NPC + pc for pc in range(NPC)]
            slots = [pz % 15 for pz in pieces]
            for tt in range(self.NT):
                if idx < 2:
                    xf_tok[idx] = load_xf(ob, tt, idx)

                ipf = idx % 2

                def f_mm(e, i, s=self, tt=tt, ob=ob, ipf=ipf, slots=slots,
                         WPC=WPC):
                    for c in range(s.DC):
                        sl = slots[c // WPC]
                        cc = c % WPC
                        e.matmul(pF[ipf][:, :],
                                 s.ctx_all[:, c * s.T_own + tt * 128:
                                           c * s.T_own + (tt + 1) * 128],
                                 wob[sl][:, cc * 512:(cc + 1) * 512],
                                 start=(c == 0), stop=False)
                    ins = e.matmul(pF[ipf][:, :],
                                   s.hw[:, tt * 128:(tt + 1) * 128],
                                   bm_sb[ob % 2][:, :],
                                   start=False, stop=True)
                    i(ins)
                tmm = st(PE, f_mm,
                         [wo_toks[pz] for pz in pieces] + [bm_toks[ob]]
                         + self.ctx_toks[-1:] + self.hw_toks[-1:]
                         + pF_free[ipf][-1:])
                bm_free[ob % 2].append(tmm)
                for sl in set(slots):
                    wob_free[sl].append(tmm)
                ta = st(DVE, lambda e, i, ipf=ipf, ix=idx % 2: i(
                    e.tensor_tensor(ob_sb[ipf][:, :], pF[ipf][:, :],
                                    xf[ix][:, :], ALU.add)),
                    [tmm, xf_tok[idx]] + ob_free[ipf][-1:])
                pF_free[ipf].append(ta)
                xf_free[idx % 2].append(ta)
                tst = st(self.c_st[ipf], lambda e, i, s=self, ob=ob, tt=tt,
                         ipf=ipf: i(e.dma_start(
                             s.out[tt * 128:(tt + 1) * 128,
                                   ob * 512:(ob + 1) * 512],
                             ob_sb[ipf][:, :])), [ta])
                ob_free[ipf].append(tst)
                if idx + 2 < total:
                    ob2 = (idx + 2) // self.NT
                    tt2 = (idx + 2) % self.NT
                    xf_tok[idx + 2] = load_xf(ob2, tt2, idx + 2)
                idx += 1
            for pz in range((ob + 1) * NPC, (ob + 2) * NPC):
                if pz < npieces and pz not in wo_toks:
                    wo_toks[pz] = load_wo(pz)
            if ob + 2 < OB and (ob + 2) not in bm_toks:
                bm_toks[ob + 2] = load_bm(ob + 2)

        toks = [c.tok() for c in self.chans]
        st(self.c_st[0], lambda e, i: None,
           [t for t in toks if self.eng_of(t[0]) == "sync"], incs=0)
        st(self.c_const, lambda e, i: None,
           [t for t in toks if self.eng_of(t[0]) == "gpsimd"], incs=0)
        esF.close()
        self.esBF.close()


def build_program(D=4096, H=32, G=8, T_own=2048, T_keys=4096):
    return Builder(D, H, G, T_own, T_keys).build()


def _rope_tables(T, base=500000.0):
    inv = 1.0 / base ** (np.arange(0, 128, 2, dtype=np.float64) / 128.0)
    fr = np.outer(np.arange(T, dtype=np.float64), inv)
    return (np.cos(fr).astype(np.float32).T.copy(),
            np.sin(fr).astype(np.float32).T.copy())


def _masks(parity):
    m = np.zeros((8, 128, 512), np.float32)
    i = np.arange(128)[:, None]
    jq = np.arange(512)[None, :]
    for j in range(8):
        if parity == 0:
            if j < 4:
                m[j] = (128 * j + i <= jq).astype(np.float32)
        else:
            if j < 4:
                m[j] = 1.0
            else:
                m[j] = (128 * (j - 4) + i <= jq).astype(np.float32)
    return m


_NC_CACHE = {}


def _get_program():
    if "real" not in _NC_CACHE:
        _NC_CACHE["real"] = build_program()
    return _NC_CACHE["real"]


def kernel(L0_attn_x, L0_attn_Wq, L0_attn_Wk, L0_attn_Wv, L0_attn_Wo,
           L0_ffn_router_w, L0_ffn_router_b, L0_ffn_A, L0_ffn_Bm):
    x = np.ascontiguousarray(np.asarray(L0_attn_x, dtype=np.float32))
    B, T, D = x.shape
    H, G, E, R = 32, 8, 8, 16

    wq = np.ascontiguousarray(np.asarray(L0_attn_Wq, np.float32))
    wk = np.ascontiguousarray(np.asarray(L0_attn_Wk, np.float32))
    wv = np.ascontiguousarray(np.asarray(L0_attn_Wv, np.float32))
    wo = np.ascontiguousarray(np.asarray(L0_attn_Wo, np.float32))
    rw = np.ascontiguousarray(np.asarray(L0_ffn_router_w, np.float32))
    rb = np.ascontiguousarray(
        np.asarray(L0_ffn_router_b, np.float32)).reshape(1, E)
    A = np.asarray(L0_ffn_A, np.float32)
    Bm = np.asarray(L0_ffn_Bm, np.float32)
    a_t = np.ascontiguousarray(A.reshape(E * R, D).T)
    bm_eff = np.ascontiguousarray(Bm.transpose(0, 2, 1).reshape(E * R, D))

    cosT, sinT = _rope_tables(T)
    ident = np.eye(128, dtype=np.float32)
    expand = np.zeros((E, E * R), np.float32)
    for e in range(E):
        expand[e, e * R:(e + 1) * R] = 1.0

    idx = {}
    for p in (0, 1):
        blocks = np.arange(p * 512, T, 1024)
        idx[p] = np.concatenate([np.arange(b0, b0 + 512) for b0 in blocks])

    nc = _get_program()
    in_maps = []
    for c in range(8):
        b, p = c // 2, c % 2
        own, oth = idx[p], idx[1 - p]
        in_maps.append({
            "x_own": np.ascontiguousarray(x[b][own]),
            "wq": wq, "wk": wk, "wv": wv, "wo": wo,
            "a_t": a_t, "bm_eff": bm_eff,
            "router_w": rw, "router_b": rb,
            "cos_own": np.ascontiguousarray(cosT[:, own]),
            "sin_own": np.ascontiguousarray(sinT[:, own]),
            "masksin": _masks(p),
            "identin": ident,
            "expandin": expand,
        })
    res = run_bass_kernel_spmd(nc, in_maps, core_ids=list(range(8)))
    y = np.empty_like(x)
    for c in range(8):
        b, p = c // 2, c % 2
        y[b][idx[p]] = res.results[c]["out"]
    return y


# version 30
# speedup vs baseline: 1.0332x; 1.0100x over previous
import numpy as np
import concourse.bass as bass
import concourse.mybir as mybir
from concourse.bass_utils import run_bass_kernel_spmd
from contextlib import ExitStack

F32 = mybir.dt.float32
BF16 = mybir.dt.bfloat16
AF = mybir.ActivationFunctionType
ALU = mybir.AluOpType
AX = mybir.AxisListType

ISCALE = 0.08838834764831845


class Chan:
    __slots__ = ("name", "inc", "count")

    def __init__(self, name, inc=1):
        self.name = name
        self.inc = inc
        self.count = 0

    def tok(self):
        return (self, self.count)


class Plan:

    def __init__(self):
        self.steps = []

    def step(self, engine_attr, chan, fn, waits=(), incs=1):
        for (c, v) in waits:
            assert v <= c.count, f"future token {c.name}: {v} > {c.count}"
        self.steps.append((engine_attr, chan,
                           [w for w in waits if w[1] > 0], fn, incs))
        chan.count += incs * chan.inc
        return chan.tok()

    def replay(self, block, sems, waited):
        per_engine = {}
        for s in self.steps:
            per_engine.setdefault(s[0], []).append(s)

        def mk(ea, steps):
            def body(eng):
                w_ = waited.setdefault(ea, {})
                for (_ea, ch, waits, fn, incs) in steps:
                    for (wc, val) in waits:
                        if w_.get(wc.name, 0) >= val:
                            continue
                        eng.wait_ge(sems[wc.name], val)
                        w_[wc.name] = val

                    def inc(instr, cc=ch):
                        instr.then_inc(sems[cc.name], cc.inc)
                        return instr
                    fn(eng, inc)
            return body
        for ea, steps in per_engine.items():
            getattr(block, ea)(mk(ea, steps))


class Builder:
    def __init__(self, D=4096, H=32, G=8, T_own=2048, T_keys=4096):
        self.D, self.H, self.G = D, H, G
        self.T_own, self.T_keys = T_own, T_keys
        self.DC = D // 128
        self.E, self.R = 8, 16
        self.SCALING = 8.0 / self.R
        self.n_slots = T_own // 512
        self.CH = T_keys // 128
        self.EPC = self.CH // self.n_slots
        assert self.EPC == 8 and T_keys - T_own == T_own
        self.NB = T_own // 512
        self.HPG = H // G
        self.NT = T_own // 128
        self.OB = D // 512
        self.D2 = D // 2
        self.TGRP = min(16, self.DC)

        self.nc = bass.Bass("TRN2", target_bir_lowering=False)
        self.plan = None
        self.chans = []
        self.es = ExitStack()
        self.waited = {}

    def chan(self, name, inc=1):
        c = Chan(name, inc)
        self.chans.append(c)
        return c

    def st(self, ch, fn, waits=(), incs=1):
        return self.plan.step(self.eng_of(ch), ch, fn, waits, incs)

    def cload(self, fn, waits=()):
        t = self.st(self.c_const, fn, list(waits) + self._const_chain)
        self._const_chain = [t]
        return t

    def eng_of(self, ch):
        if ch is self.PE:
            return "tensor"
        if ch is self.DVE:
            return "vector"
        if ch is self.ACT:
            return "scalar"
        if ch is self.POOL or ch is self.c_cc:
            return "gpsimd"
        return "gpsimd" if ch.name in self.gp_ch else "sync"

    def run_phase(self, emit_fn):
        self.plan = Plan()
        emit_fn()
        with self.nc.Block() as block:
            self.plan.replay(block, self.sems, self.waited)
        self.plan = None

    def barrier_toks(self):
        return [c.tok() for c in self.chans]

    def build(self):
        self.declare_io()
        self.declare_channels()
        self.sems = {c.name: self.es.enter_context(
            self.nc.semaphore(c.name)) for c in self.chans}
        self.declare_persistent()
        self.ctx_toks = []
        self._const_chain = []
        self.run_phase(self.emit_phase_a)
        self.run_phase(self.emit_phase_b)
        self.run_phase(self.emit_phase_f)
        self.es.close()
        return self.nc

    def declare_io(self):
        nc, D = self.nc, self.D
        T_own, G, H, E, R = self.T_own, self.G, self.H, self.E, self.R

        def din(name, shape):
            return nc.dram_tensor(name, list(shape), F32,
                                  kind="ExternalInput").ap()
        self.x_own = din("x_own", (T_own, D))
        self.wq = din("wq", (D, H * 128))
        self.wk = din("wk", (D, G * 128))
        self.wv = din("wv", (D, G * 128))
        self.wo = din("wo", (H * 128, D))
        self.a_t = din("a_t", (D, E * R))
        self.bm_eff = din("bm_eff", (E * R, D))
        self.router_w = din("router_w", (E, D))
        self.router_b = din("router_b", (1, E))
        self.cos_own = din("cos_own", (64, T_own))
        self.sin_own = din("sin_own", (64, T_own))
        self.masksin = din("masksin", (8, 128, 512))
        self.identin = din("identin", (128, 128))
        self.expandin = din("expandin", (E, E * R))
        self.out = nc.dram_tensor("out", [T_own, D], F32,
                                  kind="ExternalOutput").ap()
        self.rs_scr = [nc.dram_tensor(f"rs_scr{j}", [1, 512], F32).ap()
                       for j in range(4)]
        self.q_spill = nc.dram_tensor("q_spill", [H * 128, T_own], BF16).ap()
        self.k_spill = nc.dram_tensor("k_spill", [G * 128, T_own], BF16).ap()
        self.v_spill = nc.dram_tensor("v_spill", [G * 128, T_own], BF16).ap()
        self.kg = nc.dram_tensor("kg", [2 * G * 128, T_own], BF16).ap()
        self.vg = nc.dram_tensor("vg", [2 * G * 128, T_own], BF16).ap()

    def declare_channels(self):
        C = self.chan
        self.PE, self.DVE, self.ACT, self.POOL = (C("pe"), C("dve"),
                                                  C("act"), C("pool"))
        self.c_x = [C(f"c_x{i}", 16) for i in range(4)]
        self.c_rope = [C(f"c_rope{i}", 16) for i in range(2)]
        self.c_spq = [C(f"c_spq{i}", 16) for i in range(2)]
        self.c_spk = [C(f"c_spk{i}", 16) for i in range(2)]
        self.c_spv = [C(f"c_spv{i}", 16) for i in range(2)]
        self.c_q = [C(f"c_q{i}", 16) for i in range(4)]
        self.c_k = [C(f"c_k{i}", 16) for i in range(2)]
        self.c_v = [C(f"c_v{i}", 16) for i in range(2)]
        self.c_st = [C(f"c_st{i}", 16) for i in range(2)]
        self.c_xf = [C(f"c_xf{i}", 16) for i in range(2)]
        self.c_w = [C(f"c_w{i}", 16) for i in range(2)]
        self.c_wo = [C(f"c_wo{i}", 16) for i in range(16)]
        self.c_const = C("c_const", 16)
        self.c_cc = C("c_cc", 1)
        self.c_bc = [C(f"c_bc{i}", 16) for i in range(4)]
        self.gp_ch = set(c.name for c in self.c_w + self.c_wo + self.c_rope
                         + self.c_x + [self.c_const])
        self.sync_extra = set(c.name for c in self.c_bc)

    def declare_persistent(self):
        E, R, DC, H, T_own = self.E, self.R, self.DC, self.H, self.T_own
        sb = lambda n, sh, dt: self.es.enter_context(
            self.nc.sbuf_tensor(n, sh, dt))
        self.ones_col = sb("ones_col", [128, 1], BF16)
        self.ones_row = sb("ones_row", [1, 128], BF16)
        self.ident_bf = sb("ident_bf", [128, 128], BF16)
        self.expand_bf = sb("expand_bf", [E, E * R], BF16)
        self.rb_sb = sb("rb_sb", [1, E], BF16)
        self.rwT = sb("rwT", [128, DC * E], BF16)
        self.hw = sb("hw", [128, T_own], BF16)


    def emit_phase_a(self):
        st, DVE, ACT, PE, POOL = (self.st, self.DVE, self.ACT, self.PE,
                                  self.POOL)
        D, D2, DC, T_own = self.D, self.D2, self.DC, self.T_own
        cc = self.c_const

        esA = ExitStack()
        sbA = lambda n, sh, dt: esA.enter_context(
            self.nc.sbuf_tensor(n, sh, dt))
        psA = lambda n, sh, dt: esA.enter_context(
            self.nc.psum_tensor(n, sh, dt))

        self.t_ident = self.cload(lambda e, i, s=self: i(e.dma_start(
            s.ident_bf[:, :], s.identin)))
        self.t_expand = self.cload(lambda e, i, s=self: i(e.dma_start(
            s.expand_bf[:, :], s.expandin)))
        self.t_rb = self.cload(lambda e, i, s=self: i(e.dma_start(
            s.rb_sb[:, :], s.router_b)))
        self.t_oc = st(POOL, lambda e, i, s=self: i(e.memset(
            s.ones_col[:, :], 1.0)))
        self.t_or = st(POOL, lambda e, i, s=self: i(e.memset(
            s.ones_row[:, :], 1.0)))

        rw_tmp = sbA("rw_tmp", [self.E, 128], BF16)
        pMiscF = psA("pMiscF", [128, 8 + 128], F32)
        pMiscB = psA("pMiscB", [128, 136], BF16)
        self.pl = pMiscF[:, 0:self.E]
        self.pwx = pMiscF[:, 8:136]
        self.pw = pMiscB[0:self.E, 0:128]
        self._rw_tmp = rw_tmp
        self._p_rw = pMiscB[:, 128:136]

        self.xnT = sbA("xnT", [128, DC * T_own], BF16)
        self.hT = sbA("hT", [128, self.T_own], BF16)
        self.xnT3 = self.xnT[:, :].rearrange("p (c t) -> p c t", c=DC)
        self.xbuf = [sbA(f"xbuf{i}", [128, D2], BF16) for i in range(4)]
        self.xn_tok = sbA("xn_tok", [128, D], BF16)
        self.wbuf = [sbA(f"wbuf{i}", [128, DC * 128], BF16) for i in range(2)]
        _stgp = [sbA(f"stg{i}", [128, 512], BF16) for i in range(2)]
        self.stg = {k: _stgp for k in "qkv"}
        self.cos_sl = sbA("cos_sl", [64, T_own], BF16)
        self.sin_sl = sbA("sin_sl", [64, T_own], BF16)
        self.rtmp = [sbA(f"rtmp{i}", [64, 512], F32) for i in range(2)]
        self.stats2 = []
        for k in range(2):
            d = {nm: sbA(f"st{k}_" + nm, [128, 1], F32)
                 for nm in ["dd", "s0", "r0", "s1", "rstd", "nb"]}
            d["bn"] = sbA(f"st{k}_bn", [128, 2 * max(1, self.D2 // 512) * 6], F32)
            d["mv"] = sbA(f"st{k}_mv", [128, 2], F32)
            self.stats2.append(d)
        self.gt = {nm: sbA("gt_" + nm, [128, self.E], F32)
                   for nm in ["t", "i1", "tm", "i2", "ind", "et", "w"]}
        self.gs = {nm: sbA("gs_" + nm, [128, 1], F32)
                   for nm in ["m1", "m2", "em2", "den", "rden"]}
        self.w_bf = sbA("w_bf", [128, self.E], BF16)
        self.wT_sb = sbA("wT_sb", [self.E, 128], BF16)

        self.pA = [psA(f"pA{i}", [128, 512], F32) for i in range(2)]
        self.pT = [psA(f"pT{i}", [128, self.TGRP * 128], BF16)
                   for i in range(2)]

        self.pA_free = [[] for _ in range(2)]
        self.pT_free = [[] for _ in range(2)]
        self.xbuf_free = [[] for _ in range(4)]
        _stgf = [[], []]
        self.stg_free = {k: _stgf for k in "qkv"}
        self.wbuf_free = [[] for _ in range(2)]
        self.rtmp_free = [[], []]
        self.pl_free = []
        self.pw_free = []
        self.pwx_free = []
        self.wT_free = []

        self.pA_i = 0
        self.w_i = 0
        _stgi = [0]
        self.stg_i = {k: _stgi for k in "qkv"}
        self.xt_i = 0
        self.kspill_toks = [[] for _ in range(self.G)]
        self.vspill_toks = [[] for _ in range(self.G)]
        self.qspill_toks = []
        self.hT_toks = []
        self.hw_toks = []
        self.xnT_tb_tok = {}
        self.rope_users = []
        self.prev_tile_norm = []
        self.prev_gate = []
        self.last_wmm = []
        self.pend_evac = []
        self.tb_evac_tok = []
        self.last_ttr = []
        self.pend_post = []

        self.emit_slab("own")

        esA.close()

    def emit_slab(self, slab):
        st = self.st
        rl1 = st(self.c_rope[0], lambda e, i, s=self: i(
            e.dma_start(s.cos_sl[:, :], s.cos_own)))
        rl2 = st(self.c_rope[1], lambda e, i, s=self: i(
            e.dma_start(s.sin_sl[:, :], s.sin_own)))
        self.rope_now = [rl1, rl2]

        lw0, iw0 = self.load_wslab(self.wk, 0)
        lv0, iv0 = self.load_wslab(self.wv, 0)
        for tb in range(self.NB):
            for t in range(tb * 4, tb * 4 + 4):
                self.emit_stage1_tile(slab, t)
            for fn in self.pend_post:
                fn()
            self.pend_post = []
            self.flush_evacs()
            self.xnT_tb_tok[tb] = self.tb_evac_tok[-1]
            self.emit_wcell(slab, "k", 0, tb, lw0, iw0)
            self.emit_wcell(slab, "v", 0, tb, lv0, iv0)

        self.emit_wpass(slab, self.wk, self.G, "k", skip_first=True)
        self.emit_wpass(slab, self.wv, self.G, "v", skip_first=True)
        groups = [[0, 1], [2, 3], [4, 5], [6, 7]]
        kw = [t for toks in self.kspill_toks for t in toks[-2:]]
        self.t_ccK = st(self.c_cc, lambda e, i, s=self, g=groups: i(
            e.collective_compute("AllGather", ALU.bypass,
                                 replica_groups=g,
                                 ins=[s.k_spill.opt()],
                                 outs=[s.kg.opt()])), kw)
        vw = [t for toks in self.vspill_toks for t in toks[-2:]]
        self.t_ccV = st(self.c_cc, lambda e, i, s=self, g=groups: i(
            e.collective_compute("AllGather", ALU.bypass,
                                 replica_groups=g,
                                 ins=[s.v_spill.opt()],
                                 outs=[s.vg.opt()])), vw)
        self.emit_wpass(slab, self.wq, self.H, "q")
        self.emit_wpass(slab, self.a_t, 1, "h")
        self.emit_rwt_init()
        self.emit_router()

    def emit_stage1_tile(self, slab, t):
        st, DVE, ACT = self.st, self.DVE, self.ACT
        x_src = self.x_own if slab == "own" else self.x_oth
        DC = self.DC
        ia = self.xt_i % 4
        ib = (self.xt_i + 1) % 4
        self.xt_i += 2
        S = self.stats2[(self.xt_i // 2) % 2]
        for fn in self.pend_post:
            fn()
        self.pend_post = []
        self.flush_evacs()
        la = st(self.c_x[ia], lambda e, i, s=self, t=t, ia=ia, x=x_src: i(
            e.dma_start(s.xbuf[ia][:, :], x[t * 128:(t + 1) * 128, 0:s.D2])),
            self.xbuf_free[ia][-1:] + self.prev_tile_norm[-2:-1])
        lb = st(self.c_x[ib], lambda e, i, s=self, t=t, ib=ib, x=x_src: i(
            e.dma_start(s.xbuf[ib][:, :], x[t * 128:(t + 1) * 128,
                                            s.D2:s.D])),
            self.xbuf_free[ib][-1:] + self.prev_tile_norm[-2:-1])
        pv = self.prev_tile_norm[-3:-1]
        ck = min(512, self.D2)
        nst = self.D2 // ck

        def f_bn(e, i, s=self, ia=ia, ib=ib, nst=nst, ck=ck):
            ins = None
            for half, ix in ((0, ia), (1, ib)):
                for c in range(nst):
                    ins = e.bn_stats(
                        S["bn"][:, (half * nst + c) * 6:
                                (half * nst + c + 1) * 6],
                        s.xbuf[ix][:, c * ck:(c + 1) * ck])
            i(ins)
        s2 = st(DVE, f_bn, [la, lb] + pv)
        s4 = st(DVE, lambda e, i: i(e.bn_aggr(
            S["mv"][:, :], S["bn"][:, :])), [s2])
        s9 = st(DVE, lambda e, i: i(e.tensor_scalar(
            S["dd"][:, :], S["mv"][:, 1:2], 1e-5, None, ALU.add)), [s4])
        a1 = st(ACT, lambda e, i: i(e.activation(
            S["s0"][:, :], S["dd"][:, :], AF.Sqrt)), [s9])

        def mk_post(s=self, S=S, a1=a1, ia=ia, ib=ib, t=t):
            st = s.st
            s10 = st(s.DVE, lambda e, i: i(e.reciprocal(
                S["r0"][:, :], S["s0"][:, :])), [a1])
            s11 = st(s.DVE, lambda e, i: i(e.tensor_tensor(
                S["s1"][:, :], S["dd"][:, :], S["r0"][:, :], ALU.mult)),
                [s10])
            s12 = st(s.DVE, lambda e, i: i(e.tensor_tensor(
                S["s1"][:, :], S["s1"][:, :], S["s0"][:, :], ALU.add)),
                [s11])
            s13 = st(s.DVE, lambda e, i: i(e.tensor_scalar(
                S["s1"][:, :], S["s1"][:, :], 0.5, None, ALU.mult)), [s12])
            s14 = st(s.DVE, lambda e, i: i(e.reciprocal(
                S["rstd"][:, :], S["s1"][:, :])), [s13])
            s16 = st(s.DVE, lambda e, i: i(e.scalar_tensor_tensor(
                S["nb"][:, :], S["mv"][:, 0:1], -1.0, S["rstd"][:, :],
                ALU.mult, ALU.mult)), [s14])
            n1 = st(s.ACT, lambda e, i: i(e.activation(
                s.xn_tok[:, 0:s.D2], s.xbuf[ia][:, :], AF.Identity,
                bias=S["nb"][:, :], scale=S["rstd"][:, :])),
                [s16, a1] + s.last_ttr[-1:])
            n2 = st(s.ACT, lambda e, i: i(e.activation(
                s.xn_tok[:, s.D2:s.D], s.xbuf[ib][:, :], AF.Identity,
                bias=S["nb"][:, :], scale=S["rstd"][:, :])), [s16, n1])
            s.prev_tile_norm.append(n2)
            s.xbuf_free[ia].append(n1)
            s.xbuf_free[ib].append(n2)
            nr = (s.DC + s.TGRP - 1) // s.TGRP
            for r in range(nr):
                ir = r % 2
                c0 = r * s.TGRP
                ncs = min(s.TGRP, s.DC - c0)

                def f_tr(e, i, s=s, c0=c0, ncs=ncs, ir=ir):
                    ins = None
                    for k in range(ncs):
                        ins = e.transpose(
                            s.pT[ir][:, k * 128:(k + 1) * 128],
                            s.xn_tok[:, (c0 + k) * 128:(c0 + k + 1) * 128],
                            s.ident_bf[:, :])
                    i(ins)
                ttr = st(s.PE, f_tr,
                         [n1, n2, s.t_ident] + s.pT_free[ir][-1:])
                s.last_ttr.append(ttr)

                def mk_ev(s=s, c0=c0, ncs=ncs, ir=ir, t=t, ttr=ttr):
                    def f_ev(e, i):
                        dst = s.xnT3[:, c0:c0 + ncs, t * 128:(t + 1) * 128]
                        src = s.pT[ir][:, 0:ncs * 128].rearrange(
                            "p (c k) -> p c k", c=ncs)
                        i(e.tensor_copy(dst, src))

                    def f_ev_act(e, i):
                        dst = s.xnT3[:, c0:c0 + ncs, t * 128:(t + 1) * 128]
                        src = s.pT[ir][:, 0:ncs * 128].rearrange(
                            "p (c k) -> p c k", c=ncs)
                        i(e.activation(dst, src, AF.Identity))
                    if ir == 0:
                        tev = s.st(s.DVE, f_ev, [ttr] + s.last_wmm)
                    else:
                        tev = s.st(s.ACT, f_ev_act, [ttr] + s.last_wmm)
                    s.pT_free[ir].append(tev)
                    s.tb_evac_tok.append(tev)
                s.pend_evac.append(mk_ev)
        self.pend_post.append(mk_post)
        return None

    def load_wslab(self, whbm, j):
        iw = self.w_i % 2
        self.w_i += 1
        lw = self.st(self.c_w[iw], lambda e, i, s=self, j=j, iw=iw,
                     w=whbm: i(
            e.dma_start(s.wbuf[iw][:, :].rearrange(
                            "p (c n) -> p c n", n=128),
                        w[:, j * 128:(j + 1) * 128].rearrange(
                            "(c p) n -> p c n", p=128))),
            self.wbuf_free[iw][-1:])
        return lw, iw

    def emit_wcell(self, slab, kind, j, tb, lw, iw):
        ip = self.pA_i % 2
        self.pA_i += 1

        def f_mm(e, i, s=self, tb=tb, iw=iw, ip=ip):
            ins = None
            for c in range(s.DC):
                ins = e.matmul(
                    s.pA[ip][:, :],
                    s.wbuf[iw][:, c * 128:(c + 1) * 128],
                    s.xnT[:, c * s.T_own + tb * 512:
                          c * s.T_own + (tb + 1) * 512],
                    start=(c == 0), stop=(c == s.DC - 1))
            i(ins)
        tmm = self.st(self.PE, f_mm,
                      [lw, self.xnT_tb_tok[tb]] + self.pA_free[ip][-1:])
        self.last_wmm = [tmm]
        self.wbuf_free[iw].append(tmm)
        self.emit_epilogue(slab, kind, j, tb, ip, tmm)

    def flush_evacs(self):
        for fn in self.pend_evac:
            fn()
        self.pend_evac = []

    def emit_wpass(self, slab, whbm, dout_chunks, kind, skip_first=False):
        st = self.st
        for j in range(1 if skip_first else 0, dout_chunks):
            lw, iw = self.load_wslab(whbm, j)
            for tb in range(self.NB):
                self.emit_wcell(slab, kind, j, tb, lw, iw)

    def emit_epilogue(self, slab, kind, j, tb, ip, tmm):
        st, DVE, ACT = self.st, self.DVE, self.ACT
        perm_off = 512 * tb
        if kind in ("k", "q"):
            ir = self.stg_i[kind][0] % 2
            self.stg_i[kind][0] += 1
            stg = self.stg[kind][ir]
            cA = self.cos_sl[:, tb * 512:(tb + 1) * 512]
            sA = self.sin_sl[:, tb * 512:(tb + 1) * 512]
            m1 = st(DVE, lambda e, i, s=self, ip=ip, cA=cA: i(e.tensor_tensor(
                s.rtmp[0][:, :], s.pA[ip][0:64, :], cA, ALU.mult)),
                [tmm] + self.rope_now + self.rtmp_free[0][-1:])
            m2 = st(DVE, lambda e, i, s=self, ip=ip, sA=sA: i(e.tensor_tensor(
                s.rtmp[1][:, :], s.pA[ip][64:128, :], sA, ALU.mult)),
                [m1] + self.rtmp_free[1][-1:])
            o1 = st(DVE, lambda e, i, s=self, stg=stg: i(e.tensor_tensor(
                stg[0:64, :], s.rtmp[0][:, :], s.rtmp[1][:, :],
                ALU.subtract)), [m2] + self.stg_free[kind][ir][-1:])
            m3 = st(DVE, lambda e, i, s=self, ip=ip, sA=sA: i(e.tensor_tensor(
                s.rtmp[0][:, :], s.pA[ip][0:64, :], sA, ALU.mult)), [o1])
            m4 = st(DVE, lambda e, i, s=self, ip=ip, cA=cA: i(e.tensor_tensor(
                s.rtmp[1][:, :], s.pA[ip][64:128, :], cA, ALU.mult)), [m3])
            o2 = st(DVE, lambda e, i, s=self, stg=stg: i(e.tensor_tensor(
                stg[64:128, :], s.rtmp[0][:, :], s.rtmp[1][:, :], ALU.add)),
                [m4])
            self.rtmp_free[0].append(o2)
            self.rtmp_free[1].append(o2)
            self.pA_free[ip].append(o2)
            self.rope_users.append(o2)
            if kind == "k":
                sp = st(self.c_spk[ir], lambda e, i, s=self, j=j, stg=stg,
                        po=perm_off: i(e.dma_start(
                            s.k_spill[j * 128:(j + 1) * 128, po:po + 512],
                            stg[:, :])), [o2])
                self.kspill_toks[j].append(sp)
            else:
                sp = st(self.c_spq[ir], lambda e, i, s=self, j=j, stg=stg,
                        tb=tb: i(e.dma_start(
                            s.q_spill[j * 128:(j + 1) * 128,
                                      tb * 512:(tb + 1) * 512], stg[:, :])),
                        [o2])
                self.qspill_toks.append(sp)
            self.stg_free[kind][ir].append(sp)
        elif kind == "v":
            ir = self.stg_i["v"][0] % 2
            self.stg_i["v"][0] += 1
            ev = st(ACT, lambda e, i, s=self, ip=ip, ir=ir: i(e.activation(
                s.stg["v"][ir][:, :], s.pA[ip][:, :], AF.Identity)),
                [tmm] + self.stg_free["v"][ir][-1:])
            self.pA_free[ip].append(ev)
            sp = st(self.c_spv[ir], lambda e, i, s=self, j=j, ir=ir,
                    po=perm_off: i(e.dma_start(
                        s.v_spill[j * 128:(j + 1) * 128, po:po + 512],
                        s.stg["v"][ir][:, :])), [ev])
            self.vspill_toks[j].append(sp)
            self.stg_free["v"][ir].append(sp)
        elif kind == "h":
            ev = st(ACT, lambda e, i, s=self, ip=ip, tb=tb: i(e.activation(
                s.hT[:, tb * 512:(tb + 1) * 512], s.pA[ip][:, :],
                AF.Identity)), [tmm])
            self.pA_free[ip].append(ev)
            self.hT_toks.append(ev)

    def emit_rwt_init(self):
        st, PE, DVE = self.st, self.PE, self.DVE
        rw_tmp, p_rw = self._rw_tmp, self._p_rw
        prev = []
        for c in range(self.DC):
            tl = self.cload(lambda e, i, s=self, c=c, rw=rw_tmp: i(
                e.dma_start(rw[:, :],
                            s.router_w[:, c * 128:(c + 1) * 128])), prev)
            tt = st(PE, lambda e, i, s=self, rw=rw_tmp, p=p_rw: i(
                e.transpose(p, rw[:, :], s.ident_bf[0:s.E, 0:s.E])),
                [tl, self.t_ident] + prev)
            tv = st(DVE, lambda e, i, s=self, c=c, p=p_rw: i(e.tensor_copy(
                s.rwT[:, c * s.E:(c + 1) * s.E], p)), [tt])
            prev = [tv]
        self.t_rwT = prev[0]
        self._pw_init = prev[0]

    def emit_router(self):
        st, DVE, ACT, PE = self.st, self.DVE, self.ACT, self.PE
        g, gs = self.gt, self.gs
        for tt in range(self.NT):
            def f_rmm(e, i, s=self, tt=tt):
                for c in range(s.DC):
                    e.matmul(s.pl,
                             s.xnT[:, c * s.T_own + tt * 128:
                                   c * s.T_own + (tt + 1) * 128],
                             s.rwT[:, c * s.E:(c + 1) * s.E],
                             start=(c == 0), stop=False)
                ins = e.matmul(s.pl, s.ones_row[:, :], s.rb_sb[:, :],
                               start=False, stop=True)
                i(ins)
            pg = self.prev_gate[-2:]
            tlg = st(PE, f_rmm,
                     [self.t_rwT, self.t_rb, self.t_or,
                      self.xnT_tb_tok[tt // 4]] + self.pl_free[-1:] + pg)
            g0 = st(DVE, lambda e, i, s=self: i(e.reduce_max(
                gs["m1"][:, :], s.pl, axis=AX.X)), [tlg] + pg)
            g1 = st(DVE, lambda e, i, s=self: i(e.tensor_scalar(
                g["t"][:, :], s.pl, gs["m1"][:, :], None,
                ALU.subtract)), [g0])
            self.pl_free.append(g1)
            g2 = st(DVE, lambda e, i: i(e.tensor_scalar(
                g["i1"][:, :], g["t"][:, :], 0.0, None, ALU.is_ge)), [g1])
            g3 = st(DVE, lambda e, i: i(e.tensor_scalar(
                g["tm"][:, :], g["i1"][:, :], -1e9, None, ALU.mult)), [g2])
            g4 = st(DVE, lambda e, i: i(e.tensor_tensor(
                g["tm"][:, :], g["t"][:, :], g["tm"][:, :], ALU.add)), [g3])
            g5 = st(DVE, lambda e, i: i(e.reduce_max(
                gs["m2"][:, :], g["tm"][:, :], axis=AX.X)), [g4])
            g6 = st(DVE, lambda e, i: i(e.tensor_scalar(
                g["i2"][:, :], g["tm"][:, :], gs["m2"][:, :], None,
                ALU.is_ge)), [g5])
            g7 = st(DVE, lambda e, i: i(e.tensor_tensor(
                g["ind"][:, :], g["i1"][:, :], g["i2"][:, :], ALU.add)),
                [g6])
            a1 = st(ACT, lambda e, i: i(e.activation(
                g["et"][:, :], g["t"][:, :], AF.Exp)), [g1] + pg)
            a2 = st(ACT, lambda e, i: i(e.activation(
                gs["em2"][:, :], gs["m2"][:, :], AF.Exp)), [g5, a1])
            g8 = st(DVE, lambda e, i: i(e.tensor_scalar(
                gs["den"][:, :], gs["em2"][:, :], 1.0, None, ALU.add)),
                [a2, g7])
            g9 = st(DVE, lambda e, i: i(e.reciprocal(
                gs["rden"][:, :], gs["den"][:, :])), [g8])
            g10 = st(DVE, lambda e, i: i(e.tensor_tensor(
                g["w"][:, :], g["ind"][:, :], g["et"][:, :], ALU.mult)),
                [g9, a1])
            g11 = st(DVE, lambda e, i: i(e.tensor_scalar(
                g["w"][:, :], g["w"][:, :], gs["rden"][:, :], None,
                ALU.mult)), [g10])
            g12 = st(DVE, lambda e, i, s=self: i(e.tensor_scalar(
                s.w_bf[:, :], g["w"][:, :], s.SCALING, None, ALU.mult)),
                [g11] + self.pw_free[-1:])
            tw = st(PE, lambda e, i, s=self: i(e.transpose(
                s.pw, s.w_bf[:, :], s.ident_bf[:, :])),
                [g12, self.t_ident, self._pw_init] + self.pw_free[-1:])
            ev = st(DVE, lambda e, i, s=self: i(e.tensor_copy(
                s.wT_sb[:, :], s.pw)), [tw] + self.wT_free[-1:])
            self.pw_free.append(ev)
            tx = st(PE, lambda e, i, s=self: i(e.matmul(
                s.pwx, s.expand_bf[:, :], s.wT_sb[:, :],
                start=True, stop=True)),
                [ev, self.t_expand] + self.pwx_free[-1:])
            self.wT_free.append(tx)
            hwt = st(DVE, lambda e, i, s=self, tt=tt: i(e.tensor_tensor(
                s.hw[:, tt * 128:(tt + 1) * 128],
                s.hT[:, tt * 128:(tt + 1) * 128], s.pwx, ALU.mult)),
                [tx] + self.hT_toks[-1:])
            self.pwx_free.append(hwt)
            self.hw_toks.append(hwt)
            self.prev_gate.append(hwt)

    def emit_phase_b(self):
        bar = self.barrier_toks()
        st, DVE, ACT, PE = self.st, self.DVE, self.ACT, self.PE
        CH, G, T_keys, T_own = self.CH, self.G, self.T_keys, self.T_own

        self.esBF = ExitStack()
        sbBF = lambda n, sh, dt: self.esBF.enter_context(
            self.nc.sbuf_tensor(n, sh, dt))
        self.ctx_all = sbBF("ctx_all", [128, self.H * T_own], BF16)

        esB = ExitStack()
        sbB = lambda n, sh, dt: esB.enter_context(
            self.nc.sbuf_tensor(n, sh, dt))
        psB = lambda n, sh, dt: esB.enter_context(
            self.nc.psum_tensor(n, sh, dt))
        self.masks_sb = sbB("masks_sb", [128, 8 * 512], BF16)
        self.t_masks = self.cload(lambda e, i, s=self: i(e.dma_start(
            s.masks_sb[:, :].rearrange("p (m q) -> p m q", m=8),
            s.masksin.rearrange("m p q -> p m q"))), bar)

        kb = [sbB(f"kb{i}", [128, T_keys], BF16) for i in range(2)]
        vtok = [sbB(f"vtok{i}", [128, T_keys], BF16) for i in range(2)]
        pu = [sbB(f"pu{i}", [128, 1024], BF16) for i in range(4)]
        qb = [sbB(f"qb{i}", [128, 512], BF16) for i in range(4)]
        recip_sb = [sbB(f"recip_sb{i}", [1, 512], F32) for i in range(4)]
        bcast_sb = [sbB(f"bcast_sb{i}", [128, 512], F32) for i in range(4)]
        ctx_un = [sbB(f"ctx_un{i}", [128, 512], BF16) for i in range(4)]

        pS = [psB(f"pS{i}", [128, 1024], F32) for i in range(2)]
        pC = [psB(f"pC{i}", [128, 512], F32) for i in range(2)]
        pRS = [psB(f"pRS{i}", [128, 512], F32) for i in range(2)]

        kb_free = [[], []]
        vtok_free = [[], []]
        pu_free = [[], [], [], []]
        qb_free = [[], [], [], []]
        pS_free = [[], []]
        pC_free = [[], []]
        pRS_free = [[], []]
        recip_free = [[] for _ in range(4)]
        bcast_free = [[] for _ in range(4)]
        cun_free = [[] for _ in range(4)]

        GR = self.G * 128

        def load_kv(g_):
            ig = g_ % 2

            def f_k(e, i, s=self, g_=g_, ig=ig):
                for r in range(2):
                    dst = kb[ig][:, :].rearrange(
                        "p (b n) -> p b n", n=1024)[:, :, r * 512:(r + 1) * 512]
                    i(e.dma_start(
                        dst,
                        s.kg[r * GR + g_ * 128:r * GR + (g_ + 1) * 128,
                             :].rearrange("p (b n) -> p b n", n=512)))
            tk = st(self.c_k[ig], f_k,
                    bar + [self.t_ccK] + kb_free[ig][-1:], incs=2)

            def f_v(e, i, s=self, g_=g_, ig=ig):
                for c in range(s.CH):
                    r, sblk, j = (c % 8) // 4, c // 8, c % 4
                    i(e.dma_start(
                        vtok[ig][:, c * 128:(c + 1) * 128],
                        s.vg[r * GR + g_ * 128:r * GR + (g_ + 1) * 128,
                             sblk * 512 + j * 128:sblk * 512 + (j + 1) * 128],
                        transpose=True))
            tv = st(self.c_v[ig], f_v,
                    bar + [self.t_ccV] + vtok_free[ig][-1:],
                    incs=self.CH)
            return (tk, tv)

        kv_tok = {0: load_kv(0)}
        vt_toks = {0: [kv_tok[0][1]]}
        norm_pend = []

        HPG = self.HPG
        seq = []
        for g_ in range(G):
            for hp in range(HPG // 2):
                for s_ in range(self.n_slots):
                    seq.append((g_, HPG * g_ + 2 * hp, s_))

        q_tok = {}

        def load_q(h, s_, iq):
            return st(self.c_q[iq], lambda e, i, s=self, h=h, s_=s_,
                      iq=iq: i(e.dma_start(
                          qb[iq][:, :],
                          s.q_spill[h * 128:(h + 1) * 128,
                                    s_ * 512:(s_ + 1) * 512])),
                      bar + self.qspill_toks[-2:] + qb_free[iq][-1:])

        for idx in range(min(2, len(seq))):
            g_, h0, s_ = seq[idx]
            q_tok[(idx, 0)] = load_q(h0, s_, (2 * idx) % 4)
            q_tok[(idx, 1)] = load_q(h0 + 1, s_, (2 * idx + 1) % 4)

        pair_i = 0

        for idx, (g_, h0, s_) in enumerate(seq):
            ig = g_ % 2
            E_s = self.EPC * (s_ + 1)
            if h0 == HPG * g_ and s_ == 0 and g_ + 1 < G:
                kv_tok[g_ + 1] = load_kv(g_ + 1)
                vt_toks[g_ + 1] = [kv_tok[g_ + 1][1]]
            tq = [q_tok[(idx, 0)], q_tok[(idx, 1)]]
            iqs = [(2 * idx) % 4, (2 * idx + 1) % 4]
            n_pairs = E_s // 2

            def emit_S(p, j):
                c0 = 2 * p

                def f_s(e, i, s=self, ig=ig, j=j, c0=c0, iq=iqs[j]):
                    ins = None
                    for k in range(2):
                        ins = e.matmul(
                            pS[j][:, k * 512:(k + 1) * 512],
                            kb[ig][:, (c0 + k) * 128:(c0 + k + 1) * 128],
                            qb[iq][:, :], start=True, stop=True)
                    i(ins)
                return st(PE, f_s, [kv_tok[g_][0], tq[j]] + pS_free[j][-1:])

            tS = {}
            tS[(0, 0)] = emit_S(0, 0)
            tS[(0, 1)] = emit_S(0, 1)
            ctx_last = [None, None]
            rs_last = [None, None]
            tM_tok = {}
            pu_of = {}

            def emit_CR(p, j):
                isp = pu_of[(p, j)]
                c0 = 2 * p

                def f_c(e, i, ig=ig, isp=isp, c0=c0, j=j,
                        first=(p == 0), last=(p == n_pairs - 1)):
                    ins = None
                    for k in range(2):
                        ins = e.matmul(
                            pC[j][:, :],
                            vtok[ig][:, (c0 + k) * 128:(c0 + k + 1) * 128],
                            pu[isp][:, k * 512:(k + 1) * 512],
                            start=(first and k == 0),
                            stop=(last and k == 1))
                    i(ins)
                tC = st(PE, f_c,
                        [tM_tok[(p, j)]] + vt_toks[g_] + pC_free[j][-1:])

                def f_r(e, i, s=self, isp=isp, j=j,
                        first=(p == 0), last=(p == n_pairs - 1)):
                    ins = None
                    for k in range(2):
                        ins = e.matmul(
                            pRS[j][0:1, :], s.ones_col[:, :],
                            pu[isp][:, k * 512:(k + 1) * 512],
                            start=(first and k == 0),
                            stop=(last and k == 1))
                    i(ins)
                tR = st(PE, f_r, [tM_tok[(p, j)], self.t_oc] +
                        (pRS_free[j][-1:] if p == 0 else []))
                pu_free[isp].append(tC)
                pu_free[isp].append(tR)
                ctx_last[j], rs_last[j] = tC, tR

            for p in range(n_pairs):
                for j in (0, 1):
                    isp = pair_i % 4
                    pair_i += 1
                    pu_of[(p, j)] = isp
                    c0 = 2 * p
                    tE = st(ACT, lambda e, i, isp=isp, j=j: i(e.activation(
                        pu[isp][:, :], pS[j][:, :], AF.Exp, scale=ISCALE)),
                        [tS[(p, j)]] + pu_free[isp][-2:])
                    pS_free[j].append(tE)
                    tM = tE
                    for k in range(2):
                        c = c0 + k
                        if c >= E_s - 8:
                            mi = c - (E_s - 8)

                            def f_m(e, i, s=self, isp=isp, k=k, mi=mi):
                                i(e.tensor_tensor(
                                    pu[isp][:, k * 512:(k + 1) * 512],
                                    pu[isp][:, k * 512:(k + 1) * 512],
                                    s.masks_sb[:, mi * 512:(mi + 1) * 512],
                                    ALU.mult))
                            tM = st(DVE, f_m, [tM, self.t_masks])
                    tM_tok[(p, j)] = tM
                for j in (0, 1):
                    if p + 1 < n_pairs:
                        tS[(p + 1, j)] = emit_S(p + 1, j)
                    if p >= 1:
                        emit_CR(p - 1, j)
            for j in (0, 1):
                emit_CR(n_pairs - 1, j)
            for j in (0, 1):
                qb_free[iqs[j]].append(ctx_last[j])
            kb_free[ig].append(ctx_last[1])
            vtok_free[ig].append(ctx_last[1])
            if idx + 2 < len(seq):
                g2, h2, s2 = seq[idx + 2]
                q_tok[(idx + 2, 0)] = load_q(h2, s2, (2 * idx + 4) % 4)
                q_tok[(idx + 2, 1)] = load_q(h2 + 1, s2, (2 * idx + 5) % 4)
            for j in (0, 1):
                jv = 2 * j + (idx % 2)
                tEv = st(DVE, lambda e, i, j=j, jv=jv: i(e.tensor_copy(
                    ctx_un[jv][:, :], pC[j][:, :])),
                    [ctx_last[j]] + cun_free[jv][-1:])
                pC_free[j].append(tEv)
                tRec = st(DVE, lambda e, i, j=j, jv=jv: i(e.reciprocal(
                    recip_sb[jv][:, :], pRS[j][0:1, :])),
                    [rs_last[j]] + recip_free[jv][-1:])
                pRS_free[j].append(tRec)
                tBo = st(self.c_bc[jv], lambda e, i, s=self, jv=jv: i(
                    e.dma_start(s.rs_scr[jv], recip_sb[jv][:, :])),
                    [tRec, self.c_bc[jv].tok()])
                recip_free[jv].append(tBo)
                tBi = st(self.c_bc[jv], lambda e, i, s=self, jv=jv: i(
                    e.dma_start(bcast_sb[jv][:, :],
                                bass.AP(s.rs_scr[jv].tensor, 0,
                                        [[0, 128], [1, 512]]))),
                    [tBo] + bcast_free[jv][-1:])

                def mk_norm(s=self, h=h0 + j, s_=s_, jv=jv, tBi=tBi,
                            tEv=tEv):
                    def f_n(e, i):
                        i(e.tensor_tensor(
                            s.ctx_all[:, h * s.T_own + s_ * 512:
                                      h * s.T_own + (s_ + 1) * 512],
                            ctx_un[jv][:, :], bcast_sb[jv][:, :], ALU.mult))
                    tN = s.st(s.DVE, f_n, [tBi, tEv])
                    cun_free[jv].append(tN)
                    bcast_free[jv].append(tN)
                    s.ctx_toks.append(tN)
                norm_pend.append(mk_norm)
            while len(norm_pend) > 2:
                norm_pend.pop(0)()

        for fn in norm_pend:
            fn()
        esB.close()

    def emit_phase_f(self):
        bar = self.barrier_toks()
        st, DVE, PE = self.st, self.DVE, self.PE
        DC, T_own, OB = self.DC, self.T_own, self.OB
        WPC = max(1, DC // 8)
        NPC = DC // WPC
        esF = ExitStack()
        sbF = lambda n, sh, dt: esF.enter_context(
            self.nc.sbuf_tensor(n, sh, dt))
        psF = lambda n, sh, dt: esF.enter_context(
            self.nc.psum_tensor(n, sh, dt))
        wob = [sbF(f"wob{i}", [128, WPC * 512], BF16) for i in range(16)]
        bm_sb = [sbF(f"bm_sb{i}", [128, 512], BF16) for i in range(2)]
        xf = [sbF(f"xf{i}", [128, 512], F32) for i in range(2)]
        ob_sb = [sbF(f"ob_sb{i}", [128, 512], F32) for i in range(2)]
        pF = [psF(f"pF{i}", [128, 512], F32) for i in range(2)]

        wob_free = [[] for _ in range(16)]
        xf_free = [[], []]
        ob_free = [[], []]
        pF_free = [[], []]

        bm_free = [[], []]

        def load_bm(ob):
            ib = ob % 2
            return self.cload(lambda e, i, s=self, ob=ob, ib=ib: i(
                e.dma_start(bm_sb[ib][:, :],
                            s.bm_eff[:, ob * 512:(ob + 1) * 512])),
                bar + bm_free[ib][-1:])
        bm_toks = {0: load_bm(0)}
        if OB > 1:
            bm_toks[1] = load_bm(1)

        def load_wo(piece):
            ob, pc = piece // NPC, piece % NPC
            slot = piece % 16
            rs = ob * 512
            c0 = pc * WPC
            return st(self.c_wo[slot], lambda e, i, s=self, rs=rs, c0=c0,
                      slot=slot: i(e.dma_start(
                          wob[slot][:, :].rearrange(
                              "p (c n) -> p c n", n=512),
                          s.wo[c0 * 128:(c0 + WPC) * 128,
                               rs:rs + 512].rearrange(
                              "(c p) n -> p c n", p=128))),
                      bar + wob_free[slot][-1:])

        npieces = OB * NPC
        wo_toks = {}
        for pv_ in range(min(16, npieces)):
            wo_toks[pv_] = load_wo(pv_)

        xf_tok = {}

        def load_xf(ob, tt, idx):
            ix = idx % 2
            return st(self.c_xf[ix], lambda e, i, s=self, ob=ob, tt=tt,
                      ix=ix: i(e.dma_start(
                          xf[ix][:, :],
                          s.x_own[tt * 128:(tt + 1) * 128,
                                  ob * 512:(ob + 1) * 512])),
                      bar + xf_free[ix][-1:])

        idx = 0
        total = OB * self.NT
        for ob in range(OB):
            pieces = [ob * NPC + pc for pc in range(NPC)]
            slots = [pz % 16 for pz in pieces]
            for tt in range(self.NT):
                if idx < 2:
                    xf_tok[idx] = load_xf(ob, tt, idx)

                ipf = idx % 2

                def f_mm(e, i, s=self, tt=tt, ob=ob, ipf=ipf, slots=slots,
                         WPC=WPC):
                    for c in range(s.DC):
                        sl = slots[c // WPC]
                        cc = c % WPC
                        e.matmul(pF[ipf][:, :],
                                 s.ctx_all[:, c * s.T_own + tt * 128:
                                           c * s.T_own + (tt + 1) * 128],
                                 wob[sl][:, cc * 512:(cc + 1) * 512],
                                 start=(c == 0), stop=False)
                    ins = e.matmul(pF[ipf][:, :],
                                   s.hw[:, tt * 128:(tt + 1) * 128],
                                   bm_sb[ob % 2][:, :],
                                   start=False, stop=True)
                    i(ins)
                tmm = st(PE, f_mm,
                         [wo_toks[pz] for pz in pieces] + [bm_toks[ob]]
                         + self.ctx_toks[-1:] + self.hw_toks[-1:]
                         + pF_free[ipf][-1:])
                bm_free[ob % 2].append(tmm)
                for sl in set(slots):
                    wob_free[sl].append(tmm)
                ta = st(DVE, lambda e, i, ipf=ipf, ix=idx % 2: i(
                    e.tensor_tensor(ob_sb[ipf][:, :], pF[ipf][:, :],
                                    xf[ix][:, :], ALU.add)),
                    [tmm, xf_tok[idx]] + ob_free[ipf][-1:])
                pF_free[ipf].append(ta)
                xf_free[idx % 2].append(ta)
                tst = st(self.c_st[ipf], lambda e, i, s=self, ob=ob, tt=tt,
                         ipf=ipf: i(e.dma_start(
                             s.out[tt * 128:(tt + 1) * 128,
                                   ob * 512:(ob + 1) * 512],
                             ob_sb[ipf][:, :])), [ta])
                ob_free[ipf].append(tst)
                if idx + 2 < total:
                    ob2 = (idx + 2) // self.NT
                    tt2 = (idx + 2) % self.NT
                    xf_tok[idx + 2] = load_xf(ob2, tt2, idx + 2)
                idx += 1
            for pz in range((ob + 1) * NPC, (ob + 2) * NPC):
                if pz < npieces and pz not in wo_toks:
                    wo_toks[pz] = load_wo(pz)
            if ob + 2 < OB and (ob + 2) not in bm_toks:
                bm_toks[ob + 2] = load_bm(ob + 2)

        toks = [c.tok() for c in self.chans]
        st(self.c_st[0], lambda e, i: None,
           [t for t in toks if self.eng_of(t[0]) == "sync"], incs=0)
        st(self.c_const, lambda e, i: None,
           [t for t in toks if self.eng_of(t[0]) == "gpsimd"], incs=0)
        esF.close()
        self.esBF.close()


def build_program(D=4096, H=32, G=8, T_own=2048, T_keys=4096):
    return Builder(D, H, G, T_own, T_keys).build()


def _rope_tables(T, base=500000.0):
    inv = 1.0 / base ** (np.arange(0, 128, 2, dtype=np.float64) / 128.0)
    fr = np.outer(np.arange(T, dtype=np.float64), inv)
    return (np.cos(fr).astype(np.float32).T.copy(),
            np.sin(fr).astype(np.float32).T.copy())


def _masks(parity):
    m = np.zeros((8, 128, 512), np.float32)
    i = np.arange(128)[:, None]
    jq = np.arange(512)[None, :]
    for j in range(8):
        if parity == 0:
            if j < 4:
                m[j] = (128 * j + i <= jq).astype(np.float32)
        else:
            if j < 4:
                m[j] = 1.0
            else:
                m[j] = (128 * (j - 4) + i <= jq).astype(np.float32)
    return m


_NC_CACHE = {}


def _get_program():
    if "real" not in _NC_CACHE:
        _NC_CACHE["real"] = build_program()
    return _NC_CACHE["real"]


def kernel(L0_attn_x, L0_attn_Wq, L0_attn_Wk, L0_attn_Wv, L0_attn_Wo,
           L0_ffn_router_w, L0_ffn_router_b, L0_ffn_A, L0_ffn_Bm):
    x = np.ascontiguousarray(np.asarray(L0_attn_x, dtype=np.float32))
    B, T, D = x.shape
    H, G, E, R = 32, 8, 8, 16

    wq = np.ascontiguousarray(np.asarray(L0_attn_Wq, np.float32))
    wk = np.ascontiguousarray(np.asarray(L0_attn_Wk, np.float32))
    wv = np.ascontiguousarray(np.asarray(L0_attn_Wv, np.float32))
    wo = np.ascontiguousarray(np.asarray(L0_attn_Wo, np.float32))
    rw = np.ascontiguousarray(np.asarray(L0_ffn_router_w, np.float32))
    rb = np.ascontiguousarray(
        np.asarray(L0_ffn_router_b, np.float32)).reshape(1, E)
    A = np.asarray(L0_ffn_A, np.float32)
    Bm = np.asarray(L0_ffn_Bm, np.float32)
    a_t = np.ascontiguousarray(A.reshape(E * R, D).T)
    bm_eff = np.ascontiguousarray(Bm.transpose(0, 2, 1).reshape(E * R, D))

    cosT, sinT = _rope_tables(T)
    ident = np.eye(128, dtype=np.float32)
    expand = np.zeros((E, E * R), np.float32)
    for e in range(E):
        expand[e, e * R:(e + 1) * R] = 1.0

    idx = {}
    for p in (0, 1):
        blocks = np.arange(p * 512, T, 1024)
        idx[p] = np.concatenate([np.arange(b0, b0 + 512) for b0 in blocks])

    nc = _get_program()
    in_maps = []
    for c in range(8):
        b, p = c // 2, c % 2
        own, oth = idx[p], idx[1 - p]
        in_maps.append({
            "x_own": np.ascontiguousarray(x[b][own]),
            "wq": wq, "wk": wk, "wv": wv, "wo": wo,
            "a_t": a_t, "bm_eff": bm_eff,
            "router_w": rw, "router_b": rb,
            "cos_own": np.ascontiguousarray(cosT[:, own]),
            "sin_own": np.ascontiguousarray(sinT[:, own]),
            "masksin": _masks(p),
            "identin": ident,
            "expandin": expand,
        })
    res = run_bass_kernel_spmd(nc, in_maps, core_ids=list(range(8)))
    y = np.empty_like(x)
    for c in range(8):
        b, p = c // 2, c % 2
        y[b][idx[p]] = res.results[c]["out"]
    return y
